# version 01
# baseline (speedup 1.0000x reference)
import numpy as np
import concourse.bass as bass
import concourse.mybir as mybir
from concourse.bass import IndirectOffsetOnAxis
from concourse.bass_utils import run_bass_kernel_spmd

F32 = mybir.dt.float32
BF16 = mybir.dt.bfloat16
U8 = mybir.dt.uint8
U32 = mybir.dt.uint32
AF = mybir.ActivationFunctionType
ALU = mybir.AluOpType
AX = mybir.AxisListType

S = 4096
H = 1024
C = 2048
D = 16
CT = C // 128
SC = 512
NCH = S // SC
SB = S // 128
E = 64
EO = 8
CAP = 1024
F = 512
F2 = 1024
NCORE = 8
TG = S * NCORE
YBLK = EO * CAP + 128
YROWS = NCORE * YBLK
ZROWID = float(EO * CAP)
BIGIDX = float(1 << 20)
BIGF = 1.0e30

DEBUG = False
STAGE = 5
STAGE2 = 6
DUMMYV = 0
DUMMYR = 0
K7SKIP = 0
SKIPRS = False


class _Cut(Exception):
    pass


def _cut(n):
    if STAGE2 < n:
        raise _Cut()


class Prog:
    ENGINES = ("sync", "scalar", "vector", "tensor", "gpsimd")
    NPOOL = 8

    def __init__(self, nc):
        self.nc = nc
        self.items = {e: [] for e in self.ENGINES}
        self.sems = {}
        self.ctr = {}
        self.pools = {}
        for pool, eng in (("L", "sync"), ("G", "gpsimd")):
            names = []
            for k in range(self.NPOOL):
                n = f"q{pool}{k}"
                self.sem(n)
                names.append(n)
            self.pools[pool] = {"names": names, "eng": eng, "rr": 0}

    def sem(self, name):
        h = self.nc.semaphore(name).__enter__()
        self.sems[name] = h
        self.ctr[name] = 0

    def wait(self, eng, sem, val=None):
        v = self.ctr[sem] if val is None else val
        h = self.sems[sem]
        if v > 0:
            self.items[eng].append(lambda e, h=h, v=v: e.wait_ge(h, v))
        return v

    def dma(self, eng, out, in_, pool):
        p = self.pools[pool]
        assert eng == p["eng"]
        k = p["rr"]
        p["rr"] = (k + 1) % self.NPOOL
        name = p["names"][k]
        h = self.sems[name]
        drain = self.ctr[name]
        if drain > 0:
            self.items[eng].append(lambda e, h=h, v=drain: e.wait_ge(h, v))
        self.items[eng].append(
            lambda e, o=out, i=in_, h=h: e.dma_start(out=o, in_=i).then_inc(h, 16))
        self.ctr[name] += 16
        return (name, self.ctr[name])

    def wtok(self, eng, tok):
        if tok is None:
            return
        name, v = tok
        h = self.sems[name]
        self.items[eng].append(lambda e, h=h, v=v: e.wait_ge(h, v))

    def wall(self, eng, pool):
        for name in self.pools[pool]["names"]:
            self.wait(eng, name)

    def op(self, eng, fn, sem, fence=False):
        h = self.sems[sem]
        if fence and self.ctr[sem] > 0:
            v = self.ctr[sem]
            self.items[eng].append(lambda e, h=h, v=v: e.wait_ge(h, v))
        self.items[eng].append(lambda e, fn=fn, h=h: fn(e).then_inc(h, 1))
        self.ctr[sem] += 1
        return self.ctr[sem]

    def ind(self, fn):
        p = self.pools["G"]
        k = p["rr"]
        p["rr"] = (k + 1) % self.NPOOL
        name = p["names"][k]
        h = self.sems[name]
        drain = self.ctr[name]
        if drain > 0:
            self.items["gpsimd"].append(lambda e, h=h, v=drain: e.wait_ge(h, v))
        self.items["gpsimd"].append(lambda e, fn=fn, h=h: fn(e).then_inc(h, 16))
        self.ctr[name] += 16
        return (name, self.ctr[name])

    def coll(self, fn):
        h = self.sems["CC"]
        self.items["gpsimd"].append(lambda e, fn=fn, h=h: fn(e).then_inc(h, 1))
        self.ctr["CC"] += 1
        return self.ctr["CC"]

    def raw(self, eng, fn):
        self.items[eng].append(fn)


class Arena:
    def __init__(self, nc, nbytes):
        self.t = nc.alloc_sbuf_tensor("arena", [128, nbytes], U8)
        self.nbytes = nbytes
        self.off = 0
        self.marks = []

    def alloc(self, shape, dtype, name="", at=None):
        esz = mybir.dt.size(dtype)
        n = int(np.prod(shape[1:])) * esz
        if at is None:
            off = (self.off + 63) & ~63
            assert off + n <= self.nbytes, (
                f"SBUF arena overflow: {name}: {n}B at {off}, cap {self.nbytes}")
            self.off = off + n
        else:
            off = at
            assert off + n <= self.nbytes
        ap = self.t.ap()[:, off:off + n].bitcast(dtype)
        if len(shape) > 2:
            dims = " ".join(f"d{i}" for i in range(1, len(shape)))
            kw = {f"d{i}": shape[i] for i in range(1, len(shape))}
            ap = ap.rearrange(f"p ({dims}) -> p {dims}", **kw)
        assert shape[0] == 128
        return ap

    def mark(self):
        return (self.off + 63) & ~63

    def push(self):
        self.marks.append(self.off)

    def pop(self):
        self.off = self.marks.pop()


def tokrows(dram2d, r0, r1):
    return dram2d[r0:r1, :].rearrange("(a p) b -> p a b", p=128)


def build_program(debug=DEBUG):
    nc = bass.Bass("TRN2", target_bir_lowering=False, detect_race_conditions=False)

    def din(name, shape, dt=F32):
        return nc.dram_tensor(name, list(shape), dt, kind="ExternalInput").ap()

    x_in = din("x_loc", [S, H])
    w_in = din("W_in", [H, 2 * C])
    conv_wT = din("conv_wT", [C, 4])
    conv_b = din("conv_b", [C, 1])
    w_x = din("W_x", [C, 96])
    w_dt = din("W_dt", [64, C])
    b_dt = din("b_dt", [1, C])
    a_log = din("A_log", [C, D])
    d_vec = din("D_vec", [C, 1])
    w_out = din("W_out", [C, H])
    gwT = din("gwT", [H, E])
    e_bias = din("e_bias", [1, E])
    w1T = din("w1T", [EO, H, F])
    w2T = din("w2T", [EO, F, H])
    w3T = din("w3T", [EO, H, F])
    sgT = din("sgT", [H, F2])
    suT = din("suT", [H, F2])
    sdT = din("sdT", [F2, H])
    coremask = din("coremask", [8, 1])
    tokbase = din("tokbase", [1, 1])

    out_ap = nc.dram_tensor("out_loc", [S, H], F32, kind="ExternalOutput").ap()

    def dint(name, shape, dt=F32, dbg=False, shared=False):
        kind = "ExternalOutput" if (debug and dbg) else "Internal"
        kw = {"addr_space": "Shared"} if shared else {}
        return nc.dram_tensor(name, list(shape), dt, kind=kind, **kw).ap()

    win_bfh = dint("win_bfh", [H, 2 * C], BF16)
    wout_bfh = dint("wout_bfh", [C, H], BF16)
    x2_sp = dint("x2_sp", [S, H], F32, dbg=True)
    xln2_sp = dint("xln2_sp", [S, H], BF16)
    xln2_db = dint("xln2_db", [S, H], BF16, dbg=True) if debug else None
    xln2T_sp = dint("xln2T_sp", [H, S], BF16)
    logits_db = dint("logits_db", [S, E], F32, dbg=True)
    x3_sp = dint("x3_sp", [S, H], F32, dbg=True)
    cnt_in = dint("cnt_in", [1, E], F32)
    cnt_out = dint("cnt_out", [8, E], F32)
    slab_in = dint("slab_in", [E * CAP, 2], F32)
    slab_out = dint("slab_out", [EO * CAP, 2], F32)
    xg_out = dint("xg_out", [TG, H], BF16, shared=True)
    ye_loc = dint("ye_loc", [YBLK, H], BF16)
    ye_out = dint("ye_out", [YROWS, H], BF16, shared=True)
    ind_db = dint("ind_db", [S, E], F32, dbg=True)
    w8_db = dint("w8_db", [S, E], F32, dbg=True)
    pos_db = dint("pos_db", [S, E], F32, dbg=True)

    pb = [nc.psum_tensor(f"pb{i}", [128, 512], F32).__enter__().ap() for i in range(8)]
    ptr = [pb[6].bitcast(BF16), pb[7].bitcast(BF16)]

    ssm_p = nc.alloc_sbuf_tensor("ssm_p", [96, SC], F32).ap()
    bc_B = nc.alloc_sbuf_tensor("bc_B", [16, SC], BF16).ap()
    bc_C = nc.alloc_sbuf_tensor("bc_C", [16, SC], BF16).ap()
    ts_aug = nc.alloc_sbuf_tensor("ts_aug", [65, SC], BF16).ap()
    wdt_bf = nc.alloc_sbuf_tensor("wdt_bf", [65, C], BF16).ap()
    oh16 = nc.alloc_sbuf_tensor("oh16", [16, D * 128], BF16).ap()
    lg_ev = nc.alloc_sbuf_tensor("lg_ev", [64, SC], F32).ap()

    AR = Arena(nc, 156 * 1024)
    sigma = AR.alloc([128, CT * D], F32, "sigma")
    logits = AR.alloc([128, SB, E], F32, "logits")
    a_sb = AR.alloc([128, CT * D], F32, "a_sb")
    cwT_sb = AR.alloc([128, CT * 4], F32, "cwT")
    cb_sb = AR.alloc([128, CT], F32, "cb")
    dv_sb = AR.alloc([128, CT], F32, "dv")
    ident_bf = AR.alloc([128, 128], BF16, "identb")
    ident_f = AR.alloc([128, 128], F32, "identf")
    lnst = AR.alloc([128, 8], F32, "lnst")

    P = Prog(nc)
    for s in ("V", "Sc", "T", "Pg", "CC"):
        P.sem(s)

    for ct in range(CT):
        P.dma("sync", a_sb[:, ct * D:(ct + 1) * D], a_log[ct * 128:(ct + 1) * 128, :], "L")
        P.dma("sync", cwT_sb[:, ct * 4:(ct + 1) * 4], conv_wT[ct * 128:(ct + 1) * 128, :], "L")
        P.dma("sync", cb_sb[:, ct:ct + 1], conv_b[ct * 128:(ct + 1) * 128, :], "L")
        P.dma("sync", dv_sb[:, ct:ct + 1], d_vec[ct * 128:(ct + 1) * 128, :], "L")
    P.dma("gpsimd", wdt_bf[0:64, :], w_dt[:, :], "G")
    P.dma("gpsimd", wdt_bf[64:65, :], b_dt[:, :], "G")
    P.wall("scalar", "L")
    P.op("scalar", lambda e: e.activation(out=a_sb[:], in_=a_sb[:], func=AF.Exp), "Sc")
    P.wait("vector", "Sc")
    P.op("vector", lambda e: e.tensor_scalar_mul(a_sb[:], a_sb[:], -1.0), "V")
    P.op("gpsimd", lambda e: e.iota(ident_f[:], pattern=[[-1, 128]], base=0,
                                    channel_multiplier=1,
                                    allow_small_or_imprecise_dtypes=True), "Pg")
    P.op("gpsimd", lambda e: e.iota(oh16[:], pattern=[[-1, D], [0, 128]], base=0,
                                    channel_multiplier=1,
                                    allow_small_or_imprecise_dtypes=True), "Pg")
    P.wait("vector", "Pg")
    P.op("vector", lambda e: e.tensor_scalar(out=ident_bf[:], in0=ident_f[:],
                                             scalar1=0.0, scalar2=None, op0=ALU.is_equal), "V")
    P.op("vector", lambda e: e.tensor_scalar(out=oh16[:], in0=oh16[:],
                                             scalar1=0.0, scalar2=None, op0=ALU.is_equal), "V")
    P.op("vector", lambda e: e.tensor_scalar(out=ident_f[:], in0=ident_f[:],
                                             scalar1=0.0, scalar2=None, op0=ALU.is_equal), "V")

    AR.push()
    wx_sb = AR.alloc([128, CT, 96], BF16, "wx_sb")
    wstr = [AR.alloc([128, 8, 128], BF16, "wstr0"), AR.alloc([128, 8, 128], BF16, "wstr1")]
    xreg_off = AR.mark()
    x_t = AR.alloc([128, 2, H], F32, "x_t")
    bounce = AR.alloc([128, 4096], BF16, "bounce_alias", at=xreg_off)
    xlreg_off = AR.mark()
    xln_bf = AR.alloc([128, 4, H], BF16, "xln")
    xln2_t = AR.alloc([128, 4, H], BF16, "xln2t", at=xlreg_off)
    streg_off = AR.mark()
    xlnT = AR.alloc([128, 8, SC], BF16, "xlnT")
    mo_bf = AR.alloc([128, 8, SC], BF16, "mo", at=streg_off)
    xlnTb = AR.alloc([128, 8, SC], BF16, "xlnTb", at=streg_off)
    padded = AR.alloc([128, SC + 4], F32, "padded")
    hbuf = AR.alloc([128, CT, SC], BF16, "hbuf")
    hbf = [AR.alloc([128, SC], BF16, "hbf0"), AR.alloc([128, SC], BF16, "hbf1")]
    sg_all = AR.alloc([128, CT, SC], BF16, "sg_all")
    dt_all = AR.alloc([128, CT, SC], BF16, "dt_all")
    hist = AR.alloc([128, CT, 3], F32, "hist")
    bbc_off = AR.mark()
    bbc = AR.alloc([128, D, SC], BF16, "bbc")
    cbc_off = AR.mark()
    cbc = AR.alloc([128, D, SC], BF16, "cbc")
    state = AR.alloc([128, D, SC], BF16, "state")
    da_t = AR.alloc([128, SC], BF16, "da")
    dbu_t = AR.alloc([128, SC], BF16, "dbu")
    u_t = AR.alloc([128, SC], F32, "u_t")
    spl_t = AR.alloc([128, SC], F32, "spl_t")
    yacc = AR.alloc([128, SC], F32, "yacc")
    zred = AR.alloc([128, SC], F32, "zred")
    yfin = AR.alloc([128, SC], BF16, "yfin")
    gwT_sb = AR.alloc([128, 8, E], F32, "gwTsb")
    xln2f = AR.alloc([128, H], F32, "xln2f", at=cbc_off)
    xT_f = AR.alloc([128, 8, 128], F32, "xTf", at=bbc_off)

    for hk in range(8):
        t = P.dma("gpsimd", bounce[:], w_in[hk * 128:(hk + 1) * 128, :], "G")
        P.wtok("gpsimd", t)
        t = P.dma("gpsimd", win_bfh[hk * 128:(hk + 1) * 128, :], bounce[:], "G")
        P.wtok("gpsimd", t)
    for ck in range(CT):
        t = P.dma("gpsimd", bounce[:, 0:H], w_out[ck * 128:(ck + 1) * 128, :], "G")
        P.wtok("gpsimd", t)
        t = P.dma("gpsimd", wout_bfh[ck * 128:(ck + 1) * 128, :], bounce[:, 0:H], "G")
        P.wtok("gpsimd", t)
    for ct in range(CT):
        P.dma("gpsimd", wx_sb[:, ct, :], w_x[ct * 128:(ct + 1) * 128, :], "G")
    wPrepSnap = {n: P.ctr[n] for n in P.pools["G"]["names"]}

    def wait_prep(eng):
        for n, v in wPrepSnap.items():
            P.wait(eng, n, v)

    for hk in range(8):
        P.dma("sync", gwT_sb[:, hk, :], gwT[hk * 128:(hk + 1) * 128, :], "L")
    P.op("vector", lambda e: e.memset(sigma[:], 0.0), "V")

    def ln_norm(src_ap, out_ap, out2_ap=None):
        P.op("vector", lambda e: e.reduce_sum(lnst[:, 2:3], src_ap, axis=AX.X), "V", fence=True)
        P.wait("scalar", "V")
        P.op("scalar", lambda e: e.activation(out=zred[:], in_=src_ap[:, 0:SC],
                                              func=AF.Square, accum_out=lnst[:, 0:1]), "Sc", fence=True)
        P.op("scalar", lambda e: e.activation(out=spl_t[:], in_=src_ap[:, SC:H],
                                              func=AF.Square, accum_out=lnst[:, 1:2]), "Sc", fence=True)
        P.wait("vector", "Sc")
        P.op("vector", lambda e: e.tensor_tensor(out=lnst[:, 3:4], in0=lnst[:, 0:1],
                                                 in1=lnst[:, 1:2], op=ALU.add), "V", fence=True)
        P.op("vector", lambda e: e.tensor_scalar_mul(lnst[:, 2:3], lnst[:, 2:3], 1.0 / H), "V", fence=True)
        P.op("vector", lambda e: e.tensor_scalar_mul(lnst[:, 3:4], lnst[:, 3:4], 1.0 / H), "V", fence=True)
        P.op("vector", lambda e: e.tensor_tensor(out=lnst[:, 4:5], in0=lnst[:, 2:3],
                                                 in1=lnst[:, 2:3], op=ALU.mult), "V", fence=True)
        P.op("vector", lambda e: e.tensor_tensor(out=lnst[:, 5:6], in0=lnst[:, 3:4],
                                                 in1=lnst[:, 4:5], op=ALU.subtract), "V", fence=True)
        P.op("vector", lambda e: e.tensor_scalar_add(lnst[:, 5:6], lnst[:, 5:6], 1e-5), "V", fence=True)
        P.wait("scalar", "V")
        P.op("scalar", lambda e: e.activation(out=lnst[:, 6:7], in_=lnst[:, 5:6],
                                              func=AF.Sqrt), "Sc", fence=True)
        P.wait("vector", "Sc")
        P.op("vector", lambda e: e.reciprocal(lnst[:, 7:8], lnst[:, 6:7]), "V", fence=True)
        P.op("vector", lambda e: e.tensor_scalar(
            out=out_ap, in0=src_ap, scalar1=lnst[:, 2:3], scalar2=lnst[:, 7:8],
            op0=ALU.subtract, op1=ALU.mult), "V", fence=True)
        if out2_ap is not None:
            P.op("vector", lambda e: e.tensor_copy(out=out2_ap, in_=out_ap), "V", fence=True)

    last_wstr_mm = 0
    last_x2_store = None
    for i in range(NCH):
        s0 = i * SC
        P.wall("vector", "G")
        for half in range(2):
            P.wait("sync", "V")
            P.wait("sync", "Sc")
            P.wtok("sync", last_x2_store)
            if i == 0 and half == 0:
                P.wall("sync", "G")
            wX = P.dma("sync", x_t[:],
                       tokrows(x_in, s0 + half * 256, s0 + (half + 1) * 256), "L")
            P.wtok("vector", wX)
            for sb in range(2):
                ln_norm(x_t[:, sb, :], xln_bf[:, half * 2 + sb, :])
        wLN = P.ctr["V"]
        P.wait("tensor", "V", wLN)
        xev = []
        for hb in range(8):
            pt = ptr[hb % 2]
            if hb >= 2:
                P.wait("tensor", "V", xev[hb - 2])
            for sb in range(4):
                P.op("tensor", lambda e, sb=sb, hb=hb, pt=pt: e.transpose(
                    pt[:, sb * 128:(sb + 1) * 128],
                    xln_bf[:, sb, hb * 128:(hb + 1) * 128], ident_bf[:]), "T")
            wT = P.ctr["T"]
            P.wait("vector", "T", wT)
            P.op("vector", lambda e, hb=hb, pt=pt: e.tensor_copy(
                out=xlnT[:, hb, :], in_=pt[:, 0:SC]), "V")
            xev.append(P.ctr["V"])
        wXT = P.ctr["V"]

        if STAGE < 2:
            continue
        P.wait("tensor", "V", wXT)
        P.wait("sync", "T", last_wstr_mm)
        if i == 0:
            P.wall("sync", "G")
        tok_w = {}
        tok_w[0] = P.dma("sync", wstr[0][:],
                         win_bfh[:, 0:128].rearrange("(a p) b -> p a b", p=128), "L")
        evac = {}
        winmm = {}
        for iob in range(32):
            if iob + 1 < 32:
                if iob >= 1:
                    P.wait("sync", "T", winmm[iob - 1])
                tok_w[iob + 1] = P.dma("sync", wstr[(iob + 1) % 2][:],
                      win_bfh[:, (iob + 1) * 128:(iob + 2) * 128].rearrange(
                          "(a p) b -> p a b", p=128), "L")
            pbk = pb[iob % 2]
            P.wtok("tensor", tok_w[iob])
            if iob >= 2:
                w, engsem = evac[iob - 2]
                P.wait("tensor", engsem, w)
            for hk in range(8):
                P.op("tensor", lambda e, hk=hk, iob=iob, pbk=pbk: e.matmul(
                    pbk[:, 0:SC], wstr[iob % 2][:, hk, :], xlnT[:, hk, :],
                    start=(hk == 0), stop=(hk == 7)), "T")
            wMM = winmm[iob] = P.ctr["T"]
            ct = iob % 16
            if iob < 16:
                P.wait("vector", "T", wMM)
                P.op("vector", lambda e, pbk=pbk: e.tensor_copy(
                    out=padded[:, 4:4 + SC], in_=pbk[:, 0:SC]), "V")
                if i == 0:
                    P.op("vector", lambda e: e.memset(padded[:, 1:4], 0.0), "V")
                else:
                    P.op("vector", lambda e, ct=ct: e.tensor_copy(
                        out=padded[:, 1:4], in_=hist[:, ct, :]), "V")
                P.wait("vector", "Sc")
                cbufs = [u_t, spl_t]
                P.op("vector", lambda e, ct=ct: e.tensor_scalar(
                    out=u_t[:], in0=padded[:, 1:1 + SC],
                    scalar1=cwT_sb[:, ct * 4:ct * 4 + 1], scalar2=None, op0=ALU.mult), "V")
                for k in range(1, 4):
                    P.op("vector", lambda e, ct=ct, k=k, cbufs=cbufs: e.scalar_tensor_tensor(
                        out=cbufs[k % 2][:], in0=padded[:, 1 + k:1 + k + SC],
                        scalar=cwT_sb[:, ct * 4 + k:ct * 4 + k + 1],
                        in1=cbufs[(k + 1) % 2][:], op0=ALU.mult, op1=ALU.add), "V")
                P.op("vector", lambda e, ct=ct: e.tensor_copy(
                    out=hist[:, ct, :], in_=padded[:, SC + 1:SC + 4]), "V")
                wConv = P.ctr["V"]
                evac[iob] = (wConv, "V")
                P.wait("scalar", "V", wConv)
                P.op("scalar", lambda e, ct=ct: e.activation(
                    out=hbuf[:, ct, :], in_=spl_t[:], func=AF.Silu,
                    bias=cb_sb[:, ct:ct + 1]), "Sc")
            else:
                P.wait("scalar", "T", wMM)
                P.op("scalar", lambda e, ct=ct, pbk=pbk: e.activation(
                    out=sg_all[:, ct, :], in_=pbk[:, 0:SC], func=AF.Silu), "Sc")
                evac[iob] = (P.ctr["Sc"], "Sc")
        last_win_mm = winmm[31]
        wSiluAll = P.ctr["Sc"]

        if STAGE < 3:
            continue
        P.wait("tensor", "Sc", wSiluAll)
        if i == 0:
            wait_prep("tensor")
        wx_mm = {}
        for ct in range(CT):
            P.op("tensor", lambda e, ct=ct: e.matmul(
                pb[2][0:96, 0:SC], wx_sb[:, ct, :], hbuf[:, ct, :],
                start=(ct == 0), stop=(ct == CT - 1)), "T")
            wx_mm[ct] = P.ctr["T"]
        wWX = P.ctr["T"]
        P.wait("vector", "T", wWX)
        P.op("vector", lambda e: e.tensor_copy(out=ssm_p[:, :], in_=pb[2][0:96, 0:SC]), "V")
        wSP = P.ctr["V"]
        P.wait("gpsimd", "V", wSP)
        tok_bcB = P.dma("gpsimd", bc_B[:, :], ssm_p[64:80, :], "G")
        tok_bcC = P.dma("gpsimd", bc_C[:, :], ssm_p[80:96, :], "G")
        P.op("vector", lambda e: e.tensor_copy(out=ts_aug[0:64, :], in_=ssm_p[0:64, :]), "V")
        P.op("vector", lambda e: e.memset(ts_aug[64:65, :], 1.0), "V")
        wTS = P.ctr["V"]

        P.wait("tensor", "V", wTS)
        dtev = {}
        for cb in range(CT):
            pbk = pb[cb % 2]
            if cb >= 2:
                P.wait("tensor", "Sc", dtev[cb - 2])
            P.op("tensor", lambda e, cb=cb, pbk=pbk: e.matmul(
                pbk[:, 0:SC], wdt_bf[:, cb * 128:(cb + 1) * 128], ts_aug[:],
                start=True, stop=True), "T")
            wDT = P.ctr["T"]
            P.wait("scalar", "T", wDT)
            P.op("scalar", lambda e, pbk=pbk: e.activation(
                out=spl_t[:], in_=pbk[:, 0:SC], func=AF.Exp), "Sc")
            P.op("scalar", lambda e: e.add(out=spl_t[:], in_=spl_t[:], add=1.0), "Sc")
            P.op("scalar", lambda e, cb=cb: e.activation(
                out=dt_all[:, cb, :], in_=spl_t[:], func=AF.Ln), "Sc")
            dtev[cb] = P.ctr["Sc"]
        wDTall = P.ctr["Sc"]

        P.wtok("tensor", tok_bcB)
        P.wtok("tensor", tok_bcC)
        bcev = {}
        for d2 in range(2 * D):
            pbk = pb[3 + (d2 % 2)]
            if d2 >= 2:
                P.wait("tensor", "V", bcev[d2 - 2])
            dd = d2 % D
            src = bc_B if d2 < D else bc_C
            P.op("tensor", lambda e, dd=dd, src=src, pbk=pbk: e.matmul(
                pbk[:, 0:SC], oh16[:, dd * 128:(dd + 1) * 128], src[:, :],
                start=True, stop=True), "T")
            wB = P.ctr["T"]
            P.wait("vector", "T", wB)
            dst = bbc if d2 < D else cbc
            P.op("vector", lambda e, dst=dst, dd=dd, pbk=pbk: e.tensor_copy(
                out=dst[:, dd, :], in_=pbk[:, 0:SC]), "V")
            bcev[d2] = P.ctr["V"]
        wBC = P.ctr["V"]

        if STAGE < 4:
            continue
        P.wait("vector", "Sc", wDTall)
        P.wait("sync", "T", last_win_mm)
        tok_w = {0: P.dma("sync", wstr[0][:],
                          wout_bfh[0:128, :].rearrange("p (a b) -> p a b", a=8), "L")}
        woct = {}
        for ct in range(CT):
            P.op("vector", lambda e, ct=ct: e.tensor_tensor(
                out=u_t[:], in0=dt_all[:, ct, :], in1=hbuf[:, ct, :], op=ALU.mult), "V")
            P.op("vector", lambda e, ct=ct: e.tensor_scalar(
                out=yacc[:], in0=hbuf[:, ct, :], scalar1=dv_sb[:, ct:ct + 1],
                scalar2=None, op0=ALU.mult), "V")
            for d in range(D):
                P.wait("scalar", "V")
                P.op("scalar", lambda e, ct=ct, d=d: e.activation(
                    out=da_t[:], in_=dt_all[:, ct, :], func=AF.Exp,
                    scale=a_sb[:, ct * D + d:ct * D + d + 1]), "Sc")
                wdA = P.ctr["Sc"]
                P.op("vector", lambda e, d=d: e.tensor_tensor(
                    out=dbu_t[:], in0=u_t[:], in1=bbc[:, d, :], op=ALU.mult), "V")
                P.wait("vector", "Sc", wdA)
                P.op("vector", lambda e, ct=ct, d=d: e.tensor_tensor_scan(
                    out=state[:, d, :], data0=da_t[:], data1=dbu_t[:],
                    initial=sigma[:, ct * D + d:ct * D + d + 1],
                    op0=ALU.mult, op1=ALU.add), "V")
            P.op("vector", lambda e, ct=ct: e.tensor_copy(
                out=sigma[:, ct * D:(ct + 1) * D], in_=state[:, :, SC - 1]), "V",
                 fence=True)
            for d in range(D):
                P.op("vector", lambda e, d=d: e.tensor_tensor(
                    out=state[:, d, :], in0=state[:, d, :], in1=cbc[:, d, :],
                    op=ALU.mult), "V")
            P.op("vector", lambda e: e.tensor_reduce(
                out=zred[:], in_=state[:].rearrange("p a b -> p b a"),
                axis=AX.X, op=ALU.add), "V")
            P.op("vector", lambda e: e.tensor_add(yacc[:], yacc[:], zred[:]), "V")
            if ct >= 1:
                P.wait("vector", "T", woct[ct - 1])
            P.op("vector", lambda e, ct=ct: e.tensor_tensor(
                out=yfin[:], in0=yacc[:], in1=sg_all[:, ct, :], op=ALU.mult), "V")
            wYF = P.ctr["V"]
            P.wait("tensor", "V", wYF)
            P.wtok("tensor", tok_w[ct])
            for ob in range(8):
                P.op("tensor", lambda e, ct=ct, ob=ob: e.matmul(
                    pb[ob][:, 0:SC], wstr[ct % 2][:, ob, :], yfin[:],
                    start=(ct == 0), stop=(ct == CT - 1), skip_group_check=True), "T")
            woct[ct] = P.ctr["T"]
            if ct + 1 < CT:
                if ct >= 1:
                    P.wait("sync", "T", woct[ct - 1])
                tok_w[ct + 1] = P.dma("sync", wstr[(ct + 1) % 2][:],
                      wout_bfh[(ct + 1) * 128:(ct + 2) * 128, :].rearrange(
                          "p (a b) -> p a b", a=8), "L")
        last_wstr_mm = woct[CT - 1]
        wWOmm = P.ctr["T"]

        if STAGE < 5:
            continue
        P.wait("vector", "T", wWOmm)
        for ob in range(8):
            P.op("vector", lambda e, ob=ob: e.tensor_copy(
                out=mo_bf[:, ob, :], in_=pb[ob][:, 0:SC]), "V")
        wMO = P.ctr["V"]
        for half in range(2):
            P.wait("sync", "V")
            P.wait("sync", "Sc")
            P.wtok("sync", last_x2_store)
            wXR = P.dma("sync", x_t[:],
                        tokrows(x_in, s0 + half * 256, s0 + (half + 1) * 256), "L")
            P.wait("tensor", "V", wMO)
            P.wtok("vector", wXR)
            for sb2 in range(2):
                sb = half * 2 + sb2
                pt = ptr[sb2 % 2]
                if half + sb2 > 0:
                    P.wait("tensor", "V")
                for ob in range(8):
                    P.op("tensor", lambda e, sb=sb, ob=ob, pt=pt: e.transpose(
                        pt[:, ob * 128:(ob + 1) * 128],
                        mo_bf[:, ob, sb * 128:(sb + 1) * 128], ident_bf[:]), "T")
                wT = P.ctr["T"]
                P.wait("vector", "T", wT)
                P.op("vector", lambda e, sb2=sb2, pt=pt: e.tensor_tensor(
                    out=x_t[:, sb2, :], in0=x_t[:, sb2, :], in1=pt[:, 0:H], op=ALU.add), "V")
            wX2h = P.ctr["V"]
            P.wait("gpsimd", "V", wX2h)
            last_x2_store = P.dma(
                "gpsimd", tokrows(x2_sp, s0 + half * 256, s0 + (half + 1) * 256),
                x_t[:], "G")
            for sb2 in range(2):
                sb = half * 2 + sb2
                ln_norm(x_t[:, sb2, :], xln2f[:], out2_ap=xln2_t[:, sb, :])
                wXF = P.ctr["V"]
                P.wait("tensor", "V", wXF)
                rtev = {}
                for hb in range(8):
                    pbs = pb[4 + (hb % 2)]
                    if hb >= 2:
                        P.wait("tensor", "V", rtev[hb - 2])
                    P.op("tensor", lambda e, hb=hb, pbs=pbs: e.transpose(
                        pbs[:, 0:128], xln2f[:, hb * 128:(hb + 1) * 128], ident_f[:]), "T")
                    wT = P.ctr["T"]
                    P.wait("vector", "T", wT)
                    P.op("vector", lambda e, hb=hb, pbs=pbs: e.tensor_copy(
                        out=xT_f[:, hb, :], in_=pbs[:, 0:128]), "V")
                    rtev[hb] = P.ctr["V"]
                wXTF = P.ctr["V"]
                P.wait("tensor", "V", wXTF)
                for hb in range(8):
                    P.op("tensor", lambda e, hb=hb: e.matmul(
                        pb[3][0:E, 0:128], gwT_sb[:, hb, :], xT_f[:, hb, :],
                        start=(hb == 0), stop=(hb == 7)), "T")
                wRT = P.ctr["T"]
                P.wait("vector", "T", wRT)
                P.op("vector", lambda e, sb=sb: e.tensor_copy(
                    out=lg_ev[:, sb * 128:(sb + 1) * 128], in_=pb[3][0:E, 0:128]), "V")
        wLG = P.ctr["V"]
        P.wait("tensor", "V", wLG)
        lgev2 = {}
        for sb in range(4):
            pbs = pb[4 + (sb % 2)]
            if sb >= 2:
                P.wait("tensor", "V", lgev2[sb - 2])
            P.op("tensor", lambda e, sb=sb, pbs=pbs: e.transpose(
                pbs[0:128, 0:64], lg_ev[0:E, sb * 128:(sb + 1) * 128],
                ident_f[0:64, 0:64]), "T")
            wT = P.ctr["T"]
            P.wait("vector", "T", wT)
            gsb = i * 4 + sb
            P.op("vector", lambda e, gsb=gsb, pbs=pbs: e.tensor_copy(
                out=logits[:, gsb, :], in_=pbs[:, 0:64]), "V")
            lgev2[sb] = P.ctr["V"]
        wXL = P.ctr["V"]
        P.wait("gpsimd", "V", wXL)
        P.dma("gpsimd", tokrows(xln2_sp, s0, s0 + SC), xln2_t[:], "G")
        P.wait("tensor", "V", wXL)
        xt2 = {}
        for hb in range(8):
            pt = ptr[hb % 2]
            if hb >= 2:
                P.wait("tensor", "V", xt2[hb - 2])
            for sb in range(4):
                P.op("tensor", lambda e, sb=sb, hb=hb, pt=pt: e.transpose(
                    pt[:, sb * 128:(sb + 1) * 128],
                    xln2_t[:, sb, hb * 128:(hb + 1) * 128], ident_bf[:]), "T")
            wT = P.ctr["T"]
            P.wait("vector", "T", wT)
            P.op("vector", lambda e, hb=hb, pt=pt: e.tensor_copy(
                out=xlnTb[:, hb, :], in_=pt[:, 0:SC]), "V")
            xt2[hb] = P.ctr["V"]
        wXT2 = P.ctr["V"]
        P.wait("gpsimd", "V", wXT2)
        P.dma("gpsimd", xln2T_sp[:, s0:s0 + SC].rearrange("(a p) b -> p a b", p=128),
              xlnTb[:], "G")

    dbg_refs = {}

    def _phase2():
        _cut(1)
        AR.pop()

        scu = AR.alloc([128, SB, 8], F32, "scu")
        ycu = AR.alloc([128, SB, 8], F32, "ycu")
        wk8 = AR.alloc([128, SB, 8], F32, "wk8")
        scu_u = AR.alloc([128, SB, 8], U32, "scu_u")
        ycu_u = AR.alloc([128, SB, 8], U32, "ycu_u")
        tokid = AR.alloc([128, SB], F32, "tokid")
        dbg_refs.update(scu=scu, ycu=ycu, wk8=wk8, tokid=tokid)
        tb1 = AR.alloc([128, 1], F32, "tb1")
        ebias_bc = AR.alloc([128, E], F32, "ebias_bc")
        e_bias_sb = nc.alloc_sbuf_tensor("e_bias_sb", [1, E], F32).ap()
        cm_sb = nc.alloc_sbuf_tensor("cm_sb", [8, 1], F32).ap()
        tbase_sb = nc.alloc_sbuf_tensor("tbase_sb", [1, 1], F32).ap()
        cnts_sb = nc.alloc_sbuf_tensor("cnts_sb", [8, E], F32).ap()
        ones1f = nc.alloc_sbuf_tensor("ones1f", [1, 128], F32).ap()
        cnt1 = nc.alloc_sbuf_tensor("cnt1", [1, E], F32).ap()

        AR.push()
        NHT = SB // 2
        indT_f = AR.alloc([128, S], F32, "indT")
        posT_f = AR.alloc([128, S], F32, "posT")
        keepT_f = AR.alloc([128, S], F32, "keepT")
        indT = indT_f[0:64, :]
        posT = posT_f[0:64, :]
        keepT = keepT_f[0:64, :]
        base64 = AR.alloc([128, 1], F32, "base64")
        scoresh = AR.alloc([128, NHT, E], F32, "scoresh")
        sfch = AR.alloc([128, NHT, E], F32, "sfch")
        tmpA = AR.alloc([128, NHT, E], F32, "tmpA")
        tmpB = AR.alloc([128, NHT, E], F32, "tmpB")
        indh = AR.alloc([128, NHT, E], F32, "indh")
        rkA = AR.alloc([128, NHT, E], F32, "rkA")
        rkB = AR.alloc([128, NHT, E], F32, "rkB")
        eqh = AR.alloc([128, NHT, E], F32, "eqh")
        ivh = AR.alloc([128, NHT, E], F32, "ivh")
        io64 = AR.alloc([128, NHT, E], F32, "io64")
        io64b = AR.alloc([128, NHT, E], F32, "io64b")
        io8 = AR.alloc([128, NHT, E], F32, "io8")
        io8b = AR.alloc([128, NHT, E], F32, "io8b")
        ebase = AR.alloc([128, NHT, E], F32, "ebase")
        ybase = AR.alloc([128, NHT, E], F32, "ybase")
        sidh = AR.alloc([128, NHT, E], F32, "sidh")
        wn_h0 = AR.alloc([128, NHT, E], F32, "wn_h0")
        rk_h0 = AR.alloc([128, NHT, E], F32, "rk_h0")
        posh = AR.alloc([128, NHT, E], F32, "posh")
        keeph = AR.alloc([128, NHT, E], F32, "keeph")
        wnh = AR.alloc([128, NHT, E], F32, "wnh")
        gmh = AR.alloc([128, NHT, 8], F32, "gmh")
        gm2h = AR.alloc([128, NHT, 8], F32, "gm2h")
        gsA = AR.alloc([128, NHT, 8], F32, "gsA")
        gsB = AR.alloc([128, NHT, 8], F32, "gsB")
        gmask = AR.alloc([128, NHT, 8], F32, "gmask")
        iog = AR.alloc([128, NHT, 8], F32, "iog")
        iogb = AR.alloc([128, NHT, 8], F32, "iogb")
        m32 = AR.alloc([128, NHT], F32, "m32")
        mi32 = AR.alloc([128, NHT], F32, "mi32")
        payk = AR.alloc([128, SB, 2], F32, "payk")
        ztile = AR.alloc([128, 512], F32, "ztile")

        def vop(fn, fence=True):
            return P.op("vector", fn, "V", fence=fence)

        for ap_, pat, base in (
            (io64, [[0, NHT], [1, E]], 0),
            (io64b, [[0, NHT], [1, E]], int(BIGIDX)),
            (io8, [[0, NHT], [0, 8], [1, 8]], 0),
            (io8b, [[0, NHT], [0, 8], [1, 8]], int(BIGIDX)),
            (iog, [[0, NHT], [1, 8]], 0),
            (iogb, [[0, NHT], [1, 8]], int(BIGIDX)),
            (ebase, [[0, NHT], [CAP, E]], 0),
            (ybase, [[0, NHT], [YBLK, 8], [CAP, 8]], 0),
        ):
            P.op("gpsimd", lambda e, ap_=ap_, pat=pat, base=base: e.iota(
                ap_[:], pattern=pat, base=base, channel_multiplier=0,
                allow_small_or_imprecise_dtypes=True), "Pg")
        P.op("gpsimd", lambda e: e.iota(tokid[:], pattern=[[128, SB]], base=0,
                                        channel_multiplier=1,
                                        allow_small_or_imprecise_dtypes=True), "Pg")
        wIotas = P.ctr["Pg"]

        tok_eb = P.dma("sync", e_bias_sb[:, :], e_bias[:, :], "L")
        tok_cm = P.dma("sync", cm_sb[:, :], coremask[:, :], "L")
        tok_tb = P.dma("sync", tbase_sb[:, :], tokbase[:, :], "L")

        P.op("vector", lambda e: e.memset(ones1f[:], 1.0), "V")
        wOnes = P.ctr["V"]
        P.wtok("tensor", tok_eb)
        P.wtok("tensor", tok_tb)
        P.wait("tensor", "V", wOnes)
        P.op("tensor", lambda e: e.matmul(pb[4][:, 0:E], ones1f[:], e_bias_sb[:, :],
                                          start=True, stop=True), "T")
        P.op("tensor", lambda e: e.matmul(pb[5][:, 0:1], ones1f[:], tbase_sb[:, :],
                                          start=True, stop=True), "T")
        wEB = P.ctr["T"]
        P.wait("vector", "T", wEB)
        vop(lambda e: e.tensor_copy(out=ebias_bc[:], in_=pb[4][:, 0:E]))
        vop(lambda e: e.tensor_copy(out=tb1[:], in_=pb[5][:, 0:1]))
        P.wait("vector", "Pg", wIotas)
        vop(lambda e: e.tensor_scalar(out=tokid[:], in0=tokid[:], scalar1=tb1[:, 0:1],
                                      scalar2=None, op0=ALU.add))

        rk_ref = {}
        wn_ref = {}
        for h in range(2):
            lgv = logits[:, h * NHT:(h + 1) * NHT, :]
            P.wait("scalar", "V")
            P.op("scalar", lambda e, lgv=lgv: e.activation(out=scoresh[:], in_=lgv,
                                                           func=AF.Sigmoid), "Sc")
            P.wait("vector", "Sc")
            vop(lambda e: e.tensor_tensor(
                out=sfch[:], in0=scoresh[:],
                in1=ebias_bc[:, None, :].to_broadcast([128, NHT, E]),
                op=ALU.add))
            vop(lambda e: e.tensor_reduce(
                out=gmh[:], in_=sfch[:].rearrange("p a (g f) -> p a g f", g=8),
                axis=AX.X, op=ALU.max))
            vop(lambda e: e.tensor_tensor(
                out=eqh[:].rearrange("p a (g f) -> p a g f", g=8),
                in0=sfch[:].rearrange("p a (g f) -> p a g f", g=8),
                in1=gmh[:, :, :, None].to_broadcast([128, NHT, 8, 8]),
                op=ALU.is_equal))
            vop(lambda e: e.scalar_tensor_tensor(
                out=ivh[:], in0=eqh[:], scalar=-BIGIDX, in1=io8b[:],
                op0=ALU.mult, op1=ALU.add))
            vop(lambda e: e.tensor_reduce(
                out=gm2h[:], in_=ivh[:].rearrange("p a (g f) -> p a g f", g=8),
                axis=AX.X, op=ALU.min))
            vop(lambda e: e.tensor_tensor(
                out=eqh[:].rearrange("p a (g f) -> p a g f", g=8),
                in0=io8[:].rearrange("p a (g f) -> p a g f", g=8),
                in1=gm2h[:, :, :, None].to_broadcast([128, NHT, 8, 8]),
                op=ALU.is_equal))
            vop(lambda e: e.scalar_tensor_tensor(
                out=tmpA[:], in0=eqh[:], scalar=-BIGF, in1=sfch[:],
                op0=ALU.mult, op1=ALU.add))
            vop(lambda e: e.tensor_reduce(
                out=gm2h[:], in_=tmpA[:].rearrange("p a (g f) -> p a g f", g=8),
                axis=AX.X, op=ALU.max))
            vop(lambda e: e.tensor_tensor(out=gsA[:], in0=gmh[:], in1=gm2h[:], op=ALU.add))
            vop(lambda e: e.memset(gmask[:], 0.0))
            cur, nxt = gsA, gsB
            for r4 in range(4):
                vop(lambda e, cur=cur: e.tensor_reduce(out=m32[:], in_=cur[:], axis=AX.X,
                                                       op=ALU.max))
                vop(lambda e, cur=cur: e.tensor_tensor(
                    out=gm2h[:], in0=cur[:],
                    in1=m32[:, :, None].to_broadcast([128, NHT, 8]),
                    op=ALU.is_equal))
                vop(lambda e: e.scalar_tensor_tensor(
                    out=gm2h[:], in0=gm2h[:], scalar=-BIGIDX, in1=iogb[:],
                    op0=ALU.mult, op1=ALU.add))
                vop(lambda e: e.tensor_reduce(out=mi32[:], in_=gm2h[:], axis=AX.X,
                                              op=ALU.min))
                vop(lambda e: e.tensor_tensor(
                    out=gm2h[:], in0=iog[:],
                    in1=mi32[:, :, None].to_broadcast([128, NHT, 8]),
                    op=ALU.is_equal))
                vop(lambda e: e.tensor_add(gmask[:], gmask[:], gm2h[:]))
                vop(lambda e, cur=cur, nxt=nxt: e.scalar_tensor_tensor(
                    out=nxt[:], in0=gm2h[:], scalar=-BIGF, in1=cur[:],
                    op0=ALU.mult, op1=ALU.add))
                cur, nxt = nxt, cur
            vop(lambda e: e.tensor_tensor(
                out=tmpA[:].rearrange("p a (g f) -> p a g f", g=8),
                in0=sfch[:].rearrange("p a (g f) -> p a g f", g=8),
                in1=gmask[:, :, :, None].to_broadcast([128, NHT, 8, 8]),
                op=ALU.mult))
            vop(lambda e: e.memset(rkA[:], BIGIDX))
            vop(lambda e: e.memset(indh[:], 0.0))
            curt, nxtt = tmpA, tmpB
            curr, nxtr = rkA, rkB
            for k in range(8):
                vop(lambda e, curt=curt: e.tensor_reduce(out=m32[:], in_=curt[:],
                                                         axis=AX.X, op=ALU.max))
                vop(lambda e, curt=curt: e.tensor_tensor(
                    out=eqh[:], in0=curt[:],
                    in1=m32[:, :, None].to_broadcast([128, NHT, E]),
                    op=ALU.is_equal))
                vop(lambda e: e.scalar_tensor_tensor(
                    out=ivh[:], in0=eqh[:], scalar=-BIGIDX, in1=io64b[:],
                    op0=ALU.mult, op1=ALU.add))
                vop(lambda e: e.tensor_reduce(out=mi32[:], in_=ivh[:], axis=AX.X,
                                              op=ALU.min))
                vop(lambda e: e.tensor_tensor(
                    out=eqh[:], in0=io64[:],
                    in1=mi32[:, :, None].to_broadcast([128, NHT, E]),
                    op=ALU.is_equal))
                vop(lambda e: e.tensor_add(indh[:], indh[:], eqh[:]))
                vop(lambda e, k=k: e.tensor_scalar(
                    out=ivh[:], in0=eqh[:], scalar1=-BIGIDX, scalar2=float(k) + BIGIDX,
                    op0=ALU.mult, op1=ALU.add))
                vop(lambda e, curr=curr, nxtr=nxtr: e.tensor_tensor(
                    out=nxtr[:], in0=curr[:], in1=ivh[:], op=ALU.min))
                curr, nxtr = nxtr, curr
                vop(lambda e, curt=curt, nxtt=nxtt: e.scalar_tensor_tensor(
                    out=nxtt[:], in0=eqh[:], scalar=-BIGF, in1=curt[:],
                    op0=ALU.mult, op1=ALU.add))
                curt, nxtt = nxtt, curt
            vop(lambda e: e.tensor_tensor(out=wnh[:], in0=scoresh[:], in1=indh[:],
                                          op=ALU.mult))
            vop(lambda e: e.tensor_reduce(out=m32[:], in_=wnh[:], axis=AX.X, op=ALU.add))
            vop(lambda e: e.tensor_scalar_add(m32[:], m32[:], 1e-20))
            vop(lambda e: e.reciprocal(mi32[:], m32[:]))
            vop(lambda e: e.tensor_scalar_mul(mi32[:], mi32[:], 2.5))
            vop(lambda e: e.tensor_tensor(
                out=wnh[:], in0=wnh[:],
                in1=mi32[:, :, None].to_broadcast([128, NHT, E]),
                op=ALU.mult))
            wRouteH = P.ctr["V"]
            P.wait("tensor", "V", wRouteH)
            trev = {}
            for sbl in range(NHT):
                gsb = h * NHT + sbl
                pbs = pb[4 + (sbl % 2)]
                if sbl >= 2:
                    P.wait("tensor", "V", trev[sbl - 2])
                P.op("tensor", lambda e, sbl=sbl, pbs=pbs: e.transpose(
                    pbs[0:E, 0:128], indh[:, sbl, :], ident_f[:]), "T")
                wT = P.ctr["T"]
                P.wait("vector", "T", wT)
                P.op("vector", lambda e, gsb=gsb, pbs=pbs: e.tensor_copy(
                    out=indT[:, gsb * 128:(gsb + 1) * 128], in_=pbs[0:E, 0:128]), "V")
                trev[sbl] = P.ctr["V"]
            if h == 0:
                vop(lambda e: e.tensor_copy(out=wn_h0[:], in_=wnh[:]))
                vop(lambda e, curr=curr: e.tensor_copy(out=rk_h0[:], in_=curr[:]))
                rk_ref[0] = rk_h0
                wn_ref[0] = wn_h0
            else:
                rk_ref[1] = curr
                wn_ref[1] = wnh
        wRouteAll = P.ctr["V"]

        P.wait("vector", "V", wRouteAll)
        vop(lambda e: e.tensor_tensor_scan(
            out=posT[:], data0=indT[:], data1=indT[:], initial=0.0,
            op0=ALU.add, op1=ALU.bypass))
        wCnt = P.ctr["V"]
        P.wait("tensor", "V", wCnt)
        P.op("tensor", lambda e: e.transpose(pb[4][0:1, 0:E], posT[:, S - 1:S],
                                             ident_f[0:64, 0:64]), "T")
        wCntT = P.ctr["T"]
        P.wait("vector", "T", wCntT)
        vop(lambda e: e.tensor_copy(out=cnt1[:, :], in_=pb[4][0:1, 0:E]))
        wCnt1 = P.ctr["V"]
        vop(lambda e: e.tensor_tensor(out=posT[:], in0=posT[:], in1=indT[:],
                                      op=ALU.subtract))
        wPosT = P.ctr["V"]
        _cut(1.1)
        P.wait("gpsimd", "V", wCnt1)
        tok_ci = P.dma("gpsimd", cnt_in[0:1, :], cnt1[:, :], "G")
        P.wtok("gpsimd", tok_ci)
        _cut(1.2)
        P.coll(lambda e: e.collective_compute(
            "AllGather", ALU.bypass, replica_groups=[list(range(NCORE))],
            ins=[cnt_in.opt()], outs=[cnt_out.opt()]))
        _cut(1.4)
        P.wait("sync", "CC")
        tok_cnts = P.dma("sync", cnts_sb[:, :], cnt_out[:, :], "L")
        P.wtok("tensor", tok_cnts)
        _cut(1.6)
        P.wtok("tensor", tok_cm)
        P.op("tensor", lambda e: e.matmul(pb[4][0:E, 0:1], cnts_sb[:, :], cm_sb[:, :],
                                          start=True, stop=True), "T")
        wBase = P.ctr["T"]
        P.wait("vector", "T", wBase)
        vop(lambda e: e.tensor_copy(out=base64[:, 0:1], in_=pb[4][:, 0:1]))
        _cut(1.7)
        P.wait("vector", "V", wPosT)
        vop(lambda e: e.tensor_scalar(out=posT[:], in0=posT[:],
                                      scalar1=base64[0:64, 0:1], scalar2=None, op0=ALU.add))
        _cut(1.8)
        vop(lambda e: e.tensor_scalar(out=keepT[:], in0=posT[:], scalar1=float(CAP),
                                      scalar2=None, op0=ALU.is_lt))
        _cut(1.9)
        vop(lambda e: e.tensor_tensor(out=keepT[:], in0=keepT[:], in1=indT[:],
                                      op=ALU.mult))
        wKeep = P.ctr["V"]
        _cut(2)

        P.wait("tensor", "V", wKeep)
        trev = {}
        for h in range(2):
            if h == 1:
                _cut(2.478)
            for sbl in range(NHT):
                gsb = h * NHT + sbl
                pbs = pb[4 + (sbl % 2)]
                pbs2 = pb[2 + (sbl % 2)]
                if gsb >= 2:
                    P.wait("tensor", "V", trev[gsb - 2])
                P.op("tensor", lambda e, gsb=gsb, pbs=pbs: e.transpose(
                    pbs[0:128, 0:E], posT[:, gsb * 128:(gsb + 1) * 128],
                    ident_f[0:64, 0:64]), "T")
                P.op("tensor", lambda e, gsb=gsb, pbs2=pbs2: e.transpose(
                    pbs2[0:128, 0:E], keepT[:, gsb * 128:(gsb + 1) * 128],
                    ident_f[0:64, 0:64]), "T")
                wT = P.ctr["T"]
                P.wait("vector", "T", wT)
                P.op("vector", lambda e, sbl=sbl, pbs=pbs: e.tensor_copy(
                    out=posh[:, sbl, :], in_=pbs[0:128, 0:E]), "V")
                P.op("vector", lambda e, sbl=sbl, pbs2=pbs2: e.tensor_copy(
                    out=keeph[:, sbl, :], in_=pbs2[0:128, 0:E]), "V")
                trev[gsb] = P.ctr["V"]
            _cut(2.2)
            vop(lambda e: e.tensor_tensor(out=sidh[:], in0=ebase[:], in1=posh[:],
                                          op=ALU.add))
            vop(lambda e: e.tensor_scalar_add(sidh[:], sidh[:], -BIGIDX))
            vop(lambda e: e.tensor_tensor(out=sidh[:], in0=sidh[:], in1=keeph[:],
                                          op=ALU.mult))
            vop(lambda e: e.tensor_scalar_add(sidh[:], sidh[:], BIGIDX))
            _cut(2.3)
            vop(lambda e: e.tensor_tensor(out=ivh[:], in0=ybase[:], in1=posh[:],
                                          op=ALU.add))
            _cut(2.32)
            vop(lambda e: e.tensor_scalar_add(ivh[:], ivh[:], -ZROWID))
            _cut(2.34)
            vop(lambda e: e.tensor_tensor(out=ivh[:], in0=ivh[:], in1=keeph[:],
                                          op=ALU.mult))
            _cut(2.36)
            vop(lambda e: e.tensor_scalar_add(ivh[:], ivh[:], ZROWID))
            _cut(2.4)
            for _d in range(DUMMYV if h == 0 else 0):
                vop(lambda e: e.tensor_scalar_add(m32[:], m32[:], 0.0))
            for _d in range(DUMMYR if h == 0 else 0):
                vop(lambda e: e.tensor_reduce(out=m32[:], in_=tmpA[:], axis=AX.X,
                                              op=ALU.add))
            for k in range(8):
                _cut(2.4 + 0.01 * (h * 8 + k))
                sk = K7SKIP if (h == 0 and k == 7) else 0
                if sk & 1 == 0:
                    vop(lambda e, h=h, k=k: e.tensor_scalar(
                        out=eqh[:], in0=rk_ref[h][:], scalar1=float(k), scalar2=None,
                        op0=ALU.is_equal))
                if sk & 2 == 0:
                    vop(lambda e: e.tensor_tensor(out=tmpA[:], in0=eqh[:], in1=sidh[:],
                                                  op=ALU.mult))
                    vop(lambda e: e.tensor_tensor(out=tmpB[:], in0=eqh[:], in1=ivh[:],
                                                  op=ALU.mult)) if False else None
                if sk & 4 == 0:
                    vop(lambda e: e.tensor_reduce(out=m32[:], in_=tmpA[:], axis=AX.X,
                                                  op=ALU.add))
                if sk & 8 == 0:
                    vop(lambda e, h=h, k=k: e.tensor_copy(
                        out=scu[:, h * NHT:(h + 1) * NHT, k], in_=m32[:]))
                if sk & 2 == 0:
                    vop(lambda e: e.tensor_tensor(out=tmpA[:], in0=eqh[:], in1=ivh[:],
                                                  op=ALU.mult))
                if sk & 4 == 0:
                    vop(lambda e: e.tensor_reduce(out=m32[:], in_=tmpA[:], axis=AX.X,
                                                  op=ALU.add))
                if sk & 8 == 0:
                    vop(lambda e, h=h, k=k: e.tensor_copy(
                        out=ycu[:, h * NHT:(h + 1) * NHT, k], in_=m32[:]))
                if sk & 2 == 0:
                    vop(lambda e, h=h: e.tensor_tensor(out=tmpA[:], in0=eqh[:],
                                                       in1=wn_ref[h][:], op=ALU.mult))
                if sk & 4 == 0:
                    vop(lambda e: e.tensor_reduce(out=m32[:], in_=tmpA[:], axis=AX.X,
                                                  op=ALU.add))
                if sk & 8 == 0:
                    vop(lambda e, h=h, k=k: e.tensor_copy(
                        out=wk8[:, h * NHT:(h + 1) * NHT, k], in_=m32[:]))
        _cut(2.59)
        vop(lambda e: e.tensor_copy(out=scu_u[:], in_=scu[:]))
        vop(lambda e: e.tensor_copy(out=ycu_u[:], in_=ycu[:]))
        wExtract = P.ctr["V"]

        if debug:
            P.wait("gpsimd", "V", wExtract)
            P.dma("gpsimd", pos_db[:, :].rearrange("(p f) b -> p (f b)", p=64),
                  posT[:], "G")
            P.dma("gpsimd", ind_db[:, :].rearrange("(p f) b -> p (f b)", p=64),
                  indT[:], "G")
            for nm2, ap2 in (("sfch_db", sfch[:]), ("gmask_db", gmask[:]),
                             ("indh_db", indh[:]), ("rk1_db", rk_ref[1][:]),
                             ("wn1_db", wn_ref[1][:]), ("rk0_db", rk_h0[:]),
                             ("wn0_db", wn_h0[:]), ("io64_db", io64[:]),
                             ("io64b_db", io64b[:]), ("iog_db", iog[:]),
                             ("iogb_db", iogb[:]), ("io8_db", io8[:]),
                             ("io8b_db", io8b[:]), ("ebase_db", ebase[:]),
                             ("ybase_db", ybase[:]), ("posh_db", posh[:]),
                             ("keeph_db", keeph[:]), ("sidh_db", sidh[:]),
                             ("ivh_db", ivh[:]), ("scoresh_db", scoresh[:]),
                             ("base64_db", base64[:]), ("tokid2_db", tokid[:])):
                t2 = nc.dram_tensor(nm2, [128] + [int(x) for x in ap2.shape[1:]],
                                    ap2.dtype, kind="ExternalOutput").ap()
                P.dma("gpsimd", t2[tuple([slice(None)] * len(ap2.shape))], ap2, "G")
            P.wall("gpsimd", "G")

        _cut(3)
        vop(lambda e: e.memset(ztile[:], 0.0), fence=False)
        wZ = P.ctr["V"]
        P.wait("gpsimd", "V", wZ)
        slabv = slab_in.rearrange("(p a) c -> p a c", p=128)
        ztoks = []
        for j in range(2):
            ztoks.append(P.dma(
                "gpsimd", slabv[:, j * 256:(j + 1) * 256, :],
                ztile[:].rearrange("p (a c) -> p a c", c=2), "G"))
        for t in ztoks:
            P.wtok("gpsimd", t)
        _cut(3.2)
        P.wait("vector", "V", wExtract)
        bc_cell = {}

        def _mkbc(e):
            r = e.alloc_register(name="bcreg")
            e.reg_mov(r, E * CAP - 1)
            bc_cell["r"] = r
        P.raw("gpsimd", _mkbc)
        for k in range(8):
            P.wall("vector", "G")
            vop(lambda e: e.tensor_copy(out=payk[:, :, 0:1],
                                        in_=tokid[:, :, None]))
            vop(lambda e, k=k: e.tensor_copy(out=payk[:, :, 1:2], in_=wk8[:, :, k:k + 1]))
            wPay = P.ctr["V"]
            P.wait("gpsimd", "V", wPay)
            for sb2 in range(SB):
                P.ind(lambda e, k=k, sb2=sb2: e.indirect_dma_start(
                    out=slab_in[:, :],
                    out_offset=IndirectOffsetOnAxis(scu_u[:, sb2, k:k + 1], 0),
                    in_=payk[:, sb2, :], in_offset=None,
                    bounds_check=bc_cell["r"], oob_is_err=False))
        P.wall("gpsimd", "G")
        _cut(3.4)
        P.wait("gpsimd", "CC")
        if not SKIPRS:
            P.coll(lambda e: e.collective_compute(
                "ReduceScatter", ALU.add, replica_groups=[list(range(NCORE))],
                ins=[slab_in.opt()], outs=[slab_out.opt()]))
        wRS = P.ctr["CC"]
        _cut(3.6)
        P.wait("gpsimd", "CC", wRS)
        P.coll(lambda e: e.collective_compute(
            "AllGather", ALU.bypass, replica_groups=[list(range(NCORE))],
            ins=[xln2_sp.opt()], outs=[xg_out.opt()]))
        wAGX = P.ctr["CC"]

        _cut(4)
        AR.pop()
        AR.push()
        sgT_sb = AR.alloc([128, 8, F2], BF16, "sgTsb")
        suT_sb = AR.alloc([128, 8, F2], BF16, "suTsb")
        sdT_sb = AR.alloc([128, 8, H], BF16, "sdTsb")
        xt2T = AR.alloc([128, 8, 1024], BF16, "xt2T")
        x2h = AR.alloc([128, 4, H], F32, "x2h")
        gh = AR.alloc([128, 8, 1024], BF16, "gh")
        uh = AR.alloc([128, 8, 1024], BF16, "uh")
        dsh = AR.alloc([128, 8, 1024], BF16, "dsh")

        for hk in range(8):
            P.dma("gpsimd", sgT_sb[:, hk, :], sgT[hk * 128:(hk + 1) * 128, :], "G")
            P.dma("gpsimd", suT_sb[:, hk, :], suT[hk * 128:(hk + 1) * 128, :], "G")
            P.dma("gpsimd", sdT_sb[:, hk, :], sdT[hk * 128:(hk + 1) * 128, :], "G")
        P.wall("tensor", "G")
        P.wall("sync", "G")

        padd = {}
        tok_x3store = [None]
        wGU = [None]
        for p2 in range(4):
            s0 = p2 * 1024
            if wGU[0] is not None:
                P.wait("sync", "T", wGU[0])
            tok_x = P.dma("sync", xt2T[:],
                          xln2T_sp[:, s0:s0 + 1024].rearrange("(a p) b -> p a b", p=128), "L")
            P.wtok("tensor", tok_x)
            for dst, wsb in ((gh, sgT_sb), (uh, suT_sb)):
                for f2b in range(8):
                    pbk = pb[f2b % 2]
                    for half in range(2):
                        for hk in range(8):
                            P.op("tensor", lambda e, wsb=wsb, f2b=f2b, hk=hk, half=half,
                                 pbk=pbk: e.matmul(
                                     pbk[:, 0:512], wsb[:, hk, f2b * 128:(f2b + 1) * 128],
                                     xt2T[:, hk, half * 512:(half + 1) * 512],
                                     start=(hk == 0), stop=(hk == 7),
                                     skip_group_check=True), "T")
                        wMM = P.ctr["T"]
                        if dst is gh:
                            P.wait("scalar", "T", wMM)
                            P.op("scalar", lambda e, dst=dst, f2b=f2b, half=half, pbk=pbk:
                                 e.activation(out=dst[:, f2b, half * 512:(half + 1) * 512],
                                              in_=pbk[:, 0:512], func=AF.Silu), "Sc")
                            wEv = P.ctr["Sc"]
                            P.wait("tensor", "Sc", wEv)
                        else:
                            P.wait("vector", "T", wMM)
                            P.op("vector", lambda e, dst=dst, f2b=f2b, half=half, pbk=pbk:
                                 e.tensor_copy(out=dst[:, f2b, half * 512:(half + 1) * 512],
                                               in_=pbk[:, 0:512]), "V")
                            wEv = P.ctr["V"]
                            P.wait("tensor", "V", wEv)
            wUp = P.ctr["V"]
            wGU[0] = P.ctr["T"]
            P.wait("vector", "Sc")
            P.op("vector", lambda e: e.tensor_tensor(out=gh[:], in0=gh[:], in1=uh[:],
                                                     op=ALU.mult), "V", fence=True)
            wHH = P.ctr["V"]
            P.wait("tensor", "V", wHH)
            for hb in range(8):
                pbk = pb[hb % 2]
                for half in range(2):
                    for fk in range(8):
                        P.op("tensor", lambda e, hb=hb, fk=fk, half=half, pbk=pbk:
                             e.matmul(pbk[:, 0:512], sdT_sb[:, fk, hb * 128:(hb + 1) * 128],
                                      gh[:, fk, half * 512:(half + 1) * 512],
                                      start=(fk == 0), stop=(fk == 7),
                                      skip_group_check=True), "T")
                    wMM = P.ctr["T"]
                    P.wait("vector", "T", wMM)
                    P.op("vector", lambda e, hb=hb, half=half, pbk=pbk: e.tensor_copy(
                        out=dsh[:, hb, half * 512:(half + 1) * 512], in_=pbk[:, 0:512]),
                        "V")
                    P.wait("tensor", "V", P.ctr["V"])
            wDS = P.ctr["V"]
            for half in range(2):
                if tok_x3store[0] is not None:
                    P.wtok("sync", tok_x3store[0])
                tok_x2 = P.dma("sync", x2h[:],
                               tokrows(x2_sp, s0 + half * 512, s0 + (half + 1) * 512), "L")
                P.wait("tensor", "V", wDS)
                P.wtok("vector", tok_x2)
                for sb2 in range(4):
                    pt = ptr[sb2 % 2]
                    if padd.get(sb2 % 2) is not None:
                        P.wait("tensor", "V", padd[sb2 % 2])
                    for ob in range(8):
                        P.op("tensor", lambda e, sb2=sb2, ob=ob, half=half, pt=pt:
                             e.transpose(pt[:, ob * 128:(ob + 1) * 128],
                                         dsh[:, ob, half * 512 + sb2 * 128:
                                             half * 512 + (sb2 + 1) * 128],
                                         ident_bf[:]), "T")
                    wT = P.ctr["T"]
                    P.wait("vector", "T", wT)
                    P.op("vector", lambda e, sb2=sb2, pt=pt: e.tensor_tensor(
                        out=x2h[:, sb2, :], in0=x2h[:, sb2, :], in1=pt[:, 0:H],
                        op=ALU.add), "V")
                    padd[sb2 % 2] = P.ctr["V"]
                wX3 = P.ctr["V"]
                P.wait("gpsimd", "V", wX3)
                tok_x3store[0] = P.dma(
                    "gpsimd", tokrows(x3_sp, s0 + half * 512, s0 + (half + 1) * 512),
                    x2h[:], "G")

        _cut(5)
        AR.pop()
        AR.push()
        wv_sb = AR.alloc([128, 64, 2], F32, "wv_sb")
        tok_u = AR.alloc([128, 64], U32, "tok_u")
        xe_sb = AR.alloc([128, 8, 1024], BF16, "xe_sb")
        xeT = AR.alloc([128, 8, CAP], BF16, "xeT")
        w1_sb = [AR.alloc([128, 8, F], BF16, "w1a"), AR.alloc([128, 8, F], BF16, "w1b")]
        w3_sb = [AR.alloc([128, 8, F], BF16, "w3a"), AR.alloc([128, 8, F], BF16, "w3b")]
        w2_sb = [AR.alloc([128, 4, H], BF16, "w2a"), AR.alloc([128, 4, H], BF16, "w2b")]
        h1_sb = AR.alloc([128, 4, CAP], BF16, "h1_sb")
        h3_sb = AR.alloc([128, 4, CAP], BF16, "h3_sb")
        ye_sb = AR.alloc([128, 8, H], BF16, "ye_sb")
        zbf = AR.alloc([128, 1024], BF16, "zbf")

        vop(lambda e: e.memset(zbf[:], 0.0), fence=False)
        wZB = P.ctr["V"]
        P.wait("gpsimd", "V", wZB)
        P.dma("gpsimd", ye_loc[EO * CAP:YBLK, :].rearrange("(a p) b -> p a b", p=128),
              zbf[:].rearrange("p (a b) -> p a b", a=1), "G")

        P.wait("sync", "CC", wRS)
        for col in range(64):
            tok_wv = P.dma("sync", wv_sb[:, col, :],
                           slab_out[col * 128:(col + 1) * 128, :], "L")
        P.wall("vector", "L")
        vop(lambda e: e.tensor_copy(out=tok_u[:], in_=wv_sb[:, :, 0]))
        wTOKU = P.ctr["V"]
        P.wait("gpsimd", "V", wTOKU)
        P.wait("gpsimd", "CC", wAGX)

        wei_toks = {}
        trv = {0: padd.get(0), 1: padd.get(1)}
        wXPOSE = {}
        wWUSE = {}
        tok_ye = {}
        for ei in range(EO):
            j = ei % 2
            if ei >= 2:
                P.wait("gpsimd", "T", wWUSE[j])
            for hk in range(8):
                P.dma("gpsimd", w1_sb[j][:, hk, :],
                      w1T[ei, hk * 128:(hk + 1) * 128, :], "G")
                P.dma("gpsimd", w3_sb[j][:, hk, :],
                      w3T[ei, hk * 128:(hk + 1) * 128, :], "G")
            for fk in range(4):
                wei_toks[ei] = P.dma("gpsimd", w2_sb[j][:, fk, :],
                                     w2T[ei, fk * 128:(fk + 1) * 128, :], "G")
            if ei >= 1:
                P.wait("gpsimd", "T", wXPOSE[ei - 1])
            gat = []
            for jb in range(8):
                gat.append(P.ind(lambda e, ei=ei, jb=jb: e.indirect_dma_start(
                    out=xe_sb[:, jb, :], out_offset=None, in_=xg_out[:, :],
                    in_offset=IndirectOffsetOnAxis(tok_u[:, ei * 8 + jb:ei * 8 + jb + 1], 0))))
            for t in gat:
                P.wtok("tensor", t)
            for hb in range(8):
                g = ei * 8 + hb
                pt = ptr[hb % 2]
                prev = trv.get(g - 2) if g >= 2 else trv.get(g)
                if prev is not None:
                    P.wait("tensor", "V", prev)
                for jb in range(8):
                    P.op("tensor", lambda e, hb=hb, jb=jb, pt=pt: e.transpose(
                        pt[:, jb * 128:(jb + 1) * 128],
                        xe_sb[:, jb, hb * 128:(hb + 1) * 128], ident_bf[:]), "T")
                wT = P.ctr["T"]
                P.wait("vector", "T", wT)
                P.op("vector", lambda e, hb=hb, pt=pt: e.tensor_copy(
                    out=xeT[:, hb, :], in_=pt[:, 0:CAP]), "V")
                trv[ei * 8 + hb] = P.ctr["V"]
            wXPOSE[ei] = P.ctr["T"]
            wXE = P.ctr["V"]
            P.wait("tensor", "V", wXE)
            P.wtok("tensor", wei_toks[ei])
            for dst, wsb, act in ((h1_sb, w1_sb[j], True), (h3_sb, w3_sb[j], False)):
                for fb in range(4):
                    pbk = pb[fb % 2]
                    for half in range(2):
                        for hk in range(8):
                            P.op("tensor", lambda e, wsb=wsb, fb=fb, hk=hk, half=half,
                                 pbk=pbk: e.matmul(
                                     pbk[:, 0:512], wsb[:, hk, fb * 128:(fb + 1) * 128],
                                     xeT[:, hk, half * 512:(half + 1) * 512],
                                     start=(hk == 0), stop=(hk == 7),
                                     skip_group_check=True), "T")
                        wMM = P.ctr["T"]
                        if act:
                            P.wait("scalar", "T", wMM)
                            P.op("scalar", lambda e, dst=dst, fb=fb, half=half, pbk=pbk:
                                 e.activation(out=dst[:, fb, half * 512:(half + 1) * 512],
                                              in_=pbk[:, 0:512], func=AF.Silu), "Sc")
                            P.wait("tensor", "Sc", P.ctr["Sc"])
                        else:
                            P.wait("vector", "T", wMM)
                            P.op("vector", lambda e, dst=dst, fb=fb, half=half, pbk=pbk:
                                 e.tensor_copy(out=dst[:, fb, half * 512:(half + 1) * 512],
                                               in_=pbk[:, 0:512]), "V")
                            P.wait("tensor", "V", P.ctr["V"])
            P.wait("vector", "Sc")
            P.op("vector", lambda e: e.tensor_tensor(out=h1_sb[:], in0=h1_sb[:],
                                                     in1=h3_sb[:], op=ALU.mult),
                 "V", fence=True)
            wHH = P.ctr["V"]
            P.wait("tensor", "V", wHH)
            if ei >= 1:
                P.wtok("vector", tok_ye[ei - 1])
            for sb2 in range(8):
                pbk = pb[sb2 % 2]
                for half in range(2):
                    for fk in range(4):
                        P.op("tensor", lambda e, sb2=sb2, fk=fk, half=half, pbk=pbk, j=j:
                             e.matmul(pbk[:, 0:512],
                                      h1_sb[:, fk, sb2 * 128:(sb2 + 1) * 128],
                                      w2_sb[j][:, fk, half * 512:(half + 1) * 512],
                                      start=(fk == 0), stop=(fk == 3),
                                      skip_group_check=True), "T")
                    wMM = P.ctr["T"]
                    P.wait("vector", "T", wMM)
                    P.op("vector", lambda e, sb2=sb2, half=half, pbk=pbk, ei=ei:
                         e.tensor_scalar(
                             out=ye_sb[:, sb2, half * 512:(half + 1) * 512],
                             in0=pbk[:, 0:512],
                             scalar1=wv_sb[:, ei * 8 + sb2, 1:2], scalar2=None,
                             op0=ALU.mult), "V")
                    P.wait("tensor", "V", P.ctr["V"])
            wYE = P.ctr["V"]
            wWUSE[j] = P.ctr["T"]
            P.wait("gpsimd", "V", wYE)
            tok_ye[ei] = P.dma(
                "gpsimd", ye_loc[ei * CAP:(ei + 1) * CAP, :].rearrange(
                    "(a p) b -> p a b", p=128), ye_sb[:], "G")

        _cut(6)
        P.wall("gpsimd", "G")
        P.wall("sync", "G")
        P.wait("gpsimd", "CC")
        P.coll(lambda e: e.collective_compute(
            "AllGather", ALU.bypass, replica_groups=[list(range(NCORE))],
            ins=[ye_loc.opt()], outs=[ye_out.opt()]))
        wAGY = P.ctr["CC"]
        P.wait("gpsimd", "CC", wAGY)
        AR.pop()
        AR.push()
        x3g = AR.alloc([128, 8, H], F32, "x3g")
        x3g2 = AR.alloc([128, 8, H], F32, "x3g2")
        tok_out = {}
        for grp in range(4):
            buf = x3g if grp % 2 == 0 else x3g2
            s0 = grp * 1024
            if grp >= 2:
                P.wtok("sync", tok_out[grp - 2])
            tok_x3 = P.dma("sync", buf[:], tokrows(x3_sp, s0, s0 + 1024), "L")
            P.wtok("gpsimd", tok_x3)
            add_toks = {}
            for k in range(8):
                for sb2 in range(8):
                    gsb = grp * 8 + sb2
                    prev = add_toks.get((k - 1, sb2))
                    if prev is not None:
                        P.wtok("gpsimd", prev)
                    add_toks[(k, sb2)] = P.ind(
                        lambda e, gsb=gsb, k=k, sb2=sb2, buf=buf: e.indirect_dma_start(
                            out=buf[:, sb2, :], out_offset=None, in_=ye_out[:, :],
                            in_offset=IndirectOffsetOnAxis(ycu_u[:, gsb, k:k + 1], 0),
                            compute_op=ALU.add))
            for kk in range(8):
                P.wtok("sync", add_toks[(7, kk)])
            tok_out[grp] = P.dma("sync", tokrows(out_ap, s0, s0 + 1024), buf[:], "L")

    try:
        _phase2()
    except _Cut:
        pass

    if debug:
        P.wait("gpsimd", "V")
        P.wait("gpsimd", "Sc")
        if STAGE2 >= 6:
            for nm2, ap2 in (("scu_db", dbg_refs["scu"][:]),
                             ("ycu_db", dbg_refs["ycu"][:]),
                             ("wk8_db", dbg_refs["wk8"][:]),
                             ("tokid_db", dbg_refs["tokid"][:])):
                t2 = nc.dram_tensor(nm2, [128] + [int(x) for x in ap2.shape[1:]],
                                    ap2.dtype, kind="ExternalOutput").ap()
                P.dma("gpsimd", t2[tuple([slice(None)] * len(ap2.shape))], ap2, "G")
            slabo_db = nc.dram_tensor("slabo_db", [EO * CAP, 2], F32,
                                      kind="ExternalOutput").ap()
            P.dma("gpsimd", slabo_db[:, :], slab_out[:, :], "G")
            yel_db = nc.dram_tensor("yel_db", [YBLK, H], BF16,
                                    kind="ExternalOutput").ap()
            P.dma("gpsimd", yel_db[:, :].rearrange("(a p) b -> p a b", p=128),
                  ye_loc[:, :].rearrange("(a p) b -> p a b", p=128), "G")
            xg_db = nc.dram_tensor("xg_db", [256, H], BF16,
                                   kind="ExternalOutput").ap()
            P.dma("gpsimd", xg_db[:, :], xg_out[0:256, :], "G")
        P.dma("gpsimd", logits_db[:, :].rearrange("(a p) b -> p a b", p=128),
              logits[:], "G")
        P.dma("gpsimd", xln2_db[:, :].rearrange("(a p) b -> p a b", p=128),
              xln2_sp[:, :].rearrange("(a p) b -> p a b", p=128), "G")
        for nm, ap in (("hbuf_db", hbuf[:]), ("dt_db", dt_all[:]), ("sg_db", sg_all[:]),
                       ("xlnT_db", xlnT[:]), ("xln_db", xln_bf[:]),
                       ("lnst_db", lnst[:])):
            t = nc.dram_tensor(nm, [128] + [int(x) for x in ap.shape[1:]], ap.dtype,
                               kind="ExternalOutput").ap()
            idx = tuple([slice(None)] * len(ap.shape))
            P.dma("gpsimd", t[idx], ap, "G")
        t = nc.dram_tensor("ssmp_db", [96, SC], F32, kind="ExternalOutput").ap()
        P.dma("gpsimd", t[:, :], ssm_p[:, :], "G")

    with nc.Block() as block:
        @block.sync
        def _(eng):
            for fn in P.items["sync"]:
                fn(eng)

        @block.scalar
        def _(eng):
            for fn in P.items["scalar"]:
                fn(eng)

        @block.vector
        def _(eng):
            for fn in P.items["vector"]:
                fn(eng)

        @block.tensor
        def _(eng):
            for fn in P.items["tensor"]:
                fn(eng)

        @block.gpsimd
        def _(eng):
            for fn in P.items["gpsimd"]:
                fn(eng)
            for pool in ("L", "G"):
                for n in P.pools[pool]["names"]:
                    if P.ctr[n] > 0:
                        eng.wait_ge(P.sems[n], P.ctr[n])

    build_program.last_ctr = dict(P.ctr)
    return nc


def make_in_maps(inputs):
    in_maps = []
    for c in range(NCORE):
        cm = np.zeros((8, 1), np.float32)
        cm[:c] = 1.0
        m = {
            "x_loc": np.ascontiguousarray(np.asarray(inputs["L0_attn_x"])[c]),
            "W_in": np.asarray(inputs["L0_attn_W_in"]),
            "conv_wT": np.ascontiguousarray(np.asarray(inputs["L0_attn_conv_w"]).reshape(4, C).T),
            "conv_b": np.ascontiguousarray(np.asarray(inputs["L0_attn_conv_b"]).reshape(C, 1)),
            "W_x": np.asarray(inputs["L0_attn_W_x"]),
            "W_dt": np.asarray(inputs["L0_attn_W_dt"]),
            "b_dt": np.ascontiguousarray(np.asarray(inputs["L0_attn_b_dt"]).reshape(1, C)),
            "A_log": np.asarray(inputs["L0_attn_A_log"]),
            "D_vec": np.ascontiguousarray(np.asarray(inputs["L0_attn_D"]).reshape(C, 1)),
            "W_out": np.asarray(inputs["L0_attn_W_out"]),
            "gwT": np.ascontiguousarray(np.asarray(inputs["L0_ffn_gate_weight"]).T),
            "e_bias": np.ascontiguousarray(np.asarray(inputs["L0_ffn_e_bias"]).reshape(1, E)),
            "w1T": np.ascontiguousarray(np.asarray(inputs["L0_ffn_w1"])[c * EO:(c + 1) * EO].transpose(0, 2, 1)),
            "w2T": np.ascontiguousarray(np.asarray(inputs["L0_ffn_w2"])[c * EO:(c + 1) * EO].transpose(0, 2, 1)),
            "w3T": np.ascontiguousarray(np.asarray(inputs["L0_ffn_w3"])[c * EO:(c + 1) * EO].transpose(0, 2, 1)),
            "sgT": np.ascontiguousarray(np.asarray(inputs["L0_ffn_shared_gate"]).T),
            "suT": np.ascontiguousarray(np.asarray(inputs["L0_ffn_shared_up"]).T),
            "sdT": np.ascontiguousarray(np.asarray(inputs["L0_ffn_shared_down"]).T),
            "coremask": cm,
            "tokbase": np.array([[float(S * c)]], np.float32),
        }
        in_maps.append(m)
    return in_maps


def kernel(**inputs):
    nc = build_program()
    in_maps = make_in_maps(inputs)
    res = run_bass_kernel_spmd(nc, in_maps, core_ids=list(range(NCORE)))
    return np.stack([r["out_loc"] for r in res.results], axis=0)


# version 02
# speedup vs baseline: 1.0258x; 1.0258x over previous
import numpy as np
import concourse.bass as bass
import concourse.mybir as mybir
from concourse.bass import IndirectOffsetOnAxis
from concourse.bass_utils import run_bass_kernel_spmd

F32 = mybir.dt.float32
BF16 = mybir.dt.bfloat16
U8 = mybir.dt.uint8
U32 = mybir.dt.uint32
AF = mybir.ActivationFunctionType
ALU = mybir.AluOpType
AX = mybir.AxisListType

S = 4096
H = 1024
C = 2048
D = 16
CT = C // 128
SC = 512
NCH = S // SC
SB = S // 128
E = 64
EO = 8
CAP = 1024
F = 512
F2 = 1024
NCORE = 8
TG = S * NCORE
YBLK = EO * CAP + 128
YROWS = NCORE * YBLK
ZROWID = float(EO * CAP)
BIGIDX = float(1 << 20)
BIGF = 1.0e30

DEBUG = False
STAGE = 5
STAGE2 = 6
DUMMYV = 0
DUMMYR = 0
K7SKIP = 0
SKIPRS = False


class _Cut(Exception):
    pass


def _cut(n):
    if STAGE2 < n:
        raise _Cut()


class Prog:
    ENGINES = ("sync", "scalar", "vector", "tensor", "gpsimd")
    NPOOL = 8

    def __init__(self, nc):
        self.nc = nc
        self.items = {e: [] for e in self.ENGINES}
        self.sems = {}
        self.ctr = {}
        self.pools = {}
        for pool, eng in (("L", "sync"), ("G", "gpsimd")):
            names = []
            for k in range(self.NPOOL):
                n = f"q{pool}{k}"
                self.sem(n)
                names.append(n)
            self.pools[pool] = {"names": names, "eng": eng, "rr": 0}

    def sem(self, name):
        h = self.nc.semaphore(name).__enter__()
        self.sems[name] = h
        self.ctr[name] = 0

    def wait(self, eng, sem, val=None):
        v = self.ctr[sem] if val is None else val
        h = self.sems[sem]
        if v > 0:
            self.items[eng].append(lambda e, h=h, v=v: e.wait_ge(h, v))
        return v

    def dma(self, eng, out, in_, pool):
        p = self.pools[pool]
        assert eng == p["eng"]
        k = p["rr"]
        p["rr"] = (k + 1) % self.NPOOL
        name = p["names"][k]
        h = self.sems[name]
        drain = self.ctr[name]
        if drain > 0:
            self.items[eng].append(lambda e, h=h, v=drain: e.wait_ge(h, v))
        self.items[eng].append(
            lambda e, o=out, i=in_, h=h: e.dma_start(out=o, in_=i).then_inc(h, 16))
        self.ctr[name] += 16
        return (name, self.ctr[name])

    def wtok(self, eng, tok):
        if tok is None:
            return
        name, v = tok
        h = self.sems[name]
        self.items[eng].append(lambda e, h=h, v=v: e.wait_ge(h, v))

    def wall(self, eng, pool):
        for name in self.pools[pool]["names"]:
            self.wait(eng, name)

    def op(self, eng, fn, sem, fence=False):
        h = self.sems[sem]
        if fence and self.ctr[sem] > 0:
            v = self.ctr[sem]
            self.items[eng].append(lambda e, h=h, v=v: e.wait_ge(h, v))
        self.items[eng].append(lambda e, fn=fn, h=h: fn(e).then_inc(h, 1))
        self.ctr[sem] += 1
        return self.ctr[sem]

    def ind(self, fn):
        p = self.pools["G"]
        k = p["rr"]
        p["rr"] = (k + 1) % self.NPOOL
        name = p["names"][k]
        h = self.sems[name]
        drain = self.ctr[name]
        if drain > 0:
            self.items["gpsimd"].append(lambda e, h=h, v=drain: e.wait_ge(h, v))
        self.items["gpsimd"].append(lambda e, fn=fn, h=h: fn(e).then_inc(h, 16))
        self.ctr[name] += 16
        return (name, self.ctr[name])

    def coll(self, fn):
        h = self.sems["CC"]
        self.items["gpsimd"].append(lambda e, fn=fn, h=h: fn(e).then_inc(h, 1))
        self.ctr["CC"] += 1
        return self.ctr["CC"]

    def raw(self, eng, fn):
        self.items[eng].append(fn)


class Arena:
    def __init__(self, nc, nbytes):
        self.t = nc.alloc_sbuf_tensor("arena", [128, nbytes], U8)
        self.nbytes = nbytes
        self.off = 0
        self.marks = []

    def alloc(self, shape, dtype, name="", at=None):
        esz = mybir.dt.size(dtype)
        n = int(np.prod(shape[1:])) * esz
        if at is None:
            off = (self.off + 63) & ~63
            assert off + n <= self.nbytes, (
                f"SBUF arena overflow: {name}: {n}B at {off}, cap {self.nbytes}")
            self.off = off + n
        else:
            off = at
            assert off + n <= self.nbytes
        ap = self.t.ap()[:, off:off + n].bitcast(dtype)
        if len(shape) > 2:
            dims = " ".join(f"d{i}" for i in range(1, len(shape)))
            kw = {f"d{i}": shape[i] for i in range(1, len(shape))}
            ap = ap.rearrange(f"p ({dims}) -> p {dims}", **kw)
        assert shape[0] == 128
        return ap

    def mark(self):
        return (self.off + 63) & ~63

    def push(self):
        self.marks.append(self.off)

    def pop(self):
        self.off = self.marks.pop()


def tokrows(dram2d, r0, r1):
    return dram2d[r0:r1, :].rearrange("(a p) b -> p a b", p=128)


def build_program(debug=DEBUG):
    nc = bass.Bass("TRN2", target_bir_lowering=False, detect_race_conditions=False)

    def din(name, shape, dt=F32):
        return nc.dram_tensor(name, list(shape), dt, kind="ExternalInput").ap()

    x_in = din("x_loc", [S, H])
    w_in = din("W_in", [H, 2 * C])
    conv_wT = din("conv_wT", [C, 4])
    conv_b = din("conv_b", [C, 1])
    w_x = din("W_x", [C, 96])
    w_dt = din("W_dt", [64, C])
    b_dt = din("b_dt", [1, C])
    a_log = din("A_log", [C, D])
    d_vec = din("D_vec", [C, 1])
    w_out = din("W_out", [C, H])
    gwT = din("gwT", [H, E])
    e_bias = din("e_bias", [1, E])
    w1T = din("w1T", [EO, H, F])
    w2T = din("w2T", [EO, F, H])
    w3T = din("w3T", [EO, H, F])
    sgT = din("sgT", [H, F2])
    suT = din("suT", [H, F2])
    sdT = din("sdT", [F2, H])
    coremask = din("coremask", [8, 1])
    tokbase = din("tokbase", [1, 1])

    out_ap = nc.dram_tensor("out_loc", [S, H], F32, kind="ExternalOutput").ap()

    def dint(name, shape, dt=F32, dbg=False, shared=False):
        kind = "ExternalOutput" if (debug and dbg) else "Internal"
        kw = {"addr_space": "Shared"} if shared else {}
        return nc.dram_tensor(name, list(shape), dt, kind=kind, **kw).ap()

    win_bfh = dint("win_bfh", [H, 2 * C], BF16)
    wout_bfh = dint("wout_bfh", [C, H], BF16)
    x2_sp = dint("x2_sp", [S, H], F32, dbg=True)
    xln2_sp = dint("xln2_sp", [S, H], BF16)
    xln2_db = dint("xln2_db", [S, H], BF16, dbg=True) if debug else None
    xln2T_sp = dint("xln2T_sp", [H, S], BF16)
    logits_db = dint("logits_db", [S, E], F32, dbg=True)
    x3_sp = dint("x3_sp", [S, H], F32, dbg=True)
    cnt_in = dint("cnt_in", [1, E], F32)
    cnt_out = dint("cnt_out", [8, E], F32)
    slab_in = dint("slab_in", [E * CAP, 2], F32)
    slab_out = dint("slab_out", [EO * CAP, 2], F32)
    xg_out = dint("xg_out", [TG, H], BF16, shared=True)
    ye_loc = dint("ye_loc", [YBLK, H], BF16)
    ye_out = dint("ye_out", [YROWS, H], BF16, shared=True)
    ind_db = dint("ind_db", [S, E], F32, dbg=True)
    w8_db = dint("w8_db", [S, E], F32, dbg=True)
    pos_db = dint("pos_db", [S, E], F32, dbg=True)

    pb = [nc.psum_tensor(f"pb{i}", [128, 512], F32).__enter__().ap() for i in range(8)]
    ptr = [pb[6].bitcast(BF16), pb[7].bitcast(BF16)]

    ssm_p = nc.alloc_sbuf_tensor("ssm_p", [96, SC], F32).ap()
    bc_B = nc.alloc_sbuf_tensor("bc_B", [16, SC], BF16).ap()
    bc_C = nc.alloc_sbuf_tensor("bc_C", [16, SC], BF16).ap()
    ts_aug = nc.alloc_sbuf_tensor("ts_aug", [65, SC], BF16).ap()
    wdt_bf = nc.alloc_sbuf_tensor("wdt_bf", [65, C], BF16).ap()
    oh16 = nc.alloc_sbuf_tensor("oh16", [16, D * 128], BF16).ap()
    lg_ev = nc.alloc_sbuf_tensor("lg_ev", [64, SC], F32).ap()

    AR = Arena(nc, 156 * 1024)
    sigma = AR.alloc([128, CT * D], F32, "sigma")
    logits = AR.alloc([128, SB, E], F32, "logits")
    a_sb = AR.alloc([128, CT * D], F32, "a_sb")
    cwT_sb = AR.alloc([128, CT * 4], F32, "cwT")
    cb_sb = AR.alloc([128, CT], F32, "cb")
    dv_sb = AR.alloc([128, CT], F32, "dv")
    ident_bf = AR.alloc([128, 128], BF16, "identb")
    ident_f = AR.alloc([128, 128], F32, "identf")
    lnst = AR.alloc([128, 8], F32, "lnst")

    P = Prog(nc)
    for s in ("V", "Sc", "T", "Pg", "CC"):
        P.sem(s)

    for ct in range(CT):
        P.dma("sync", a_sb[:, ct * D:(ct + 1) * D], a_log[ct * 128:(ct + 1) * 128, :], "L")
        P.dma("sync", cwT_sb[:, ct * 4:(ct + 1) * 4], conv_wT[ct * 128:(ct + 1) * 128, :], "L")
        P.dma("sync", cb_sb[:, ct:ct + 1], conv_b[ct * 128:(ct + 1) * 128, :], "L")
        P.dma("sync", dv_sb[:, ct:ct + 1], d_vec[ct * 128:(ct + 1) * 128, :], "L")
    P.dma("gpsimd", wdt_bf[0:64, :], w_dt[:, :], "G")
    P.dma("gpsimd", wdt_bf[64:65, :], b_dt[:, :], "G")
    P.wall("scalar", "L")
    P.op("scalar", lambda e: e.activation(out=a_sb[:], in_=a_sb[:], func=AF.Exp), "Sc")
    P.wait("vector", "Sc")
    P.op("vector", lambda e: e.tensor_scalar_mul(a_sb[:], a_sb[:], -1.0), "V")
    P.op("gpsimd", lambda e: e.iota(ident_f[:], pattern=[[-1, 128]], base=0,
                                    channel_multiplier=1,
                                    allow_small_or_imprecise_dtypes=True), "Pg")
    P.op("gpsimd", lambda e: e.iota(oh16[:], pattern=[[-1, D], [0, 128]], base=0,
                                    channel_multiplier=1,
                                    allow_small_or_imprecise_dtypes=True), "Pg")
    P.wait("vector", "Pg")
    P.op("vector", lambda e: e.tensor_scalar(out=ident_bf[:], in0=ident_f[:],
                                             scalar1=0.0, scalar2=None, op0=ALU.is_equal), "V")
    P.op("vector", lambda e: e.tensor_scalar(out=oh16[:], in0=oh16[:],
                                             scalar1=0.0, scalar2=None, op0=ALU.is_equal), "V")
    P.op("vector", lambda e: e.tensor_scalar(out=ident_f[:], in0=ident_f[:],
                                             scalar1=0.0, scalar2=None, op0=ALU.is_equal), "V")

    AR.push()
    wx_sb = AR.alloc([128, CT, 96], BF16, "wx_sb")
    wstr = [AR.alloc([128, 8, 128], BF16, "wstr0"), AR.alloc([128, 8, 128], BF16, "wstr1")]
    xreg_off = AR.mark()
    x_t = AR.alloc([128, 2, H], F32, "x_t")
    bounce = AR.alloc([128, 4096], BF16, "bounce_alias", at=xreg_off)
    xlreg_off = AR.mark()
    xln_bf = AR.alloc([128, 4, H], BF16, "xln")
    xln2_t = AR.alloc([128, 4, H], BF16, "xln2t", at=xlreg_off)
    streg_off = AR.mark()
    xlnT = AR.alloc([128, 8, SC], BF16, "xlnT")
    mo_bf = AR.alloc([128, 8, SC], BF16, "mo", at=streg_off)
    xlnTb = AR.alloc([128, 8, SC], BF16, "xlnTb", at=streg_off)
    padded = AR.alloc([128, SC + 4], F32, "padded")
    hbuf = AR.alloc([128, CT, SC], BF16, "hbuf")
    hbf = [AR.alloc([128, SC], BF16, "hbf0"), AR.alloc([128, SC], BF16, "hbf1")]
    sg_all = AR.alloc([128, CT, SC], BF16, "sg_all")
    dt_all = AR.alloc([128, CT, SC], BF16, "dt_all")
    hist = AR.alloc([128, CT, 3], F32, "hist")
    bbc_off = AR.mark()
    bbc = AR.alloc([128, D, SC], BF16, "bbc")
    cbc_off = AR.mark()
    cbc = AR.alloc([128, D, SC], BF16, "cbc")
    state = AR.alloc([128, D, SC], BF16, "state")
    da_t = AR.alloc([128, SC], BF16, "da")
    dbu_t = AR.alloc([128, SC], BF16, "dbu")
    u_t = AR.alloc([128, SC], F32, "u_t")
    spl_t = AR.alloc([128, SC], F32, "spl_t")
    yacc = AR.alloc([128, SC], F32, "yacc")
    zred = AR.alloc([128, SC], F32, "zred")
    yfin = AR.alloc([128, SC], BF16, "yfin")
    gwT_sb = AR.alloc([128, 8, E], F32, "gwTsb")
    xln2f = AR.alloc([128, H], F32, "xln2f", at=cbc_off)
    xT_f = AR.alloc([128, 8, 128], F32, "xTf", at=bbc_off)

    for hk in range(8):
        t = P.dma("gpsimd", bounce[:], w_in[hk * 128:(hk + 1) * 128, :], "G")
        P.wtok("gpsimd", t)
        t = P.dma("gpsimd", win_bfh[hk * 128:(hk + 1) * 128, :], bounce[:], "G")
        P.wtok("gpsimd", t)
    for ck in range(CT):
        t = P.dma("gpsimd", bounce[:, 0:H], w_out[ck * 128:(ck + 1) * 128, :], "G")
        P.wtok("gpsimd", t)
        t = P.dma("gpsimd", wout_bfh[ck * 128:(ck + 1) * 128, :], bounce[:, 0:H], "G")
        P.wtok("gpsimd", t)
    for ct in range(CT):
        P.dma("gpsimd", wx_sb[:, ct, :], w_x[ct * 128:(ct + 1) * 128, :], "G")
    wPrepSnap = {n: P.ctr[n] for n in P.pools["G"]["names"]}

    def wait_prep(eng):
        for n, v in wPrepSnap.items():
            P.wait(eng, n, v)

    for hk in range(8):
        P.dma("sync", gwT_sb[:, hk, :], gwT[hk * 128:(hk + 1) * 128, :], "L")
    P.op("vector", lambda e: e.memset(sigma[:], 0.0), "V")

    def ln_norm(src_ap, out_ap, out2_ap=None):
        P.op("vector", lambda e: e.reduce_sum(lnst[:, 2:3], src_ap, axis=AX.X), "V", fence=True)
        P.wait("scalar", "V")
        P.op("scalar", lambda e: e.activation(out=zred[:], in_=src_ap[:, 0:SC],
                                              func=AF.Square, accum_out=lnst[:, 0:1]), "Sc", fence=True)
        P.op("scalar", lambda e: e.activation(out=spl_t[:], in_=src_ap[:, SC:H],
                                              func=AF.Square, accum_out=lnst[:, 1:2]), "Sc", fence=True)
        P.wait("vector", "Sc")
        P.op("vector", lambda e: e.tensor_tensor(out=lnst[:, 3:4], in0=lnst[:, 0:1],
                                                 in1=lnst[:, 1:2], op=ALU.add), "V", fence=True)
        P.op("vector", lambda e: e.tensor_scalar_mul(lnst[:, 2:3], lnst[:, 2:3], 1.0 / H), "V", fence=True)
        P.op("vector", lambda e: e.tensor_scalar_mul(lnst[:, 3:4], lnst[:, 3:4], 1.0 / H), "V", fence=True)
        P.op("vector", lambda e: e.tensor_tensor(out=lnst[:, 4:5], in0=lnst[:, 2:3],
                                                 in1=lnst[:, 2:3], op=ALU.mult), "V", fence=True)
        P.op("vector", lambda e: e.tensor_tensor(out=lnst[:, 5:6], in0=lnst[:, 3:4],
                                                 in1=lnst[:, 4:5], op=ALU.subtract), "V", fence=True)
        P.op("vector", lambda e: e.tensor_scalar_add(lnst[:, 5:6], lnst[:, 5:6], 1e-5), "V", fence=True)
        P.wait("scalar", "V")
        P.op("scalar", lambda e: e.activation(out=lnst[:, 6:7], in_=lnst[:, 5:6],
                                              func=AF.Sqrt), "Sc", fence=True)
        P.wait("vector", "Sc")
        P.op("vector", lambda e: e.reciprocal(lnst[:, 7:8], lnst[:, 6:7]), "V", fence=True)
        P.op("vector", lambda e: e.tensor_scalar(
            out=out_ap, in0=src_ap, scalar1=lnst[:, 2:3], scalar2=lnst[:, 7:8],
            op0=ALU.subtract, op1=ALU.mult), "V", fence=True)
        if out2_ap is not None:
            P.op("vector", lambda e: e.tensor_copy(out=out2_ap, in_=out_ap), "V", fence=True)

    last_wstr_mm = 0
    last_x2_store = None
    for i in range(NCH):
        s0 = i * SC
        P.wall("vector", "G")
        for half in range(2):
            P.wait("sync", "V")
            P.wait("sync", "Sc")
            P.wtok("sync", last_x2_store)
            if i == 0 and half == 0:
                P.wall("sync", "G")
            wX = P.dma("sync", x_t[:],
                       tokrows(x_in, s0 + half * 256, s0 + (half + 1) * 256), "L")
            P.wtok("vector", wX)
            for sb in range(2):
                ln_norm(x_t[:, sb, :], xln_bf[:, half * 2 + sb, :])
        wLN = P.ctr["V"]
        P.wait("tensor", "V", wLN)
        xev = []
        for hb in range(8):
            pt = ptr[hb % 2]
            if hb >= 2:
                P.wait("tensor", "V", xev[hb - 2])
            for sb in range(4):
                P.op("tensor", lambda e, sb=sb, hb=hb, pt=pt: e.transpose(
                    pt[:, sb * 128:(sb + 1) * 128],
                    xln_bf[:, sb, hb * 128:(hb + 1) * 128], ident_bf[:]), "T")
            wT = P.ctr["T"]
            P.wait("vector", "T", wT)
            P.op("vector", lambda e, hb=hb, pt=pt: e.tensor_copy(
                out=xlnT[:, hb, :], in_=pt[:, 0:SC]), "V")
            xev.append(P.ctr["V"])
        wXT = P.ctr["V"]

        if STAGE < 2:
            continue
        P.wait("tensor", "V", wXT)
        P.wait("sync", "T", last_wstr_mm)
        if i == 0:
            P.wall("sync", "G")
        tok_w = {}
        tok_w[0] = P.dma("sync", wstr[0][:],
                         win_bfh[:, 0:128].rearrange("(a p) b -> p a b", p=128), "L")
        evac = {}
        winmm = {}
        for iob in range(32):
            if iob + 1 < 32:
                if iob >= 1:
                    P.wait("sync", "T", winmm[iob - 1])
                tok_w[iob + 1] = P.dma("sync", wstr[(iob + 1) % 2][:],
                      win_bfh[:, (iob + 1) * 128:(iob + 2) * 128].rearrange(
                          "(a p) b -> p a b", p=128), "L")
            pbk = pb[iob % 2]
            P.wtok("tensor", tok_w[iob])
            if iob >= 2:
                w, engsem = evac[iob - 2]
                P.wait("tensor", engsem, w)
            for hk in range(8):
                P.op("tensor", lambda e, hk=hk, iob=iob, pbk=pbk: e.matmul(
                    pbk[:, 0:SC], wstr[iob % 2][:, hk, :], xlnT[:, hk, :],
                    start=(hk == 0), stop=(hk == 7)), "T")
            wMM = winmm[iob] = P.ctr["T"]
            ct = iob % 16
            if iob < 16:
                P.wait("vector", "T", wMM)
                P.op("vector", lambda e, pbk=pbk: e.tensor_copy(
                    out=padded[:, 4:4 + SC], in_=pbk[:, 0:SC]), "V")
                if i == 0:
                    P.op("vector", lambda e: e.memset(padded[:, 1:4], 0.0), "V")
                else:
                    P.op("vector", lambda e, ct=ct: e.tensor_copy(
                        out=padded[:, 1:4], in_=hist[:, ct, :]), "V")
                P.wait("vector", "Sc")
                cbufs = [u_t, spl_t]
                P.op("vector", lambda e, ct=ct: e.tensor_scalar(
                    out=u_t[:], in0=padded[:, 1:1 + SC],
                    scalar1=cwT_sb[:, ct * 4:ct * 4 + 1], scalar2=None, op0=ALU.mult), "V")
                for k in range(1, 4):
                    P.op("vector", lambda e, ct=ct, k=k, cbufs=cbufs: e.scalar_tensor_tensor(
                        out=cbufs[k % 2][:], in0=padded[:, 1 + k:1 + k + SC],
                        scalar=cwT_sb[:, ct * 4 + k:ct * 4 + k + 1],
                        in1=cbufs[(k + 1) % 2][:], op0=ALU.mult, op1=ALU.add), "V")
                P.op("vector", lambda e, ct=ct: e.tensor_copy(
                    out=hist[:, ct, :], in_=padded[:, SC + 1:SC + 4]), "V")
                wConv = P.ctr["V"]
                evac[iob] = (wConv, "V")
                P.wait("scalar", "V", wConv)
                P.op("scalar", lambda e, ct=ct: e.activation(
                    out=hbuf[:, ct, :], in_=spl_t[:], func=AF.Silu,
                    bias=cb_sb[:, ct:ct + 1]), "Sc")
            else:
                P.wait("scalar", "T", wMM)
                P.op("scalar", lambda e, ct=ct, pbk=pbk: e.activation(
                    out=sg_all[:, ct, :], in_=pbk[:, 0:SC], func=AF.Silu), "Sc")
                evac[iob] = (P.ctr["Sc"], "Sc")
        last_win_mm = winmm[31]
        wSiluAll = P.ctr["Sc"]

        if STAGE < 3:
            continue
        P.wait("tensor", "Sc", wSiluAll)
        if i == 0:
            wait_prep("tensor")
        wx_mm = {}
        for ct in range(CT):
            P.op("tensor", lambda e, ct=ct: e.matmul(
                pb[2][0:96, 0:SC], wx_sb[:, ct, :], hbuf[:, ct, :],
                start=(ct == 0), stop=(ct == CT - 1)), "T")
            wx_mm[ct] = P.ctr["T"]
        wWX = P.ctr["T"]
        P.wait("vector", "T", wWX)
        P.op("vector", lambda e: e.tensor_copy(out=ssm_p[:, :], in_=pb[2][0:96, 0:SC]), "V")
        wSP = P.ctr["V"]
        P.wait("gpsimd", "V", wSP)
        tok_bcB = P.dma("gpsimd", bc_B[:, :], ssm_p[64:80, :], "G")
        tok_bcC = P.dma("gpsimd", bc_C[:, :], ssm_p[80:96, :], "G")
        P.op("vector", lambda e: e.tensor_copy(out=ts_aug[0:64, :], in_=ssm_p[0:64, :]), "V")
        P.op("vector", lambda e: e.memset(ts_aug[64:65, :], 1.0), "V")
        wTS = P.ctr["V"]

        P.wait("tensor", "V", wTS)
        dtev = {}
        for cb in range(CT):
            pbk = pb[cb % 2]
            if cb >= 2:
                P.wait("tensor", "Sc", dtev[cb - 2])
            P.op("tensor", lambda e, cb=cb, pbk=pbk: e.matmul(
                pbk[:, 0:SC], wdt_bf[:, cb * 128:(cb + 1) * 128], ts_aug[:],
                start=True, stop=True), "T")
            wDT = P.ctr["T"]
            P.wait("scalar", "T", wDT)
            P.op("scalar", lambda e, pbk=pbk: e.activation(
                out=spl_t[:], in_=pbk[:, 0:SC], func=AF.Exp), "Sc")
            P.op("scalar", lambda e: e.add(out=spl_t[:], in_=spl_t[:], add=1.0), "Sc")
            P.op("scalar", lambda e, cb=cb: e.activation(
                out=dt_all[:, cb, :], in_=spl_t[:], func=AF.Ln), "Sc")
            dtev[cb] = P.ctr["Sc"]
        wDTall = P.ctr["Sc"]

        P.wtok("tensor", tok_bcB)
        P.wtok("tensor", tok_bcC)
        bcev = {}
        for d2 in range(2 * D):
            pbk = pb[3 + (d2 % 2)]
            if d2 >= 2:
                P.wait("tensor", "V", bcev[d2 - 2])
            dd = d2 % D
            src = bc_B if d2 < D else bc_C
            P.op("tensor", lambda e, dd=dd, src=src, pbk=pbk: e.matmul(
                pbk[:, 0:SC], oh16[:, dd * 128:(dd + 1) * 128], src[:, :],
                start=True, stop=True), "T")
            wB = P.ctr["T"]
            P.wait("vector", "T", wB)
            dst = bbc if d2 < D else cbc
            P.op("vector", lambda e, dst=dst, dd=dd, pbk=pbk: e.tensor_copy(
                out=dst[:, dd, :], in_=pbk[:, 0:SC]), "V")
            bcev[d2] = P.ctr["V"]
        wBC = P.ctr["V"]

        if STAGE < 4:
            continue
        P.wait("vector", "Sc", wDTall)
        P.wait("sync", "T", last_win_mm)
        tok_w = {0: P.dma("sync", wstr[0][:],
                          wout_bfh[0:128, :].rearrange("p (a b) -> p a b", a=8), "L")}
        woct = {}
        for ct in range(CT):
            P.op("vector", lambda e, ct=ct: e.tensor_tensor(
                out=u_t[:], in0=dt_all[:, ct, :], in1=hbuf[:, ct, :], op=ALU.mult), "V")
            P.op("vector", lambda e, ct=ct: e.tensor_scalar(
                out=yacc[:], in0=hbuf[:, ct, :], scalar1=dv_sb[:, ct:ct + 1],
                scalar2=None, op0=ALU.mult), "V")
            for d in range(D):
                P.wait("scalar", "V")
                P.op("scalar", lambda e, ct=ct, d=d: e.activation(
                    out=da_t[:], in_=dt_all[:, ct, :], func=AF.Exp,
                    scale=a_sb[:, ct * D + d:ct * D + d + 1]), "Sc")
                wdA = P.ctr["Sc"]
                P.op("vector", lambda e, d=d: e.tensor_tensor(
                    out=dbu_t[:], in0=u_t[:], in1=bbc[:, d, :], op=ALU.mult), "V")
                P.wait("vector", "Sc", wdA)
                P.op("vector", lambda e, ct=ct, d=d: e.tensor_tensor_scan(
                    out=state[:, d, :], data0=da_t[:], data1=dbu_t[:],
                    initial=sigma[:, ct * D + d:ct * D + d + 1],
                    op0=ALU.mult, op1=ALU.add), "V")
            P.op("vector", lambda e, ct=ct: e.tensor_copy(
                out=sigma[:, ct * D:(ct + 1) * D], in_=state[:, :, SC - 1]), "V",
                 fence=True)
            for d in range(D):
                P.op("vector", lambda e, d=d: e.tensor_tensor(
                    out=state[:, d, :], in0=state[:, d, :], in1=cbc[:, d, :],
                    op=ALU.mult), "V")
            P.op("vector", lambda e: e.tensor_reduce(
                out=zred[:], in_=state[:].rearrange("p a b -> p b a"),
                axis=AX.X, op=ALU.add), "V")
            P.op("vector", lambda e: e.tensor_add(yacc[:], yacc[:], zred[:]), "V")
            if ct >= 1:
                P.wait("vector", "T", woct[ct - 1])
            P.op("vector", lambda e, ct=ct: e.tensor_tensor(
                out=yfin[:], in0=yacc[:], in1=sg_all[:, ct, :], op=ALU.mult), "V")
            wYF = P.ctr["V"]
            P.wait("tensor", "V", wYF)
            P.wtok("tensor", tok_w[ct])
            for ob in range(8):
                P.op("tensor", lambda e, ct=ct, ob=ob: e.matmul(
                    pb[ob][:, 0:SC], wstr[ct % 2][:, ob, :], yfin[:],
                    start=(ct == 0), stop=(ct == CT - 1), skip_group_check=True), "T")
            woct[ct] = P.ctr["T"]
            if ct + 1 < CT:
                if ct >= 1:
                    P.wait("sync", "T", woct[ct - 1])
                tok_w[ct + 1] = P.dma("sync", wstr[(ct + 1) % 2][:],
                      wout_bfh[(ct + 1) * 128:(ct + 2) * 128, :].rearrange(
                          "p (a b) -> p a b", a=8), "L")
        last_wstr_mm = woct[CT - 1]
        wWOmm = P.ctr["T"]

        if STAGE < 5:
            continue
        P.wait("vector", "T", wWOmm)
        for ob in range(8):
            P.op("vector", lambda e, ob=ob: e.tensor_copy(
                out=mo_bf[:, ob, :], in_=pb[ob][:, 0:SC]), "V")
        wMO = P.ctr["V"]
        for half in range(2):
            P.wait("sync", "V")
            P.wait("sync", "Sc")
            P.wtok("sync", last_x2_store)
            wXR = P.dma("sync", x_t[:],
                        tokrows(x_in, s0 + half * 256, s0 + (half + 1) * 256), "L")
            P.wait("tensor", "V", wMO)
            P.wtok("vector", wXR)
            for sb2 in range(2):
                sb = half * 2 + sb2
                pt = ptr[sb2 % 2]
                if half + sb2 > 0:
                    P.wait("tensor", "V")
                for ob in range(8):
                    P.op("tensor", lambda e, sb=sb, ob=ob, pt=pt: e.transpose(
                        pt[:, ob * 128:(ob + 1) * 128],
                        mo_bf[:, ob, sb * 128:(sb + 1) * 128], ident_bf[:]), "T")
                wT = P.ctr["T"]
                P.wait("vector", "T", wT)
                P.op("vector", lambda e, sb2=sb2, pt=pt: e.tensor_tensor(
                    out=x_t[:, sb2, :], in0=x_t[:, sb2, :], in1=pt[:, 0:H], op=ALU.add), "V")
            wX2h = P.ctr["V"]
            P.wait("gpsimd", "V", wX2h)
            last_x2_store = P.dma(
                "gpsimd", tokrows(x2_sp, s0 + half * 256, s0 + (half + 1) * 256),
                x_t[:], "G")
            for sb2 in range(2):
                sb = half * 2 + sb2
                ln_norm(x_t[:, sb2, :], xln2f[:], out2_ap=xln2_t[:, sb, :])
                wXF = P.ctr["V"]
                P.wait("tensor", "V", wXF)
                rtev = {}
                for hb in range(8):
                    pbs = pb[4 + (hb % 2)]
                    if hb >= 2:
                        P.wait("tensor", "V", rtev[hb - 2])
                    P.op("tensor", lambda e, hb=hb, pbs=pbs: e.transpose(
                        pbs[:, 0:128], xln2f[:, hb * 128:(hb + 1) * 128], ident_f[:]), "T")
                    wT = P.ctr["T"]
                    P.wait("vector", "T", wT)
                    P.op("vector", lambda e, hb=hb, pbs=pbs: e.tensor_copy(
                        out=xT_f[:, hb, :], in_=pbs[:, 0:128]), "V")
                    rtev[hb] = P.ctr["V"]
                wXTF = P.ctr["V"]
                P.wait("tensor", "V", wXTF)
                for hb in range(8):
                    P.op("tensor", lambda e, hb=hb: e.matmul(
                        pb[3][0:E, 0:128], gwT_sb[:, hb, :], xT_f[:, hb, :],
                        start=(hb == 0), stop=(hb == 7)), "T")
                wRT = P.ctr["T"]
                P.wait("vector", "T", wRT)
                P.op("vector", lambda e, sb=sb: e.tensor_copy(
                    out=lg_ev[:, sb * 128:(sb + 1) * 128], in_=pb[3][0:E, 0:128]), "V")
        wLG = P.ctr["V"]
        P.wait("tensor", "V", wLG)
        lgev2 = {}
        for sb in range(4):
            pbs = pb[4 + (sb % 2)]
            if sb >= 2:
                P.wait("tensor", "V", lgev2[sb - 2])
            P.op("tensor", lambda e, sb=sb, pbs=pbs: e.transpose(
                pbs[0:128, 0:64], lg_ev[0:E, sb * 128:(sb + 1) * 128],
                ident_f[0:64, 0:64]), "T")
            wT = P.ctr["T"]
            P.wait("vector", "T", wT)
            gsb = i * 4 + sb
            P.op("vector", lambda e, gsb=gsb, pbs=pbs: e.tensor_copy(
                out=logits[:, gsb, :], in_=pbs[:, 0:64]), "V")
            lgev2[sb] = P.ctr["V"]
        wXL = P.ctr["V"]
        P.wait("gpsimd", "V", wXL)
        P.dma("gpsimd", tokrows(xln2_sp, s0, s0 + SC), xln2_t[:], "G")
        P.wait("tensor", "V", wXL)
        xt2 = {}
        for hb in range(8):
            pt = ptr[hb % 2]
            if hb >= 2:
                P.wait("tensor", "V", xt2[hb - 2])
            for sb in range(4):
                P.op("tensor", lambda e, sb=sb, hb=hb, pt=pt: e.transpose(
                    pt[:, sb * 128:(sb + 1) * 128],
                    xln2_t[:, sb, hb * 128:(hb + 1) * 128], ident_bf[:]), "T")
            wT = P.ctr["T"]
            P.wait("vector", "T", wT)
            P.op("vector", lambda e, hb=hb, pt=pt: e.tensor_copy(
                out=xlnTb[:, hb, :], in_=pt[:, 0:SC]), "V")
            xt2[hb] = P.ctr["V"]
        wXT2 = P.ctr["V"]
        P.wait("gpsimd", "V", wXT2)
        P.dma("gpsimd", xln2T_sp[:, s0:s0 + SC].rearrange("(a p) b -> p a b", p=128),
              xlnTb[:], "G")

    dbg_refs = {}

    def _phase2():
        _cut(1)
        AR.pop()

        scu = AR.alloc([128, SB, 8], F32, "scu")
        ycu = AR.alloc([128, SB, 8], F32, "ycu")
        wk8 = AR.alloc([128, SB, 8], F32, "wk8")
        scu_u = AR.alloc([128, SB, 8], U32, "scu_u")
        ycu_u = AR.alloc([128, SB, 8], U32, "ycu_u")
        tokid = AR.alloc([128, SB], F32, "tokid")
        dbg_refs.update(scu=scu, ycu=ycu, wk8=wk8, tokid=tokid)
        tb1 = AR.alloc([128, 1], F32, "tb1")
        ebias_bc = AR.alloc([128, E], F32, "ebias_bc")
        e_bias_sb = nc.alloc_sbuf_tensor("e_bias_sb", [1, E], F32).ap()
        cm_sb = nc.alloc_sbuf_tensor("cm_sb", [8, 1], F32).ap()
        tbase_sb = nc.alloc_sbuf_tensor("tbase_sb", [1, 1], F32).ap()
        cnts_sb = nc.alloc_sbuf_tensor("cnts_sb", [8, E], F32).ap()
        ones1f = nc.alloc_sbuf_tensor("ones1f", [1, 128], F32).ap()
        cnt1 = nc.alloc_sbuf_tensor("cnt1", [1, E], F32).ap()

        AR.push()
        NHT = SB // 2
        indT_f = AR.alloc([128, S], F32, "indT")
        posT_f = AR.alloc([128, S], F32, "posT")
        keepT_f = AR.alloc([128, S], F32, "keepT")
        indT = indT_f[0:64, :]
        posT = posT_f[0:64, :]
        keepT = keepT_f[0:64, :]
        base64 = AR.alloc([128, 1], F32, "base64")
        scoresh = AR.alloc([128, NHT, E], F32, "scoresh")
        sfch = AR.alloc([128, NHT, E], F32, "sfch")
        tmpA = AR.alloc([128, NHT, E], F32, "tmpA")
        tmpB = AR.alloc([128, NHT, E], F32, "tmpB")
        indh = AR.alloc([128, NHT, E], F32, "indh")
        rkA = AR.alloc([128, NHT, E], F32, "rkA")
        rkB = AR.alloc([128, NHT, E], F32, "rkB")
        eqh = AR.alloc([128, NHT, E], F32, "eqh")
        ivh = AR.alloc([128, NHT, E], F32, "ivh")
        io64 = AR.alloc([128, NHT, E], F32, "io64")
        io64b = AR.alloc([128, NHT, E], F32, "io64b")
        io8 = AR.alloc([128, NHT, E], F32, "io8")
        io8b = AR.alloc([128, NHT, E], F32, "io8b")
        ebase = AR.alloc([128, NHT, E], F32, "ebase")
        ybase = AR.alloc([128, NHT, E], F32, "ybase")
        sidh = AR.alloc([128, NHT, E], F32, "sidh")
        wn_h0 = AR.alloc([128, NHT, E], F32, "wn_h0")
        rk_h0 = AR.alloc([128, NHT, E], F32, "rk_h0")
        posh = AR.alloc([128, NHT, E], F32, "posh")
        keeph = AR.alloc([128, NHT, E], F32, "keeph")
        wnh = AR.alloc([128, NHT, E], F32, "wnh")
        gmh = AR.alloc([128, NHT, 8], F32, "gmh")
        gm2h = AR.alloc([128, NHT, 8], F32, "gm2h")
        gsA = AR.alloc([128, NHT, 8], F32, "gsA")
        gsB = AR.alloc([128, NHT, 8], F32, "gsB")
        gmask = AR.alloc([128, NHT, 8], F32, "gmask")
        iog = AR.alloc([128, NHT, 8], F32, "iog")
        iogb = AR.alloc([128, NHT, 8], F32, "iogb")
        m32 = AR.alloc([128, NHT], F32, "m32")
        mi32 = AR.alloc([128, NHT], F32, "mi32")
        payk = AR.alloc([128, SB, 2], F32, "payk")
        ztile = AR.alloc([128, 512], F32, "ztile")

        def vop(fn, fence=True):
            return P.op("vector", fn, "V", fence=fence)

        for ap_, pat, base in (
            (io64, [[0, NHT], [1, E]], 0),
            (io64b, [[0, NHT], [1, E]], int(BIGIDX)),
            (io8, [[0, NHT], [0, 8], [1, 8]], 0),
            (io8b, [[0, NHT], [0, 8], [1, 8]], int(BIGIDX)),
            (iog, [[0, NHT], [1, 8]], 0),
            (iogb, [[0, NHT], [1, 8]], int(BIGIDX)),
            (ebase, [[0, NHT], [CAP, E]], 0),
            (ybase, [[0, NHT], [YBLK, 8], [CAP, 8]], 0),
        ):
            P.op("gpsimd", lambda e, ap_=ap_, pat=pat, base=base: e.iota(
                ap_[:], pattern=pat, base=base, channel_multiplier=0,
                allow_small_or_imprecise_dtypes=True), "Pg")
        P.op("gpsimd", lambda e: e.iota(tokid[:], pattern=[[128, SB]], base=0,
                                        channel_multiplier=1,
                                        allow_small_or_imprecise_dtypes=True), "Pg")
        wIotas = P.ctr["Pg"]

        tok_eb = P.dma("sync", e_bias_sb[:, :], e_bias[:, :], "L")
        tok_cm = P.dma("sync", cm_sb[:, :], coremask[:, :], "L")
        tok_tb = P.dma("sync", tbase_sb[:, :], tokbase[:, :], "L")

        P.op("vector", lambda e: e.memset(ones1f[:], 1.0), "V")
        wOnes = P.ctr["V"]
        P.wtok("tensor", tok_eb)
        P.wtok("tensor", tok_tb)
        P.wait("tensor", "V", wOnes)
        P.op("tensor", lambda e: e.matmul(pb[4][:, 0:E], ones1f[:], e_bias_sb[:, :],
                                          start=True, stop=True), "T")
        P.op("tensor", lambda e: e.matmul(pb[5][:, 0:1], ones1f[:], tbase_sb[:, :],
                                          start=True, stop=True), "T")
        wEB = P.ctr["T"]
        P.wait("vector", "T", wEB)
        vop(lambda e: e.tensor_copy(out=ebias_bc[:], in_=pb[4][:, 0:E]))
        vop(lambda e: e.tensor_copy(out=tb1[:], in_=pb[5][:, 0:1]))
        P.wait("vector", "Pg", wIotas)
        vop(lambda e: e.tensor_scalar(out=tokid[:], in0=tokid[:], scalar1=tb1[:, 0:1],
                                      scalar2=None, op0=ALU.add))

        rk_ref = {}
        wn_ref = {}
        for h in range(2):
            lgv = logits[:, h * NHT:(h + 1) * NHT, :]
            P.wait("scalar", "V")
            P.op("scalar", lambda e, lgv=lgv: e.activation(out=scoresh[:], in_=lgv,
                                                           func=AF.Sigmoid), "Sc")
            P.wait("vector", "Sc")
            vop(lambda e: e.tensor_tensor(
                out=sfch[:], in0=scoresh[:],
                in1=ebias_bc[:, None, :].to_broadcast([128, NHT, E]),
                op=ALU.add))
            vop(lambda e: e.tensor_reduce(
                out=gmh[:], in_=sfch[:].rearrange("p a (g f) -> p a g f", g=8),
                axis=AX.X, op=ALU.max))
            vop(lambda e: e.tensor_tensor(
                out=eqh[:].rearrange("p a (g f) -> p a g f", g=8),
                in0=sfch[:].rearrange("p a (g f) -> p a g f", g=8),
                in1=gmh[:, :, :, None].to_broadcast([128, NHT, 8, 8]),
                op=ALU.is_equal))
            vop(lambda e: e.scalar_tensor_tensor(
                out=ivh[:], in0=eqh[:], scalar=-BIGIDX, in1=io8b[:],
                op0=ALU.mult, op1=ALU.add))
            vop(lambda e: e.tensor_reduce(
                out=gm2h[:], in_=ivh[:].rearrange("p a (g f) -> p a g f", g=8),
                axis=AX.X, op=ALU.min))
            vop(lambda e: e.tensor_tensor(
                out=eqh[:].rearrange("p a (g f) -> p a g f", g=8),
                in0=io8[:].rearrange("p a (g f) -> p a g f", g=8),
                in1=gm2h[:, :, :, None].to_broadcast([128, NHT, 8, 8]),
                op=ALU.is_equal))
            vop(lambda e: e.scalar_tensor_tensor(
                out=tmpA[:], in0=eqh[:], scalar=-BIGF, in1=sfch[:],
                op0=ALU.mult, op1=ALU.add))
            vop(lambda e: e.tensor_reduce(
                out=gm2h[:], in_=tmpA[:].rearrange("p a (g f) -> p a g f", g=8),
                axis=AX.X, op=ALU.max))
            vop(lambda e: e.tensor_tensor(out=gsA[:], in0=gmh[:], in1=gm2h[:], op=ALU.add))
            vop(lambda e: e.memset(gmask[:], 0.0))
            cur, nxt = gsA, gsB
            for r4 in range(4):
                vop(lambda e, cur=cur: e.tensor_reduce(out=m32[:], in_=cur[:], axis=AX.X,
                                                       op=ALU.max))
                vop(lambda e, cur=cur: e.tensor_tensor(
                    out=gm2h[:], in0=cur[:],
                    in1=m32[:, :, None].to_broadcast([128, NHT, 8]),
                    op=ALU.is_equal))
                vop(lambda e: e.scalar_tensor_tensor(
                    out=gm2h[:], in0=gm2h[:], scalar=-BIGIDX, in1=iogb[:],
                    op0=ALU.mult, op1=ALU.add))
                vop(lambda e: e.tensor_reduce(out=mi32[:], in_=gm2h[:], axis=AX.X,
                                              op=ALU.min))
                vop(lambda e: e.tensor_tensor(
                    out=gm2h[:], in0=iog[:],
                    in1=mi32[:, :, None].to_broadcast([128, NHT, 8]),
                    op=ALU.is_equal))
                vop(lambda e: e.tensor_add(gmask[:], gmask[:], gm2h[:]))
                vop(lambda e, cur=cur, nxt=nxt: e.scalar_tensor_tensor(
                    out=nxt[:], in0=gm2h[:], scalar=-BIGF, in1=cur[:],
                    op0=ALU.mult, op1=ALU.add))
                cur, nxt = nxt, cur
            vop(lambda e: e.tensor_tensor(
                out=tmpA[:].rearrange("p a (g f) -> p a g f", g=8),
                in0=sfch[:].rearrange("p a (g f) -> p a g f", g=8),
                in1=gmask[:, :, :, None].to_broadcast([128, NHT, 8, 8]),
                op=ALU.mult))
            vop(lambda e: e.memset(rkA[:], BIGIDX))
            vop(lambda e: e.memset(indh[:], 0.0))
            curt, nxtt = tmpA, tmpB
            curr, nxtr = rkA, rkB
            for k in range(8):
                vop(lambda e, curt=curt: e.tensor_reduce(out=m32[:], in_=curt[:],
                                                         axis=AX.X, op=ALU.max))
                vop(lambda e, curt=curt: e.tensor_tensor(
                    out=eqh[:], in0=curt[:],
                    in1=m32[:, :, None].to_broadcast([128, NHT, E]),
                    op=ALU.is_equal))
                vop(lambda e: e.scalar_tensor_tensor(
                    out=ivh[:], in0=eqh[:], scalar=-BIGIDX, in1=io64b[:],
                    op0=ALU.mult, op1=ALU.add))
                vop(lambda e: e.tensor_reduce(out=mi32[:], in_=ivh[:], axis=AX.X,
                                              op=ALU.min))
                vop(lambda e: e.tensor_tensor(
                    out=eqh[:], in0=io64[:],
                    in1=mi32[:, :, None].to_broadcast([128, NHT, E]),
                    op=ALU.is_equal))
                vop(lambda e: e.tensor_add(indh[:], indh[:], eqh[:]))
                vop(lambda e, k=k: e.tensor_scalar(
                    out=ivh[:], in0=eqh[:], scalar1=-BIGIDX, scalar2=float(k) + BIGIDX,
                    op0=ALU.mult, op1=ALU.add))
                vop(lambda e, curr=curr, nxtr=nxtr: e.tensor_tensor(
                    out=nxtr[:], in0=curr[:], in1=ivh[:], op=ALU.min))
                curr, nxtr = nxtr, curr
                vop(lambda e, curt=curt, nxtt=nxtt: e.scalar_tensor_tensor(
                    out=nxtt[:], in0=eqh[:], scalar=-BIGF, in1=curt[:],
                    op0=ALU.mult, op1=ALU.add))
                curt, nxtt = nxtt, curt
            vop(lambda e: e.tensor_tensor(out=wnh[:], in0=scoresh[:], in1=indh[:],
                                          op=ALU.mult))
            vop(lambda e: e.tensor_reduce(out=m32[:], in_=wnh[:], axis=AX.X, op=ALU.add))
            vop(lambda e: e.tensor_scalar_add(m32[:], m32[:], 1e-20))
            vop(lambda e: e.reciprocal(mi32[:], m32[:]))
            vop(lambda e: e.tensor_scalar_mul(mi32[:], mi32[:], 2.5))
            vop(lambda e: e.tensor_tensor(
                out=wnh[:], in0=wnh[:],
                in1=mi32[:, :, None].to_broadcast([128, NHT, E]),
                op=ALU.mult))
            wRouteH = P.ctr["V"]
            P.wait("tensor", "V", wRouteH)
            trev = {}
            for sbl in range(NHT):
                gsb = h * NHT + sbl
                pbs = pb[4 + (sbl % 2)]
                if sbl >= 2:
                    P.wait("tensor", "V", trev[sbl - 2])
                P.op("tensor", lambda e, sbl=sbl, pbs=pbs: e.transpose(
                    pbs[0:E, 0:128], indh[:, sbl, :], ident_f[:]), "T")
                wT = P.ctr["T"]
                P.wait("vector", "T", wT)
                P.op("vector", lambda e, gsb=gsb, pbs=pbs: e.tensor_copy(
                    out=indT[:, gsb * 128:(gsb + 1) * 128], in_=pbs[0:E, 0:128]), "V")
                trev[sbl] = P.ctr["V"]
            if h == 0:
                vop(lambda e: e.tensor_copy(out=wn_h0[:], in_=wnh[:]))
                vop(lambda e, curr=curr: e.tensor_copy(out=rk_h0[:], in_=curr[:]))
                rk_ref[0] = rk_h0
                wn_ref[0] = wn_h0
            else:
                rk_ref[1] = curr
                wn_ref[1] = wnh
        wRouteAll = P.ctr["V"]

        P.wait("vector", "V", wRouteAll)
        vop(lambda e: e.tensor_tensor_scan(
            out=posT[:], data0=indT[:], data1=indT[:], initial=0.0,
            op0=ALU.add, op1=ALU.bypass))
        wCnt = P.ctr["V"]
        P.wait("tensor", "V", wCnt)
        P.op("tensor", lambda e: e.transpose(pb[4][0:1, 0:E], posT[:, S - 1:S],
                                             ident_f[0:64, 0:64]), "T")
        wCntT = P.ctr["T"]
        P.wait("vector", "T", wCntT)
        vop(lambda e: e.tensor_copy(out=cnt1[:, :], in_=pb[4][0:1, 0:E]))
        wCnt1 = P.ctr["V"]
        vop(lambda e: e.tensor_tensor(out=posT[:], in0=posT[:], in1=indT[:],
                                      op=ALU.subtract))
        wPosT = P.ctr["V"]
        _cut(1.1)
        P.wait("gpsimd", "V", wCnt1)
        tok_ci = P.dma("gpsimd", cnt_in[0:1, :], cnt1[:, :], "G")
        P.wtok("gpsimd", tok_ci)
        _cut(1.2)
        P.coll(lambda e: e.collective_compute(
            "AllGather", ALU.bypass, replica_groups=[list(range(NCORE))],
            ins=[cnt_in.opt()], outs=[cnt_out.opt()]))
        _cut(1.4)
        P.wait("sync", "CC")
        tok_cnts = P.dma("sync", cnts_sb[:, :], cnt_out[:, :], "L")
        P.wtok("tensor", tok_cnts)
        _cut(1.6)
        P.wtok("tensor", tok_cm)
        P.op("tensor", lambda e: e.matmul(pb[4][0:E, 0:1], cnts_sb[:, :], cm_sb[:, :],
                                          start=True, stop=True), "T")
        wBase = P.ctr["T"]
        P.wait("vector", "T", wBase)
        vop(lambda e: e.tensor_copy(out=base64[:, 0:1], in_=pb[4][:, 0:1]))
        _cut(1.7)
        P.wait("vector", "V", wPosT)
        vop(lambda e: e.tensor_scalar(out=posT[:], in0=posT[:],
                                      scalar1=base64[0:64, 0:1], scalar2=None, op0=ALU.add))
        _cut(1.8)
        vop(lambda e: e.tensor_scalar(out=keepT[:], in0=posT[:], scalar1=float(CAP),
                                      scalar2=None, op0=ALU.is_lt))
        _cut(1.9)
        vop(lambda e: e.tensor_tensor(out=keepT[:], in0=keepT[:], in1=indT[:],
                                      op=ALU.mult))
        wKeep = P.ctr["V"]
        _cut(2)

        P.wait("tensor", "V", wKeep)
        trev = {}
        for h in range(2):
            if h == 1:
                _cut(2.478)
            for sbl in range(NHT):
                gsb = h * NHT + sbl
                pbs = pb[4 + (sbl % 2)]
                pbs2 = pb[2 + (sbl % 2)]
                if gsb >= 2:
                    P.wait("tensor", "V", trev[gsb - 2])
                P.op("tensor", lambda e, gsb=gsb, pbs=pbs: e.transpose(
                    pbs[0:128, 0:E], posT[:, gsb * 128:(gsb + 1) * 128],
                    ident_f[0:64, 0:64]), "T")
                P.op("tensor", lambda e, gsb=gsb, pbs2=pbs2: e.transpose(
                    pbs2[0:128, 0:E], keepT[:, gsb * 128:(gsb + 1) * 128],
                    ident_f[0:64, 0:64]), "T")
                wT = P.ctr["T"]
                P.wait("vector", "T", wT)
                P.op("vector", lambda e, sbl=sbl, pbs=pbs: e.tensor_copy(
                    out=posh[:, sbl, :], in_=pbs[0:128, 0:E]), "V")
                P.op("vector", lambda e, sbl=sbl, pbs2=pbs2: e.tensor_copy(
                    out=keeph[:, sbl, :], in_=pbs2[0:128, 0:E]), "V")
                trev[gsb] = P.ctr["V"]
            _cut(2.2)
            vop(lambda e: e.tensor_tensor(out=sidh[:], in0=ebase[:], in1=posh[:],
                                          op=ALU.add))
            vop(lambda e: e.tensor_scalar_add(sidh[:], sidh[:], -BIGIDX))
            vop(lambda e: e.tensor_tensor(out=sidh[:], in0=sidh[:], in1=keeph[:],
                                          op=ALU.mult))
            vop(lambda e: e.tensor_scalar_add(sidh[:], sidh[:], BIGIDX))
            _cut(2.3)
            vop(lambda e: e.tensor_tensor(out=ivh[:], in0=ybase[:], in1=posh[:],
                                          op=ALU.add))
            _cut(2.32)
            vop(lambda e: e.tensor_scalar_add(ivh[:], ivh[:], -ZROWID))
            _cut(2.34)
            vop(lambda e: e.tensor_tensor(out=ivh[:], in0=ivh[:], in1=keeph[:],
                                          op=ALU.mult))
            _cut(2.36)
            vop(lambda e: e.tensor_scalar_add(ivh[:], ivh[:], ZROWID))
            _cut(2.4)
            for _d in range(DUMMYV if h == 0 else 0):
                vop(lambda e: e.tensor_scalar_add(m32[:], m32[:], 0.0))
            for _d in range(DUMMYR if h == 0 else 0):
                vop(lambda e: e.tensor_reduce(out=m32[:], in_=tmpA[:], axis=AX.X,
                                              op=ALU.add))
            for k in range(8):
                _cut(2.4 + 0.01 * (h * 8 + k))
                sk = K7SKIP if (h == 0 and k == 7) else 0
                if sk & 1 == 0:
                    vop(lambda e, h=h, k=k: e.tensor_scalar(
                        out=eqh[:], in0=rk_ref[h][:], scalar1=float(k), scalar2=None,
                        op0=ALU.is_equal))
                if sk & 2 == 0:
                    vop(lambda e: e.tensor_tensor(out=tmpA[:], in0=eqh[:], in1=sidh[:],
                                                  op=ALU.mult))
                    vop(lambda e: e.tensor_tensor(out=tmpB[:], in0=eqh[:], in1=ivh[:],
                                                  op=ALU.mult)) if False else None
                if sk & 4 == 0:
                    vop(lambda e: e.tensor_reduce(out=m32[:], in_=tmpA[:], axis=AX.X,
                                                  op=ALU.add))
                if sk & 8 == 0:
                    vop(lambda e, h=h, k=k: e.tensor_copy(
                        out=scu[:, h * NHT:(h + 1) * NHT, k], in_=m32[:]))
                if sk & 2 == 0:
                    vop(lambda e: e.tensor_tensor(out=tmpA[:], in0=eqh[:], in1=ivh[:],
                                                  op=ALU.mult))
                if sk & 4 == 0:
                    vop(lambda e: e.tensor_reduce(out=m32[:], in_=tmpA[:], axis=AX.X,
                                                  op=ALU.add))
                if sk & 8 == 0:
                    vop(lambda e, h=h, k=k: e.tensor_copy(
                        out=ycu[:, h * NHT:(h + 1) * NHT, k], in_=m32[:]))
                if sk & 2 == 0:
                    vop(lambda e, h=h: e.tensor_tensor(out=tmpA[:], in0=eqh[:],
                                                       in1=wn_ref[h][:], op=ALU.mult))
                if sk & 4 == 0:
                    vop(lambda e: e.tensor_reduce(out=m32[:], in_=tmpA[:], axis=AX.X,
                                                  op=ALU.add))
                if sk & 8 == 0:
                    vop(lambda e, h=h, k=k: e.tensor_copy(
                        out=wk8[:, h * NHT:(h + 1) * NHT, k], in_=m32[:]))
        _cut(2.59)
        vop(lambda e: e.tensor_copy(out=scu_u[:], in_=scu[:]))
        vop(lambda e: e.tensor_copy(out=ycu_u[:], in_=ycu[:]))
        wExtract = P.ctr["V"]

        if debug:
            P.wait("gpsimd", "V", wExtract)
            P.dma("gpsimd", pos_db[:, :].rearrange("(p f) b -> p (f b)", p=64),
                  posT[:], "G")
            P.dma("gpsimd", ind_db[:, :].rearrange("(p f) b -> p (f b)", p=64),
                  indT[:], "G")
            for nm2, ap2 in (("sfch_db", sfch[:]), ("gmask_db", gmask[:]),
                             ("indh_db", indh[:]), ("rk1_db", rk_ref[1][:]),
                             ("wn1_db", wn_ref[1][:]), ("rk0_db", rk_h0[:]),
                             ("wn0_db", wn_h0[:]), ("io64_db", io64[:]),
                             ("io64b_db", io64b[:]), ("iog_db", iog[:]),
                             ("iogb_db", iogb[:]), ("io8_db", io8[:]),
                             ("io8b_db", io8b[:]), ("ebase_db", ebase[:]),
                             ("ybase_db", ybase[:]), ("posh_db", posh[:]),
                             ("keeph_db", keeph[:]), ("sidh_db", sidh[:]),
                             ("ivh_db", ivh[:]), ("scoresh_db", scoresh[:]),
                             ("base64_db", base64[:]), ("tokid2_db", tokid[:])):
                t2 = nc.dram_tensor(nm2, [128] + [int(x) for x in ap2.shape[1:]],
                                    ap2.dtype, kind="ExternalOutput").ap()
                P.dma("gpsimd", t2[tuple([slice(None)] * len(ap2.shape))], ap2, "G")
            P.wall("gpsimd", "G")

        _cut(3)
        vop(lambda e: e.memset(ztile[:], 0.0), fence=False)
        wZ = P.ctr["V"]
        P.wait("gpsimd", "V", wZ)
        slabv = slab_in.rearrange("(p a) c -> p a c", p=128)
        ztoks = []
        for j in range(2):
            ztoks.append(P.dma(
                "gpsimd", slabv[:, j * 256:(j + 1) * 256, :],
                ztile[:].rearrange("p (a c) -> p a c", c=2), "G"))
        for t in ztoks:
            P.wtok("gpsimd", t)
        _cut(3.2)
        P.wait("vector", "V", wExtract)
        bc_cell = {}

        def _mkbc(e):
            r = e.alloc_register(name="bcreg")
            e.reg_mov(r, E * CAP - 1)
            bc_cell["r"] = r
        P.raw("gpsimd", _mkbc)
        for k in range(8):
            P.wall("vector", "G")
            vop(lambda e: e.tensor_copy(out=payk[:, :, 0:1],
                                        in_=tokid[:, :, None]))
            vop(lambda e, k=k: e.tensor_copy(out=payk[:, :, 1:2], in_=wk8[:, :, k:k + 1]))
            wPay = P.ctr["V"]
            P.wait("gpsimd", "V", wPay)
            for sb2 in range(SB):
                P.ind(lambda e, k=k, sb2=sb2: e.indirect_dma_start(
                    out=slab_in[:, :],
                    out_offset=IndirectOffsetOnAxis(scu_u[:, sb2, k:k + 1], 0),
                    in_=payk[:, sb2, :], in_offset=None,
                    bounds_check=bc_cell["r"], oob_is_err=False))
        P.wall("gpsimd", "G")
        _cut(3.4)
        P.wait("gpsimd", "CC")
        if not SKIPRS:
            P.coll(lambda e: e.collective_compute(
                "ReduceScatter", ALU.add, replica_groups=[list(range(NCORE))],
                ins=[slab_in.opt()], outs=[slab_out.opt()]))
        wRS = P.ctr["CC"]
        _cut(3.6)
        P.wait("gpsimd", "CC", wRS)
        P.coll(lambda e: e.collective_compute(
            "AllGather", ALU.bypass, replica_groups=[list(range(NCORE))],
            ins=[xln2_sp.opt()], outs=[xg_out.opt()]))
        wAGX = P.ctr["CC"]

        _cut(4)
        AR.pop()
        AR.push()
        sgT_sb = AR.alloc([128, 8, F2], BF16, "sgTsb")
        suT_sb = AR.alloc([128, 8, F2], BF16, "suTsb")
        sdT_sb = AR.alloc([128, 8, H], BF16, "sdTsb")
        xt2T = AR.alloc([128, 8, 1024], BF16, "xt2T")
        x2h = AR.alloc([128, 4, H], F32, "x2h")
        gh = AR.alloc([128, 8, 1024], BF16, "gh")
        uh = AR.alloc([128, 8, 1024], BF16, "uh")
        dsh = AR.alloc([128, 8, 1024], BF16, "dsh")

        for hk in range(8):
            P.dma("gpsimd", sgT_sb[:, hk, :], sgT[hk * 128:(hk + 1) * 128, :], "G")
            P.dma("gpsimd", suT_sb[:, hk, :], suT[hk * 128:(hk + 1) * 128, :], "G")
            P.dma("gpsimd", sdT_sb[:, hk, :], sdT[hk * 128:(hk + 1) * 128, :], "G")
        P.wall("tensor", "G")
        P.wall("sync", "G")

        padd = {}
        tok_x3store = [None]
        wGU = [None]
        bl = {}
        P.wait("tensor", "V")
        P.wait("tensor", "Sc")
        for p2 in range(4):
            s0 = p2 * 1024
            if wGU[0] is not None:
                P.wait("sync", "T", wGU[0])
            tok_x = P.dma("sync", xt2T[:],
                          xln2T_sp[:, s0:s0 + 1024].rearrange("(a p) b -> p a b", p=128), "L")
            P.wtok("tensor", tok_x)
            gidx = 0
            for dst, wsb in ((gh, sgT_sb), (uh, suT_sb)):
                for f2b in range(8):
                    for half in range(2):
                        b = gidx % 4
                        gidx += 1
                        pbk = pb[b]
                        prev = bl.get(b)
                        if prev is not None:
                            P.wait("tensor", prev[0], prev[1])
                        for hk in range(8):
                            P.op("tensor", lambda e, wsb=wsb, f2b=f2b, hk=hk, half=half,
                                 pbk=pbk: e.matmul(
                                     pbk[:, 0:512], wsb[:, hk, f2b * 128:(f2b + 1) * 128],
                                     xt2T[:, hk, half * 512:(half + 1) * 512],
                                     start=(hk == 0), stop=(hk == 7),
                                     skip_group_check=True), "T")
                        wMM = P.ctr["T"]
                        if dst is gh:
                            P.wait("scalar", "T", wMM)
                            P.op("scalar", lambda e, dst=dst, f2b=f2b, half=half, pbk=pbk:
                                 e.activation(out=dst[:, f2b, half * 512:(half + 1) * 512],
                                              in_=pbk[:, 0:512], func=AF.Silu), "Sc")
                            bl[b] = ("Sc", P.ctr["Sc"])
                        else:
                            P.wait("vector", "T", wMM)
                            P.op("vector", lambda e, dst=dst, f2b=f2b, half=half, pbk=pbk:
                                 e.tensor_copy(out=dst[:, f2b, half * 512:(half + 1) * 512],
                                               in_=pbk[:, 0:512]), "V")
                            bl[b] = ("V", P.ctr["V"])
            wUp = P.ctr["V"]
            wGU[0] = P.ctr["T"]
            P.wait("vector", "Sc")
            P.op("vector", lambda e: e.tensor_tensor(out=gh[:], in0=gh[:], in1=uh[:],
                                                     op=ALU.mult), "V", fence=True)
            wHH = P.ctr["V"]
            P.wait("tensor", "V", wHH)
            gidx = 0
            for hb in range(8):
                for half in range(2):
                    b = gidx % 4
                    gidx += 1
                    pbk = pb[b]
                    prev = bl.get(b)
                    if prev is not None:
                        P.wait("tensor", prev[0], prev[1])
                    for fk in range(8):
                        P.op("tensor", lambda e, hb=hb, fk=fk, half=half, pbk=pbk:
                             e.matmul(pbk[:, 0:512], sdT_sb[:, fk, hb * 128:(hb + 1) * 128],
                                      gh[:, fk, half * 512:(half + 1) * 512],
                                      start=(fk == 0), stop=(fk == 7),
                                      skip_group_check=True), "T")
                    wMM = P.ctr["T"]
                    P.wait("vector", "T", wMM)
                    P.op("vector", lambda e, hb=hb, half=half, pbk=pbk: e.tensor_copy(
                        out=dsh[:, hb, half * 512:(half + 1) * 512], in_=pbk[:, 0:512]),
                        "V")
                    bl[b] = ("V", P.ctr["V"])
            wDS = P.ctr["V"]
            for half in range(2):
                if tok_x3store[0] is not None:
                    P.wtok("sync", tok_x3store[0])
                tok_x2 = P.dma("sync", x2h[:],
                               tokrows(x2_sp, s0 + half * 512, s0 + (half + 1) * 512), "L")
                P.wait("tensor", "V", wDS)
                P.wtok("vector", tok_x2)
                for sb2 in range(4):
                    pt = ptr[sb2 % 2]
                    if padd.get(sb2 % 2) is not None:
                        P.wait("tensor", "V", padd[sb2 % 2])
                    for ob in range(8):
                        P.op("tensor", lambda e, sb2=sb2, ob=ob, half=half, pt=pt:
                             e.transpose(pt[:, ob * 128:(ob + 1) * 128],
                                         dsh[:, ob, half * 512 + sb2 * 128:
                                             half * 512 + (sb2 + 1) * 128],
                                         ident_bf[:]), "T")
                    wT = P.ctr["T"]
                    P.wait("vector", "T", wT)
                    P.op("vector", lambda e, sb2=sb2, pt=pt: e.tensor_tensor(
                        out=x2h[:, sb2, :], in0=x2h[:, sb2, :], in1=pt[:, 0:H],
                        op=ALU.add), "V")
                    padd[sb2 % 2] = P.ctr["V"]
                wX3 = P.ctr["V"]
                P.wait("gpsimd", "V", wX3)
                tok_x3store[0] = P.dma(
                    "gpsimd", tokrows(x3_sp, s0 + half * 512, s0 + (half + 1) * 512),
                    x2h[:], "G")

        _cut(5)
        AR.pop()
        AR.push()
        wv_sb = AR.alloc([128, 64, 2], F32, "wv_sb")
        tok_u = AR.alloc([128, 64], U32, "tok_u")
        xe_sb = AR.alloc([128, 8, 1024], BF16, "xe_sb")
        xeT = AR.alloc([128, 8, CAP], BF16, "xeT")
        w1_sb = [AR.alloc([128, 8, F], BF16, "w1a"), AR.alloc([128, 8, F], BF16, "w1b")]
        w3_sb = [AR.alloc([128, 8, F], BF16, "w3a"), AR.alloc([128, 8, F], BF16, "w3b")]
        w2_sb = [AR.alloc([128, 4, H], BF16, "w2a"), AR.alloc([128, 4, H], BF16, "w2b")]
        h1_sb = AR.alloc([128, 4, CAP], BF16, "h1_sb")
        h3_sb = AR.alloc([128, 4, CAP], BF16, "h3_sb")
        ye_sb = AR.alloc([128, 8, H], BF16, "ye_sb")
        zbf = AR.alloc([128, 1024], BF16, "zbf")

        vop(lambda e: e.memset(zbf[:], 0.0), fence=False)
        wZB = P.ctr["V"]
        P.wait("gpsimd", "V", wZB)
        P.dma("gpsimd", ye_loc[EO * CAP:YBLK, :].rearrange("(a p) b -> p a b", p=128),
              zbf[:].rearrange("p (a b) -> p a b", a=1), "G")

        P.wait("sync", "CC", wRS)
        for col in range(64):
            tok_wv = P.dma("sync", wv_sb[:, col, :],
                           slab_out[col * 128:(col + 1) * 128, :], "L")
        P.wall("vector", "L")
        vop(lambda e: e.tensor_copy(out=tok_u[:], in_=wv_sb[:, :, 0]))
        wTOKU = P.ctr["V"]
        P.wait("gpsimd", "V", wTOKU)
        P.wait("gpsimd", "CC", wAGX)

        wei_toks = {}
        trv = {0: padd.get(0), 1: padd.get(1)}
        wXPOSE = {}
        wWUSE = {}
        tok_ye = {}
        for ei in range(EO):
            j = ei % 2
            if ei >= 2:
                P.wait("gpsimd", "T", wWUSE[j])
            for hk in range(8):
                P.dma("gpsimd", w1_sb[j][:, hk, :],
                      w1T[ei, hk * 128:(hk + 1) * 128, :], "G")
                P.dma("gpsimd", w3_sb[j][:, hk, :],
                      w3T[ei, hk * 128:(hk + 1) * 128, :], "G")
            for fk in range(4):
                wei_toks[ei] = P.dma("gpsimd", w2_sb[j][:, fk, :],
                                     w2T[ei, fk * 128:(fk + 1) * 128, :], "G")
            if ei >= 1:
                P.wait("gpsimd", "T", wXPOSE[ei - 1])
            gat = []
            for jb in range(8):
                gat.append(P.ind(lambda e, ei=ei, jb=jb: e.indirect_dma_start(
                    out=xe_sb[:, jb, :], out_offset=None, in_=xg_out[:, :],
                    in_offset=IndirectOffsetOnAxis(tok_u[:, ei * 8 + jb:ei * 8 + jb + 1], 0))))
            for t in gat:
                P.wtok("tensor", t)
            for hb in range(8):
                g = ei * 8 + hb
                pt = ptr[hb % 2]
                prev = trv.get(g - 2) if g >= 2 else trv.get(g)
                if prev is not None:
                    P.wait("tensor", "V", prev)
                for jb in range(8):
                    P.op("tensor", lambda e, hb=hb, jb=jb, pt=pt: e.transpose(
                        pt[:, jb * 128:(jb + 1) * 128],
                        xe_sb[:, jb, hb * 128:(hb + 1) * 128], ident_bf[:]), "T")
                wT = P.ctr["T"]
                P.wait("vector", "T", wT)
                P.op("vector", lambda e, hb=hb, pt=pt: e.tensor_copy(
                    out=xeT[:, hb, :], in_=pt[:, 0:CAP]), "V")
                trv[ei * 8 + hb] = P.ctr["V"]
            wXPOSE[ei] = P.ctr["T"]
            wXE = P.ctr["V"]
            P.wait("tensor", "V", wXE)
            P.wtok("tensor", wei_toks[ei])
            gidx = 0
            for dst, wsb, act in ((h1_sb, w1_sb[j], True), (h3_sb, w3_sb[j], False)):
                for fb in range(4):
                    for half in range(2):
                        b = gidx % 4
                        gidx += 1
                        pbk = pb[b]
                        prev = bl.get(b)
                        if prev is not None:
                            P.wait("tensor", prev[0], prev[1])
                        for hk in range(8):
                            P.op("tensor", lambda e, wsb=wsb, fb=fb, hk=hk, half=half,
                                 pbk=pbk: e.matmul(
                                     pbk[:, 0:512], wsb[:, hk, fb * 128:(fb + 1) * 128],
                                     xeT[:, hk, half * 512:(half + 1) * 512],
                                     start=(hk == 0), stop=(hk == 7),
                                     skip_group_check=True), "T")
                        wMM = P.ctr["T"]
                        if act:
                            P.wait("scalar", "T", wMM)
                            P.op("scalar", lambda e, dst=dst, fb=fb, half=half, pbk=pbk:
                                 e.activation(out=dst[:, fb, half * 512:(half + 1) * 512],
                                              in_=pbk[:, 0:512], func=AF.Silu), "Sc")
                            bl[b] = ("Sc", P.ctr["Sc"])
                        else:
                            P.wait("vector", "T", wMM)
                            P.op("vector", lambda e, dst=dst, fb=fb, half=half, pbk=pbk:
                                 e.tensor_copy(out=dst[:, fb, half * 512:(half + 1) * 512],
                                               in_=pbk[:, 0:512]), "V")
                            bl[b] = ("V", P.ctr["V"])
            P.wait("vector", "Sc")
            P.op("vector", lambda e: e.tensor_tensor(out=h1_sb[:], in0=h1_sb[:],
                                                     in1=h3_sb[:], op=ALU.mult),
                 "V", fence=True)
            wHH = P.ctr["V"]
            P.wait("tensor", "V", wHH)
            if ei >= 1:
                P.wtok("vector", tok_ye[ei - 1])
            gidx = 0
            for sb2 in range(8):
                for half in range(2):
                    b = gidx % 4
                    gidx += 1
                    pbk = pb[b]
                    prev = bl.get(b)
                    if prev is not None:
                        P.wait("tensor", prev[0], prev[1])
                    for fk in range(4):
                        P.op("tensor", lambda e, sb2=sb2, fk=fk, half=half, pbk=pbk, j=j:
                             e.matmul(pbk[:, 0:512],
                                      h1_sb[:, fk, sb2 * 128:(sb2 + 1) * 128],
                                      w2_sb[j][:, fk, half * 512:(half + 1) * 512],
                                      start=(fk == 0), stop=(fk == 3),
                                      skip_group_check=True), "T")
                    wMM = P.ctr["T"]
                    P.wait("vector", "T", wMM)
                    P.op("vector", lambda e, sb2=sb2, half=half, pbk=pbk, ei=ei:
                         e.tensor_scalar(
                             out=ye_sb[:, sb2, half * 512:(half + 1) * 512],
                             in0=pbk[:, 0:512],
                             scalar1=wv_sb[:, ei * 8 + sb2, 1:2], scalar2=None,
                             op0=ALU.mult), "V")
                    bl[b] = ("V", P.ctr["V"])
            wYE = P.ctr["V"]
            wWUSE[j] = P.ctr["T"]
            P.wait("gpsimd", "V", wYE)
            tok_ye[ei] = P.dma(
                "gpsimd", ye_loc[ei * CAP:(ei + 1) * CAP, :].rearrange(
                    "(a p) b -> p a b", p=128), ye_sb[:], "G")

        _cut(6)
        P.wall("gpsimd", "G")
        P.wall("sync", "G")
        P.wait("gpsimd", "CC")
        P.coll(lambda e: e.collective_compute(
            "AllGather", ALU.bypass, replica_groups=[list(range(NCORE))],
            ins=[ye_loc.opt()], outs=[ye_out.opt()]))
        wAGY = P.ctr["CC"]
        P.wait("gpsimd", "CC", wAGY)
        AR.pop()
        AR.push()
        x3g = AR.alloc([128, 8, H], F32, "x3g")
        x3g2 = AR.alloc([128, 8, H], F32, "x3g2")
        tok_out = {}
        for grp in range(4):
            buf = x3g if grp % 2 == 0 else x3g2
            s0 = grp * 1024
            if grp >= 2:
                P.wtok("sync", tok_out[grp - 2])
            tok_x3 = P.dma("sync", buf[:], tokrows(x3_sp, s0, s0 + 1024), "L")
            P.wtok("gpsimd", tok_x3)
            add_toks = {}
            for k in range(8):
                for sb2 in range(8):
                    gsb = grp * 8 + sb2
                    prev = add_toks.get((k - 1, sb2))
                    if prev is not None:
                        P.wtok("gpsimd", prev)
                    add_toks[(k, sb2)] = P.ind(
                        lambda e, gsb=gsb, k=k, sb2=sb2, buf=buf: e.indirect_dma_start(
                            out=buf[:, sb2, :], out_offset=None, in_=ye_out[:, :],
                            in_offset=IndirectOffsetOnAxis(ycu_u[:, gsb, k:k + 1], 0),
                            compute_op=ALU.add))
            for kk in range(8):
                P.wtok("sync", add_toks[(7, kk)])
            tok_out[grp] = P.dma("sync", tokrows(out_ap, s0, s0 + 1024), buf[:], "L")

    try:
        _phase2()
    except _Cut:
        pass

    if debug:
        P.wait("gpsimd", "V")
        P.wait("gpsimd", "Sc")
        if STAGE2 >= 6:
            for nm2, ap2 in (("scu_db", dbg_refs["scu"][:]),
                             ("ycu_db", dbg_refs["ycu"][:]),
                             ("wk8_db", dbg_refs["wk8"][:]),
                             ("tokid_db", dbg_refs["tokid"][:])):
                t2 = nc.dram_tensor(nm2, [128] + [int(x) for x in ap2.shape[1:]],
                                    ap2.dtype, kind="ExternalOutput").ap()
                P.dma("gpsimd", t2[tuple([slice(None)] * len(ap2.shape))], ap2, "G")
            slabo_db = nc.dram_tensor("slabo_db", [EO * CAP, 2], F32,
                                      kind="ExternalOutput").ap()
            P.dma("gpsimd", slabo_db[:, :], slab_out[:, :], "G")
            yel_db = nc.dram_tensor("yel_db", [YBLK, H], BF16,
                                    kind="ExternalOutput").ap()
            P.dma("gpsimd", yel_db[:, :].rearrange("(a p) b -> p a b", p=128),
                  ye_loc[:, :].rearrange("(a p) b -> p a b", p=128), "G")
            xg_db = nc.dram_tensor("xg_db", [256, H], BF16,
                                   kind="ExternalOutput").ap()
            P.dma("gpsimd", xg_db[:, :], xg_out[0:256, :], "G")
        P.dma("gpsimd", logits_db[:, :].rearrange("(a p) b -> p a b", p=128),
              logits[:], "G")
        P.dma("gpsimd", xln2_db[:, :].rearrange("(a p) b -> p a b", p=128),
              xln2_sp[:, :].rearrange("(a p) b -> p a b", p=128), "G")
        for nm, ap in (("hbuf_db", hbuf[:]), ("dt_db", dt_all[:]), ("sg_db", sg_all[:]),
                       ("xlnT_db", xlnT[:]), ("xln_db", xln_bf[:]),
                       ("lnst_db", lnst[:])):
            t = nc.dram_tensor(nm, [128] + [int(x) for x in ap.shape[1:]], ap.dtype,
                               kind="ExternalOutput").ap()
            idx = tuple([slice(None)] * len(ap.shape))
            P.dma("gpsimd", t[idx], ap, "G")
        t = nc.dram_tensor("ssmp_db", [96, SC], F32, kind="ExternalOutput").ap()
        P.dma("gpsimd", t[:, :], ssm_p[:, :], "G")

    with nc.Block() as block:
        @block.sync
        def _(eng):
            for fn in P.items["sync"]:
                fn(eng)

        @block.scalar
        def _(eng):
            for fn in P.items["scalar"]:
                fn(eng)

        @block.vector
        def _(eng):
            for fn in P.items["vector"]:
                fn(eng)

        @block.tensor
        def _(eng):
            for fn in P.items["tensor"]:
                fn(eng)

        @block.gpsimd
        def _(eng):
            for fn in P.items["gpsimd"]:
                fn(eng)
            for pool in ("L", "G"):
                for n in P.pools[pool]["names"]:
                    if P.ctr[n] > 0:
                        eng.wait_ge(P.sems[n], P.ctr[n])

    build_program.last_ctr = dict(P.ctr)
    return nc


def make_in_maps(inputs):
    in_maps = []
    for c in range(NCORE):
        cm = np.zeros((8, 1), np.float32)
        cm[:c] = 1.0
        m = {
            "x_loc": np.ascontiguousarray(np.asarray(inputs["L0_attn_x"])[c]),
            "W_in": np.asarray(inputs["L0_attn_W_in"]),
            "conv_wT": np.ascontiguousarray(np.asarray(inputs["L0_attn_conv_w"]).reshape(4, C).T),
            "conv_b": np.ascontiguousarray(np.asarray(inputs["L0_attn_conv_b"]).reshape(C, 1)),
            "W_x": np.asarray(inputs["L0_attn_W_x"]),
            "W_dt": np.asarray(inputs["L0_attn_W_dt"]),
            "b_dt": np.ascontiguousarray(np.asarray(inputs["L0_attn_b_dt"]).reshape(1, C)),
            "A_log": np.asarray(inputs["L0_attn_A_log"]),
            "D_vec": np.ascontiguousarray(np.asarray(inputs["L0_attn_D"]).reshape(C, 1)),
            "W_out": np.asarray(inputs["L0_attn_W_out"]),
            "gwT": np.ascontiguousarray(np.asarray(inputs["L0_ffn_gate_weight"]).T),
            "e_bias": np.ascontiguousarray(np.asarray(inputs["L0_ffn_e_bias"]).reshape(1, E)),
            "w1T": np.ascontiguousarray(np.asarray(inputs["L0_ffn_w1"])[c * EO:(c + 1) * EO].transpose(0, 2, 1)),
            "w2T": np.ascontiguousarray(np.asarray(inputs["L0_ffn_w2"])[c * EO:(c + 1) * EO].transpose(0, 2, 1)),
            "w3T": np.ascontiguousarray(np.asarray(inputs["L0_ffn_w3"])[c * EO:(c + 1) * EO].transpose(0, 2, 1)),
            "sgT": np.ascontiguousarray(np.asarray(inputs["L0_ffn_shared_gate"]).T),
            "suT": np.ascontiguousarray(np.asarray(inputs["L0_ffn_shared_up"]).T),
            "sdT": np.ascontiguousarray(np.asarray(inputs["L0_ffn_shared_down"]).T),
            "coremask": cm,
            "tokbase": np.array([[float(S * c)]], np.float32),
        }
        in_maps.append(m)
    return in_maps


def kernel(**inputs):
    nc = build_program()
    in_maps = make_in_maps(inputs)
    res = run_bass_kernel_spmd(nc, in_maps, core_ids=list(range(NCORE)))
    return np.stack([r["out_loc"] for r in res.results], axis=0)


# version 03 (final)
# speedup vs baseline: 1.1136x; 1.0856x over previous
import numpy as np
import concourse.bass as bass
import concourse.mybir as mybir
from concourse.bass import IndirectOffsetOnAxis
from concourse.bass_utils import run_bass_kernel_spmd

F32 = mybir.dt.float32
BF16 = mybir.dt.bfloat16
U8 = mybir.dt.uint8
U32 = mybir.dt.uint32
AF = mybir.ActivationFunctionType
ALU = mybir.AluOpType
AX = mybir.AxisListType

S = 4096
H = 1024
C = 2048
D = 16
CT = C // 128
SC = 512
NCH = S // SC
SB = S // 128
E = 64
EO = 8
CAP = 1024
F = 512
F2 = 1024
NCORE = 8
TG = S * NCORE
YBLK = EO * CAP + 128
YROWS = NCORE * YBLK
ZROWID = float(EO * CAP)
BIGIDX = float(1 << 20)
BIGF = 1.0e30

DEBUG = False
STAGE = 5
STAGE2 = 6
DUMMYV = 0
DUMMYR = 0
K7SKIP = 0
SKIPRS = False


class _Cut(Exception):
    pass


def _cut(n):
    if STAGE2 < n:
        raise _Cut()


class Prog:
    ENGINES = ("sync", "scalar", "vector", "tensor", "gpsimd")
    NPOOL = 8

    def __init__(self, nc):
        self.nc = nc
        self.items = {e: [] for e in self.ENGINES}
        self.sems = {}
        self.ctr = {}
        self.pools = {}
        for pool, eng in (("L", "sync"), ("G", "gpsimd")):
            names = []
            for k in range(self.NPOOL):
                n = f"q{pool}{k}"
                self.sem(n)
                names.append(n)
            self.pools[pool] = {"names": names, "eng": eng, "rr": 0}

    def sem(self, name):
        h = self.nc.semaphore(name).__enter__()
        self.sems[name] = h
        self.ctr[name] = 0

    def wait(self, eng, sem, val=None):
        v = self.ctr[sem] if val is None else val
        h = self.sems[sem]
        if v > 0:
            self.items[eng].append(lambda e, h=h, v=v: e.wait_ge(h, v))
        return v

    def dma(self, eng, out, in_, pool):
        p = self.pools[pool]
        assert eng == p["eng"]
        k = p["rr"]
        p["rr"] = (k + 1) % self.NPOOL
        name = p["names"][k]
        h = self.sems[name]
        drain = self.ctr[name]
        if drain > 0:
            self.items[eng].append(lambda e, h=h, v=drain: e.wait_ge(h, v))
        self.items[eng].append(
            lambda e, o=out, i=in_, h=h: e.dma_start(out=o, in_=i).then_inc(h, 16))
        self.ctr[name] += 16
        return (name, self.ctr[name])

    def wtok(self, eng, tok):
        if tok is None:
            return
        name, v = tok
        h = self.sems[name]
        self.items[eng].append(lambda e, h=h, v=v: e.wait_ge(h, v))

    def wall(self, eng, pool):
        for name in self.pools[pool]["names"]:
            self.wait(eng, name)

    def op(self, eng, fn, sem, fence=False):
        h = self.sems[sem]
        if fence and self.ctr[sem] > 0:
            v = self.ctr[sem]
            self.items[eng].append(lambda e, h=h, v=v: e.wait_ge(h, v))
        self.items[eng].append(lambda e, fn=fn, h=h: fn(e).then_inc(h, 1))
        self.ctr[sem] += 1
        return self.ctr[sem]

    def ind(self, fn):
        p = self.pools["G"]
        k = p["rr"]
        p["rr"] = (k + 1) % self.NPOOL
        name = p["names"][k]
        h = self.sems[name]
        drain = self.ctr[name]
        if drain > 0:
            self.items["gpsimd"].append(lambda e, h=h, v=drain: e.wait_ge(h, v))
        self.items["gpsimd"].append(lambda e, fn=fn, h=h: fn(e).then_inc(h, 16))
        self.ctr[name] += 16
        return (name, self.ctr[name])

    def coll(self, fn):
        h = self.sems["CC"]
        self.items["gpsimd"].append(lambda e, fn=fn, h=h: fn(e).then_inc(h, 1))
        self.ctr["CC"] += 1
        return self.ctr["CC"]

    def raw(self, eng, fn):
        self.items[eng].append(fn)


class Arena:
    def __init__(self, nc, nbytes):
        self.t = nc.alloc_sbuf_tensor("arena", [128, nbytes], U8)
        self.nbytes = nbytes
        self.off = 0
        self.marks = []

    def alloc(self, shape, dtype, name="", at=None):
        esz = mybir.dt.size(dtype)
        n = int(np.prod(shape[1:])) * esz
        if at is None:
            off = (self.off + 63) & ~63
            assert off + n <= self.nbytes, (
                f"SBUF arena overflow: {name}: {n}B at {off}, cap {self.nbytes}")
            self.off = off + n
        else:
            off = at
            assert off + n <= self.nbytes
        ap = self.t.ap()[:, off:off + n].bitcast(dtype)
        if len(shape) > 2:
            dims = " ".join(f"d{i}" for i in range(1, len(shape)))
            kw = {f"d{i}": shape[i] for i in range(1, len(shape))}
            ap = ap.rearrange(f"p ({dims}) -> p {dims}", **kw)
        assert shape[0] == 128
        return ap

    def mark(self):
        return (self.off + 63) & ~63

    def push(self):
        self.marks.append(self.off)

    def pop(self):
        self.off = self.marks.pop()


def tokrows(dram2d, r0, r1):
    return dram2d[r0:r1, :].rearrange("(a p) b -> p a b", p=128)


def build_program(debug=DEBUG):
    nc = bass.Bass("TRN2", target_bir_lowering=False, detect_race_conditions=False)

    def din(name, shape, dt=F32):
        return nc.dram_tensor(name, list(shape), dt, kind="ExternalInput").ap()

    x_in = din("x_loc", [S, H])
    w_in = din("W_in", [H, 2 * C])
    conv_wT = din("conv_wT", [C, 4])
    conv_b = din("conv_b", [C, 1])
    w_x = din("W_x", [C, 96])
    w_dt = din("W_dt", [64, C])
    b_dt = din("b_dt", [1, C])
    a_log = din("A_log", [C, D])
    d_vec = din("D_vec", [C, 1])
    w_out = din("W_out", [C, H])
    gwT = din("gwT", [H, E])
    e_bias = din("e_bias", [1, E])
    w1T = din("w1T", [EO, H, F])
    w2T = din("w2T", [EO, F, H])
    w3T = din("w3T", [EO, H, F])
    sgT = din("sgT", [H, F2])
    suT = din("suT", [H, F2])
    sdT = din("sdT", [F2, H])
    coremask = din("coremask", [8, 1])
    tokbase = din("tokbase", [1, 1])

    out_ap = nc.dram_tensor("out_loc", [S, H], F32, kind="ExternalOutput").ap()

    def dint(name, shape, dt=F32, dbg=False, shared=False):
        kind = "ExternalOutput" if (debug and dbg) else "Internal"
        kw = {"addr_space": "Shared"} if shared else {}
        return nc.dram_tensor(name, list(shape), dt, kind=kind, **kw).ap()

    win_bfh = dint("win_bfh", [H, 2 * C], BF16)
    wout_bfh = dint("wout_bfh", [C, H], BF16)
    x2_sp = dint("x2_sp", [S, H], F32, dbg=True)
    xln2_sp = dint("xln2_sp", [S, H], BF16)
    xln2_db = dint("xln2_db", [S, H], BF16, dbg=True) if debug else None
    xln2T_sp = dint("xln2T_sp", [H, S], BF16)
    logits_db = dint("logits_db", [S, E], F32, dbg=True)
    x3_sp = dint("x3_sp", [S, H], F32, dbg=True)
    cnt_in = dint("cnt_in", [1, E], F32)
    cnt_out = dint("cnt_out", [8, E], F32)
    slab_in = dint("slab_in", [E * CAP, 2], F32)
    slab_out = dint("slab_out", [EO * CAP, 2], F32)
    xg_out = dint("xg_out", [TG, H], BF16, shared=True)
    ye_loc = dint("ye_loc", [YBLK, H], BF16)
    ye_out = dint("ye_out", [YROWS, H], BF16, shared=True)
    ind_db = dint("ind_db", [S, E], F32, dbg=True)
    w8_db = dint("w8_db", [S, E], F32, dbg=True)
    pos_db = dint("pos_db", [S, E], F32, dbg=True)

    pb = [nc.psum_tensor(f"pb{i}", [128, 512], F32).__enter__().ap() for i in range(8)]
    ptr = [pb[6].bitcast(BF16), pb[7].bitcast(BF16)]

    ssm_p = nc.alloc_sbuf_tensor("ssm_p", [96, SC], F32).ap()
    bc_B = nc.alloc_sbuf_tensor("bc_B", [16, SC], BF16).ap()
    bc_C = nc.alloc_sbuf_tensor("bc_C", [16, SC], BF16).ap()
    ts_aug = nc.alloc_sbuf_tensor("ts_aug", [65, SC], BF16).ap()
    wdt_bf = nc.alloc_sbuf_tensor("wdt_bf", [65, C], BF16).ap()
    oh16 = nc.alloc_sbuf_tensor("oh16", [16, D * 128], BF16).ap()
    lg_ev = nc.alloc_sbuf_tensor("lg_ev", [64, SC], F32).ap()

    AR = Arena(nc, 158 * 1024)
    sigma = AR.alloc([128, CT * D], F32, "sigma")
    logits = AR.alloc([128, SB, E], F32, "logits")
    a_sb = AR.alloc([128, CT * D], F32, "a_sb")
    cwT_sb = AR.alloc([128, CT * 4], F32, "cwT")
    cb_sb = AR.alloc([128, CT], F32, "cb")
    dv_sb = AR.alloc([128, CT], F32, "dv")
    ident_bf = AR.alloc([128, 128], BF16, "identb")
    ident_f = AR.alloc([128, 128], F32, "identf")
    lnst = AR.alloc([128, 8], F32, "lnst")

    P = Prog(nc)
    for s in ("V", "Sc", "T", "Pg", "CC"):
        P.sem(s)

    for ct in range(CT):
        P.dma("sync", a_sb[:, ct * D:(ct + 1) * D], a_log[ct * 128:(ct + 1) * 128, :], "L")
        P.dma("sync", cwT_sb[:, ct * 4:(ct + 1) * 4], conv_wT[ct * 128:(ct + 1) * 128, :], "L")
        P.dma("sync", cb_sb[:, ct:ct + 1], conv_b[ct * 128:(ct + 1) * 128, :], "L")
        P.dma("sync", dv_sb[:, ct:ct + 1], d_vec[ct * 128:(ct + 1) * 128, :], "L")
    P.dma("gpsimd", wdt_bf[0:64, :], w_dt[:, :], "G")
    P.dma("gpsimd", wdt_bf[64:65, :], b_dt[:, :], "G")
    P.wall("scalar", "L")
    P.op("scalar", lambda e: e.activation(out=a_sb[:], in_=a_sb[:], func=AF.Exp), "Sc")
    P.wait("vector", "Sc")
    P.op("vector", lambda e: e.tensor_scalar_mul(a_sb[:], a_sb[:], -1.0), "V")
    P.op("gpsimd", lambda e: e.iota(ident_f[:], pattern=[[-1, 128]], base=0,
                                    channel_multiplier=1,
                                    allow_small_or_imprecise_dtypes=True), "Pg")
    P.op("gpsimd", lambda e: e.iota(oh16[:], pattern=[[-1, D], [0, 128]], base=0,
                                    channel_multiplier=1,
                                    allow_small_or_imprecise_dtypes=True), "Pg")
    P.wait("vector", "Pg")
    P.op("vector", lambda e: e.tensor_scalar(out=ident_bf[:], in0=ident_f[:],
                                             scalar1=0.0, scalar2=None, op0=ALU.is_equal), "V")
    P.op("vector", lambda e: e.tensor_scalar(out=oh16[:], in0=oh16[:],
                                             scalar1=0.0, scalar2=None, op0=ALU.is_equal), "V")
    P.op("vector", lambda e: e.tensor_scalar(out=ident_f[:], in0=ident_f[:],
                                             scalar1=0.0, scalar2=None, op0=ALU.is_equal), "V")

    AR.push()
    wx_sb = AR.alloc([128, CT, 96], BF16, "wx_sb")
    wstr = [AR.alloc([128, 8, 128], BF16, "wstr0"), AR.alloc([128, 8, 128], BF16, "wstr1")]
    xreg_off = AR.mark()
    x_t = AR.alloc([128, 2, H], F32, "x_t")
    bounce = AR.alloc([128, 4096], BF16, "bounce_alias", at=xreg_off)
    xlreg_off = AR.mark()
    xln_bf = AR.alloc([128, 4, H], BF16, "xln")
    xln2_t = AR.alloc([128, 4, H], BF16, "xln2t", at=xlreg_off)
    streg_off = AR.mark()
    xlnT = AR.alloc([128, 8, SC], BF16, "xlnT")
    mo_bf = AR.alloc([128, 8, SC], BF16, "mo", at=streg_off)
    xlnTb = AR.alloc([128, 8, SC], BF16, "xlnTb", at=streg_off)
    padded = AR.alloc([128, SC + 4], F32, "padded")
    hbuf = AR.alloc([128, CT, SC], BF16, "hbuf")
    hbf = [AR.alloc([128, SC], BF16, "hbf0"), AR.alloc([128, SC], BF16, "hbf1")]
    sg_all = AR.alloc([128, CT, SC], BF16, "sg_all")
    dt_all = AR.alloc([128, CT, SC], BF16, "dt_all")
    hist = AR.alloc([128, CT, 3], F32, "hist")
    bbc_off = AR.mark()
    bbc = AR.alloc([128, D, SC], BF16, "bbc")
    cbc_off = AR.mark()
    cbc = AR.alloc([128, D, SC], BF16, "cbc")
    state = AR.alloc([128, D, SC], BF16, "state")
    da_t = AR.alloc([128, SC], BF16, "da")
    da_t2 = AR.alloc([128, SC], BF16, "da2")
    da_last = {}
    dbu_t = AR.alloc([128, SC], BF16, "dbu")
    u_t = AR.alloc([128, SC], F32, "u_t")
    spl_t = AR.alloc([128, SC], F32, "spl_t")
    yacc = AR.alloc([128, SC], F32, "yacc")
    zred = AR.alloc([128, SC], F32, "zred")
    yfin = AR.alloc([128, SC], BF16, "yfin")
    gwT_sb = AR.alloc([128, 8, E], F32, "gwTsb")
    xln2f = AR.alloc([128, H], F32, "xln2f", at=cbc_off)
    xT_f = AR.alloc([128, 8, 128], F32, "xTf", at=bbc_off)

    for hk in range(8):
        t = P.dma("gpsimd", bounce[:], w_in[hk * 128:(hk + 1) * 128, :], "G")
        P.wtok("gpsimd", t)
        t = P.dma("gpsimd", win_bfh[hk * 128:(hk + 1) * 128, :], bounce[:], "G")
        P.wtok("gpsimd", t)
    for ck in range(CT):
        t = P.dma("gpsimd", bounce[:, 0:H], w_out[ck * 128:(ck + 1) * 128, :], "G")
        P.wtok("gpsimd", t)
        t = P.dma("gpsimd", wout_bfh[ck * 128:(ck + 1) * 128, :], bounce[:, 0:H], "G")
        P.wtok("gpsimd", t)
    for ct in range(CT):
        P.dma("gpsimd", wx_sb[:, ct, :], w_x[ct * 128:(ct + 1) * 128, :], "G")
    wPrepSnap = {n: P.ctr[n] for n in P.pools["G"]["names"]}

    def wait_prep(eng):
        for n, v in wPrepSnap.items():
            P.wait(eng, n, v)

    for hk in range(8):
        P.dma("sync", gwT_sb[:, hk, :], gwT[hk * 128:(hk + 1) * 128, :], "L")
    P.op("vector", lambda e: e.memset(sigma[:], 0.0), "V")

    def ln_norm(src_ap, out_ap, out2_ap=None):
        P.op("vector", lambda e: e.reduce_sum(lnst[:, 2:3], src_ap, axis=AX.X), "V", fence=True)
        P.wait("scalar", "V")
        P.op("scalar", lambda e: e.activation(out=zred[:], in_=src_ap[:, 0:SC],
                                              func=AF.Square, accum_out=lnst[:, 0:1]), "Sc", fence=True)
        P.op("scalar", lambda e: e.activation(out=spl_t[:], in_=src_ap[:, SC:H],
                                              func=AF.Square, accum_out=lnst[:, 1:2]), "Sc", fence=True)
        P.wait("vector", "Sc")
        P.op("vector", lambda e: e.tensor_tensor(out=lnst[:, 3:4], in0=lnst[:, 0:1],
                                                 in1=lnst[:, 1:2], op=ALU.add), "V", fence=True)
        P.op("vector", lambda e: e.tensor_scalar_mul(lnst[:, 2:3], lnst[:, 2:3], 1.0 / H), "V", fence=True)
        P.op("vector", lambda e: e.tensor_scalar_mul(lnst[:, 3:4], lnst[:, 3:4], 1.0 / H), "V", fence=True)
        P.op("vector", lambda e: e.tensor_tensor(out=lnst[:, 4:5], in0=lnst[:, 2:3],
                                                 in1=lnst[:, 2:3], op=ALU.mult), "V", fence=True)
        P.op("vector", lambda e: e.tensor_tensor(out=lnst[:, 5:6], in0=lnst[:, 3:4],
                                                 in1=lnst[:, 4:5], op=ALU.subtract), "V", fence=True)
        P.op("vector", lambda e: e.tensor_scalar_add(lnst[:, 5:6], lnst[:, 5:6], 1e-5), "V", fence=True)
        P.wait("scalar", "V")
        P.op("scalar", lambda e: e.activation(out=lnst[:, 6:7], in_=lnst[:, 5:6],
                                              func=AF.Sqrt), "Sc", fence=True)
        P.wait("vector", "Sc")
        P.op("vector", lambda e: e.reciprocal(lnst[:, 7:8], lnst[:, 6:7]), "V", fence=True)
        P.op("vector", lambda e: e.tensor_scalar(
            out=out_ap, in0=src_ap, scalar1=lnst[:, 2:3], scalar2=lnst[:, 7:8],
            op0=ALU.subtract, op1=ALU.mult), "V", fence=True)
        if out2_ap is not None:
            P.op("vector", lambda e: e.tensor_copy(out=out2_ap, in_=out_ap), "V", fence=True)

    last_wstr_mm = 0
    last_x2_store = None
    for i in range(NCH):
        s0 = i * SC
        P.wall("vector", "G")
        for half in range(2):
            P.wait("sync", "V")
            P.wait("sync", "Sc")
            P.wtok("sync", last_x2_store)
            if i == 0 and half == 0:
                P.wall("sync", "G")
            wX = P.dma("sync", x_t[:],
                       tokrows(x_in, s0 + half * 256, s0 + (half + 1) * 256), "L")
            P.wtok("vector", wX)
            for sb in range(2):
                ln_norm(x_t[:, sb, :], xln_bf[:, half * 2 + sb, :])
        wLN = P.ctr["V"]
        P.wait("tensor", "V", wLN)
        xev = []
        for hb in range(8):
            pt = ptr[hb % 2]
            if hb >= 2:
                P.wait("tensor", "V", xev[hb - 2])
            for sb in range(4):
                P.op("tensor", lambda e, sb=sb, hb=hb, pt=pt: e.transpose(
                    pt[:, sb * 128:(sb + 1) * 128],
                    xln_bf[:, sb, hb * 128:(hb + 1) * 128], ident_bf[:]), "T")
            wT = P.ctr["T"]
            P.wait("vector", "T", wT)
            P.op("vector", lambda e, hb=hb, pt=pt: e.tensor_copy(
                out=xlnT[:, hb, :], in_=pt[:, 0:SC]), "V")
            xev.append(P.ctr["V"])
        wXT = P.ctr["V"]

        if STAGE < 2:
            continue
        P.wait("tensor", "V", wXT)
        P.wait("sync", "T", last_wstr_mm)
        if i == 0:
            P.wall("sync", "G")
        tok_w = {}
        tok_w[0] = P.dma("sync", wstr[0][:],
                         win_bfh[:, 0:128].rearrange("(a p) b -> p a b", p=128), "L")
        evac = {}
        winmm = {}
        for iob in range(32):
            if iob + 1 < 32:
                if iob >= 1:
                    P.wait("sync", "T", winmm[iob - 1])
                tok_w[iob + 1] = P.dma("sync", wstr[(iob + 1) % 2][:],
                      win_bfh[:, (iob + 1) * 128:(iob + 2) * 128].rearrange(
                          "(a p) b -> p a b", p=128), "L")
            pbk = pb[iob % 2]
            P.wtok("tensor", tok_w[iob])
            if iob >= 2:
                w, engsem = evac[iob - 2]
                P.wait("tensor", engsem, w)
            for hk in range(8):
                P.op("tensor", lambda e, hk=hk, iob=iob, pbk=pbk: e.matmul(
                    pbk[:, 0:SC], wstr[iob % 2][:, hk, :], xlnT[:, hk, :],
                    start=(hk == 0), stop=(hk == 7)), "T")
            wMM = winmm[iob] = P.ctr["T"]
            ct = iob % 16
            if iob < 16:
                P.wait("vector", "T", wMM)
                P.op("vector", lambda e, pbk=pbk: e.tensor_copy(
                    out=padded[:, 4:4 + SC], in_=pbk[:, 0:SC]), "V")
                if i == 0:
                    P.op("vector", lambda e: e.memset(padded[:, 1:4], 0.0), "V")
                else:
                    P.op("vector", lambda e, ct=ct: e.tensor_copy(
                        out=padded[:, 1:4], in_=hist[:, ct, :]), "V")
                P.wait("vector", "Sc")
                cbufs = [u_t, spl_t]
                P.op("vector", lambda e, ct=ct: e.tensor_scalar(
                    out=u_t[:], in0=padded[:, 1:1 + SC],
                    scalar1=cwT_sb[:, ct * 4:ct * 4 + 1], scalar2=None, op0=ALU.mult), "V")
                for k in range(1, 4):
                    P.op("vector", lambda e, ct=ct, k=k, cbufs=cbufs: e.scalar_tensor_tensor(
                        out=cbufs[k % 2][:], in0=padded[:, 1 + k:1 + k + SC],
                        scalar=cwT_sb[:, ct * 4 + k:ct * 4 + k + 1],
                        in1=cbufs[(k + 1) % 2][:], op0=ALU.mult, op1=ALU.add), "V")
                P.op("vector", lambda e, ct=ct: e.tensor_copy(
                    out=hist[:, ct, :], in_=padded[:, SC + 1:SC + 4]), "V")
                wConv = P.ctr["V"]
                evac[iob] = (wConv, "V")
                P.wait("scalar", "V", wConv)
                P.op("scalar", lambda e, ct=ct: e.activation(
                    out=hbuf[:, ct, :], in_=spl_t[:], func=AF.Silu,
                    bias=cb_sb[:, ct:ct + 1]), "Sc")
            else:
                P.wait("scalar", "T", wMM)
                P.op("scalar", lambda e, ct=ct, pbk=pbk: e.activation(
                    out=sg_all[:, ct, :], in_=pbk[:, 0:SC], func=AF.Silu), "Sc")
                evac[iob] = (P.ctr["Sc"], "Sc")
        last_win_mm = winmm[31]
        wSiluAll = P.ctr["Sc"]

        if STAGE < 3:
            continue
        P.wait("tensor", "Sc", wSiluAll)
        if i == 0:
            wait_prep("tensor")
        wx_mm = {}
        for ct in range(CT):
            P.op("tensor", lambda e, ct=ct: e.matmul(
                pb[2][0:96, 0:SC], wx_sb[:, ct, :], hbuf[:, ct, :],
                start=(ct == 0), stop=(ct == CT - 1)), "T")
            wx_mm[ct] = P.ctr["T"]
        wWX = P.ctr["T"]
        P.wait("vector", "T", wWX)
        P.op("vector", lambda e: e.tensor_copy(out=ssm_p[:, :], in_=pb[2][0:96, 0:SC]), "V")
        wSP = P.ctr["V"]
        P.wait("gpsimd", "V", wSP)
        tok_bcB = P.dma("gpsimd", bc_B[:, :], ssm_p[64:80, :], "G")
        tok_bcC = P.dma("gpsimd", bc_C[:, :], ssm_p[80:96, :], "G")
        P.op("vector", lambda e: e.tensor_copy(out=ts_aug[0:64, :], in_=ssm_p[0:64, :]), "V")
        P.op("vector", lambda e: e.memset(ts_aug[64:65, :], 1.0), "V")
        wTS = P.ctr["V"]

        P.wait("tensor", "V", wTS)
        dtev = {}
        for cb in range(CT):
            pbk = pb[cb % 2]
            if cb >= 2:
                P.wait("tensor", "Sc", dtev[cb - 2])
            P.op("tensor", lambda e, cb=cb, pbk=pbk: e.matmul(
                pbk[:, 0:SC], wdt_bf[:, cb * 128:(cb + 1) * 128], ts_aug[:],
                start=True, stop=True), "T")
            wDT = P.ctr["T"]
            P.wait("scalar", "T", wDT)
            P.op("scalar", lambda e, pbk=pbk: e.activation(
                out=spl_t[:], in_=pbk[:, 0:SC], func=AF.Exp), "Sc")
            P.op("scalar", lambda e: e.add(out=spl_t[:], in_=spl_t[:], add=1.0), "Sc")
            P.op("scalar", lambda e, cb=cb: e.activation(
                out=dt_all[:, cb, :], in_=spl_t[:], func=AF.Ln), "Sc")
            dtev[cb] = P.ctr["Sc"]
        wDTall = P.ctr["Sc"]

        P.wtok("tensor", tok_bcB)
        P.wtok("tensor", tok_bcC)
        bcev = {}
        for d2 in range(2 * D):
            pbk = pb[3 + (d2 % 2)]
            if d2 >= 2:
                P.wait("tensor", "V", bcev[d2 - 2])
            dd = d2 % D
            src = bc_B if d2 < D else bc_C
            P.op("tensor", lambda e, dd=dd, src=src, pbk=pbk: e.matmul(
                pbk[:, 0:SC], oh16[:, dd * 128:(dd + 1) * 128], src[:, :],
                start=True, stop=True), "T")
            wB = P.ctr["T"]
            P.wait("vector", "T", wB)
            dst = bbc if d2 < D else cbc
            P.op("vector", lambda e, dst=dst, dd=dd, pbk=pbk: e.tensor_copy(
                out=dst[:, dd, :], in_=pbk[:, 0:SC]), "V")
            bcev[d2] = P.ctr["V"]
        wBC = P.ctr["V"]

        if STAGE < 4:
            continue
        P.wait("vector", "Sc", wDTall)
        P.wait("sync", "T", last_win_mm)
        tok_w = {0: P.dma("sync", wstr[0][:],
                          wout_bfh[0:128, :].rearrange("p (a b) -> p a b", a=8), "L")}
        woct = {}
        for ct in range(CT):
            P.op("vector", lambda e, ct=ct: e.tensor_tensor(
                out=u_t[:], in0=dt_all[:, ct, :], in1=hbuf[:, ct, :], op=ALU.mult), "V")
            P.op("vector", lambda e, ct=ct: e.tensor_scalar(
                out=yacc[:], in0=hbuf[:, ct, :], scalar1=dv_sb[:, ct:ct + 1],
                scalar2=None, op0=ALU.mult), "V")
            for d in range(D):
                dab = da_t if d % 2 == 0 else da_t2
                prevscan = da_last.get(d % 2)
                if prevscan is None:
                    P.wait("scalar", "V")
                else:
                    P.wait("scalar", "V", prevscan)
                P.op("scalar", lambda e, ct=ct, d=d, dab=dab: e.activation(
                    out=dab[:], in_=dt_all[:, ct, :], func=AF.Exp,
                    scale=a_sb[:, ct * D + d:ct * D + d + 1]), "Sc")
                wdA = P.ctr["Sc"]
                P.op("vector", lambda e, d=d: e.tensor_tensor(
                    out=dbu_t[:], in0=u_t[:], in1=bbc[:, d, :], op=ALU.mult), "V")
                P.wait("vector", "Sc", wdA)
                P.op("vector", lambda e, ct=ct, d=d, dab=dab: e.tensor_tensor_scan(
                    out=state[:, d, :], data0=dab[:], data1=dbu_t[:],
                    initial=sigma[:, ct * D + d:ct * D + d + 1],
                    op0=ALU.mult, op1=ALU.add), "V")
                da_last[d % 2] = P.ctr["V"]
            P.op("vector", lambda e, ct=ct: e.tensor_copy(
                out=sigma[:, ct * D:(ct + 1) * D], in_=state[:, :, SC - 1]), "V",
                 fence=True)
            for d in range(D):
                P.op("vector", lambda e, d=d: e.tensor_tensor(
                    out=state[:, d, :], in0=state[:, d, :], in1=cbc[:, d, :],
                    op=ALU.mult), "V")
            P.op("vector", lambda e: e.tensor_reduce(
                out=zred[:], in_=state[:].rearrange("p a b -> p b a"),
                axis=AX.X, op=ALU.add), "V")
            P.op("vector", lambda e: e.tensor_add(yacc[:], yacc[:], zred[:]), "V")
            if ct >= 1:
                P.wait("vector", "T", woct[ct - 1])
            P.op("vector", lambda e, ct=ct: e.tensor_tensor(
                out=yfin[:], in0=yacc[:], in1=sg_all[:, ct, :], op=ALU.mult), "V")
            wYF = P.ctr["V"]
            P.wait("tensor", "V", wYF)
            P.wtok("tensor", tok_w[ct])
            for ob in range(8):
                P.op("tensor", lambda e, ct=ct, ob=ob: e.matmul(
                    pb[ob][:, 0:SC], wstr[ct % 2][:, ob, :], yfin[:],
                    start=(ct == 0), stop=(ct == CT - 1), skip_group_check=True), "T")
            woct[ct] = P.ctr["T"]
            if ct + 1 < CT:
                if ct >= 1:
                    P.wait("sync", "T", woct[ct - 1])
                tok_w[ct + 1] = P.dma("sync", wstr[(ct + 1) % 2][:],
                      wout_bfh[(ct + 1) * 128:(ct + 2) * 128, :].rearrange(
                          "p (a b) -> p a b", a=8), "L")
        last_wstr_mm = woct[CT - 1]
        wWOmm = P.ctr["T"]

        if STAGE < 5:
            continue
        P.wait("vector", "T", wWOmm)
        for ob in range(8):
            P.op("vector", lambda e, ob=ob: e.tensor_copy(
                out=mo_bf[:, ob, :], in_=pb[ob][:, 0:SC]), "V")
        wMO = P.ctr["V"]
        for half in range(2):
            P.wait("sync", "V")
            P.wait("sync", "Sc")
            P.wtok("sync", last_x2_store)
            wXR = P.dma("sync", x_t[:],
                        tokrows(x_in, s0 + half * 256, s0 + (half + 1) * 256), "L")
            P.wait("tensor", "V", wMO)
            P.wtok("vector", wXR)
            for sb2 in range(2):
                sb = half * 2 + sb2
                pt = ptr[sb2 % 2]
                if half + sb2 > 0:
                    P.wait("tensor", "V")
                for ob in range(8):
                    P.op("tensor", lambda e, sb=sb, ob=ob, pt=pt: e.transpose(
                        pt[:, ob * 128:(ob + 1) * 128],
                        mo_bf[:, ob, sb * 128:(sb + 1) * 128], ident_bf[:]), "T")
                wT = P.ctr["T"]
                P.wait("vector", "T", wT)
                P.op("vector", lambda e, sb2=sb2, pt=pt: e.tensor_tensor(
                    out=x_t[:, sb2, :], in0=x_t[:, sb2, :], in1=pt[:, 0:H], op=ALU.add), "V")
            wX2h = P.ctr["V"]
            P.wait("gpsimd", "V", wX2h)
            last_x2_store = P.dma(
                "gpsimd", tokrows(x2_sp, s0 + half * 256, s0 + (half + 1) * 256),
                x_t[:], "G")
            for sb2 in range(2):
                sb = half * 2 + sb2
                ln_norm(x_t[:, sb2, :], xln2f[:], out2_ap=xln2_t[:, sb, :])
                wXF = P.ctr["V"]
                P.wait("tensor", "V", wXF)
                rtev = {}
                for hb in range(8):
                    pbs = pb[4 + (hb % 2)]
                    if hb >= 2:
                        P.wait("tensor", "V", rtev[hb - 2])
                    P.op("tensor", lambda e, hb=hb, pbs=pbs: e.transpose(
                        pbs[:, 0:128], xln2f[:, hb * 128:(hb + 1) * 128], ident_f[:]), "T")
                    wT = P.ctr["T"]
                    P.wait("vector", "T", wT)
                    P.op("vector", lambda e, hb=hb, pbs=pbs: e.tensor_copy(
                        out=xT_f[:, hb, :], in_=pbs[:, 0:128]), "V")
                    rtev[hb] = P.ctr["V"]
                wXTF = P.ctr["V"]
                P.wait("tensor", "V", wXTF)
                for hb in range(8):
                    P.op("tensor", lambda e, hb=hb: e.matmul(
                        pb[3][0:E, 0:128], gwT_sb[:, hb, :], xT_f[:, hb, :],
                        start=(hb == 0), stop=(hb == 7)), "T")
                wRT = P.ctr["T"]
                P.wait("vector", "T", wRT)
                P.op("vector", lambda e, sb=sb: e.tensor_copy(
                    out=lg_ev[:, sb * 128:(sb + 1) * 128], in_=pb[3][0:E, 0:128]), "V")
        wLG = P.ctr["V"]
        P.wait("tensor", "V", wLG)
        lgev2 = {}
        for sb in range(4):
            pbs = pb[4 + (sb % 2)]
            if sb >= 2:
                P.wait("tensor", "V", lgev2[sb - 2])
            P.op("tensor", lambda e, sb=sb, pbs=pbs: e.transpose(
                pbs[0:128, 0:64], lg_ev[0:E, sb * 128:(sb + 1) * 128],
                ident_f[0:64, 0:64]), "T")
            wT = P.ctr["T"]
            P.wait("vector", "T", wT)
            gsb = i * 4 + sb
            P.op("vector", lambda e, gsb=gsb, pbs=pbs: e.tensor_copy(
                out=logits[:, gsb, :], in_=pbs[:, 0:64]), "V")
            lgev2[sb] = P.ctr["V"]
        wXL = P.ctr["V"]
        P.wait("gpsimd", "V", wXL)
        P.dma("gpsimd", tokrows(xln2_sp, s0, s0 + SC), xln2_t[:], "G")
        P.wait("tensor", "V", wXL)
        xt2 = {}
        for hb in range(8):
            pt = ptr[hb % 2]
            if hb >= 2:
                P.wait("tensor", "V", xt2[hb - 2])
            for sb in range(4):
                P.op("tensor", lambda e, sb=sb, hb=hb, pt=pt: e.transpose(
                    pt[:, sb * 128:(sb + 1) * 128],
                    xln2_t[:, sb, hb * 128:(hb + 1) * 128], ident_bf[:]), "T")
            wT = P.ctr["T"]
            P.wait("vector", "T", wT)
            P.op("vector", lambda e, hb=hb, pt=pt: e.tensor_copy(
                out=xlnTb[:, hb, :], in_=pt[:, 0:SC]), "V")
            xt2[hb] = P.ctr["V"]
        wXT2 = P.ctr["V"]
        P.wait("gpsimd", "V", wXT2)
        P.dma("gpsimd", xln2T_sp[:, s0:s0 + SC].rearrange("(a p) b -> p a b", p=128),
              xlnTb[:], "G")

    dbg_refs = {}

    def _phase2():
        _cut(1)
        AR.pop()

        scu = AR.alloc([128, SB, 8], F32, "scu")
        ycu = AR.alloc([128, SB, 8], F32, "ycu")
        wk8 = AR.alloc([128, SB, 8], F32, "wk8")
        scu_u = AR.alloc([128, SB, 8], U32, "scu_u")
        ycu_u = AR.alloc([128, SB, 8], U32, "ycu_u")
        tokid = AR.alloc([128, SB], F32, "tokid")
        dbg_refs.update(scu=scu, ycu=ycu, wk8=wk8, tokid=tokid)
        tb1 = AR.alloc([128, 1], F32, "tb1")
        ebias_bc = AR.alloc([128, E], F32, "ebias_bc")
        e_bias_sb = nc.alloc_sbuf_tensor("e_bias_sb", [1, E], F32).ap()
        cm_sb = nc.alloc_sbuf_tensor("cm_sb", [8, 1], F32).ap()
        tbase_sb = nc.alloc_sbuf_tensor("tbase_sb", [1, 1], F32).ap()
        cnts_sb = nc.alloc_sbuf_tensor("cnts_sb", [8, E], F32).ap()
        ones1f = nc.alloc_sbuf_tensor("ones1f", [1, 128], F32).ap()
        cnt1 = nc.alloc_sbuf_tensor("cnt1", [1, E], F32).ap()

        AR.push()
        NHT = SB // 2
        indT_f = AR.alloc([128, S], F32, "indT")
        posT_f = AR.alloc([128, S], F32, "posT")
        keepT_f = AR.alloc([128, S], F32, "keepT")
        indT = indT_f[0:64, :]
        posT = posT_f[0:64, :]
        keepT = keepT_f[0:64, :]
        base64 = AR.alloc([128, 1], F32, "base64")
        scoresh = AR.alloc([128, NHT, E], F32, "scoresh")
        sfch = AR.alloc([128, NHT, E], F32, "sfch")
        tmpA = AR.alloc([128, NHT, E], F32, "tmpA")
        tmpB = AR.alloc([128, NHT, E], F32, "tmpB")
        indh = AR.alloc([128, NHT, E], F32, "indh")
        rkA = AR.alloc([128, NHT, E], F32, "rkA")
        rkB = AR.alloc([128, NHT, E], F32, "rkB")
        eqh = AR.alloc([128, NHT, E], F32, "eqh")
        ivh = AR.alloc([128, NHT, E], F32, "ivh")
        io64 = AR.alloc([128, NHT, E], F32, "io64")
        io64b = AR.alloc([128, NHT, E], F32, "io64b")
        io8 = AR.alloc([128, NHT, E], F32, "io8")
        io8b = AR.alloc([128, NHT, E], F32, "io8b")
        ebase = AR.alloc([128, NHT, E], F32, "ebase")
        ybase = AR.alloc([128, NHT, E], F32, "ybase")
        sidh = AR.alloc([128, NHT, E], F32, "sidh")
        wn_h0 = AR.alloc([128, NHT, E], F32, "wn_h0")
        rk_h0 = AR.alloc([128, NHT, E], F32, "rk_h0")
        posh = AR.alloc([128, NHT, E], F32, "posh")
        keeph = AR.alloc([128, NHT, E], F32, "keeph")
        wnh = AR.alloc([128, NHT, E], F32, "wnh")
        gmh = AR.alloc([128, NHT, 8], F32, "gmh")
        gm2h = AR.alloc([128, NHT, 8], F32, "gm2h")
        gsA = AR.alloc([128, NHT, 8], F32, "gsA")
        gsB = AR.alloc([128, NHT, 8], F32, "gsB")
        gmask = AR.alloc([128, NHT, 8], F32, "gmask")
        iog = AR.alloc([128, NHT, 8], F32, "iog")
        iogb = AR.alloc([128, NHT, 8], F32, "iogb")
        m32 = AR.alloc([128, NHT], F32, "m32")
        mi32 = AR.alloc([128, NHT], F32, "mi32")
        payk = AR.alloc([128, SB, 2], F32, "payk")
        ztile = AR.alloc([128, 512], F32, "ztile")

        def vop(fn, fence=True):
            return P.op("vector", fn, "V", fence=fence)

        for ap_, pat, base in (
            (io64, [[0, NHT], [1, E]], 0),
            (io64b, [[0, NHT], [1, E]], int(BIGIDX)),
            (io8, [[0, NHT], [0, 8], [1, 8]], 0),
            (io8b, [[0, NHT], [0, 8], [1, 8]], int(BIGIDX)),
            (iog, [[0, NHT], [1, 8]], 0),
            (iogb, [[0, NHT], [1, 8]], int(BIGIDX)),
            (ebase, [[0, NHT], [CAP, E]], 0),
            (ybase, [[0, NHT], [YBLK, 8], [CAP, 8]], 0),
        ):
            P.op("gpsimd", lambda e, ap_=ap_, pat=pat, base=base: e.iota(
                ap_[:], pattern=pat, base=base, channel_multiplier=0,
                allow_small_or_imprecise_dtypes=True), "Pg")
        P.op("gpsimd", lambda e: e.iota(tokid[:], pattern=[[128, SB]], base=0,
                                        channel_multiplier=1,
                                        allow_small_or_imprecise_dtypes=True), "Pg")
        wIotas = P.ctr["Pg"]

        tok_eb = P.dma("sync", e_bias_sb[:, :], e_bias[:, :], "L")
        tok_cm = P.dma("sync", cm_sb[:, :], coremask[:, :], "L")
        tok_tb = P.dma("sync", tbase_sb[:, :], tokbase[:, :], "L")

        P.op("vector", lambda e: e.memset(ones1f[:], 1.0), "V")
        wOnes = P.ctr["V"]
        P.wtok("tensor", tok_eb)
        P.wtok("tensor", tok_tb)
        P.wait("tensor", "V", wOnes)
        P.op("tensor", lambda e: e.matmul(pb[4][:, 0:E], ones1f[:], e_bias_sb[:, :],
                                          start=True, stop=True), "T")
        P.op("tensor", lambda e: e.matmul(pb[5][:, 0:1], ones1f[:], tbase_sb[:, :],
                                          start=True, stop=True), "T")
        wEB = P.ctr["T"]
        P.wait("vector", "T", wEB)
        vop(lambda e: e.tensor_copy(out=ebias_bc[:], in_=pb[4][:, 0:E]))
        vop(lambda e: e.tensor_copy(out=tb1[:], in_=pb[5][:, 0:1]))
        P.wait("vector", "Pg", wIotas)
        vop(lambda e: e.tensor_scalar(out=tokid[:], in0=tokid[:], scalar1=tb1[:, 0:1],
                                      scalar2=None, op0=ALU.add))

        rk_ref = {}
        wn_ref = {}
        for h in range(2):
            lgv = logits[:, h * NHT:(h + 1) * NHT, :]
            P.wait("scalar", "V")
            P.op("scalar", lambda e, lgv=lgv: e.activation(out=scoresh[:], in_=lgv,
                                                           func=AF.Sigmoid), "Sc")
            P.wait("vector", "Sc")
            vop(lambda e: e.tensor_tensor(
                out=sfch[:], in0=scoresh[:],
                in1=ebias_bc[:, None, :].to_broadcast([128, NHT, E]),
                op=ALU.add))
            vop(lambda e: e.tensor_reduce(
                out=gmh[:], in_=sfch[:].rearrange("p a (g f) -> p a g f", g=8),
                axis=AX.X, op=ALU.max))
            vop(lambda e: e.tensor_tensor(
                out=eqh[:].rearrange("p a (g f) -> p a g f", g=8),
                in0=sfch[:].rearrange("p a (g f) -> p a g f", g=8),
                in1=gmh[:, :, :, None].to_broadcast([128, NHT, 8, 8]),
                op=ALU.is_equal))
            vop(lambda e: e.scalar_tensor_tensor(
                out=ivh[:], in0=eqh[:], scalar=-BIGIDX, in1=io8b[:],
                op0=ALU.mult, op1=ALU.add))
            vop(lambda e: e.tensor_reduce(
                out=gm2h[:], in_=ivh[:].rearrange("p a (g f) -> p a g f", g=8),
                axis=AX.X, op=ALU.min))
            vop(lambda e: e.tensor_tensor(
                out=eqh[:].rearrange("p a (g f) -> p a g f", g=8),
                in0=io8[:].rearrange("p a (g f) -> p a g f", g=8),
                in1=gm2h[:, :, :, None].to_broadcast([128, NHT, 8, 8]),
                op=ALU.is_equal))
            vop(lambda e: e.scalar_tensor_tensor(
                out=tmpA[:], in0=eqh[:], scalar=-BIGF, in1=sfch[:],
                op0=ALU.mult, op1=ALU.add))
            vop(lambda e: e.tensor_reduce(
                out=gm2h[:], in_=tmpA[:].rearrange("p a (g f) -> p a g f", g=8),
                axis=AX.X, op=ALU.max))
            vop(lambda e: e.tensor_tensor(out=gsA[:], in0=gmh[:], in1=gm2h[:], op=ALU.add))
            vop(lambda e: e.memset(gmask[:], 0.0))
            cur, nxt = gsA, gsB
            for r4 in range(4):
                vop(lambda e, cur=cur: e.tensor_reduce(out=m32[:], in_=cur[:], axis=AX.X,
                                                       op=ALU.max))
                vop(lambda e, cur=cur: e.tensor_tensor(
                    out=gm2h[:], in0=cur[:],
                    in1=m32[:, :, None].to_broadcast([128, NHT, 8]),
                    op=ALU.is_equal))
                vop(lambda e: e.scalar_tensor_tensor(
                    out=gm2h[:], in0=gm2h[:], scalar=-BIGIDX, in1=iogb[:],
                    op0=ALU.mult, op1=ALU.add))
                vop(lambda e: e.tensor_reduce(out=mi32[:], in_=gm2h[:], axis=AX.X,
                                              op=ALU.min))
                vop(lambda e: e.tensor_tensor(
                    out=gm2h[:], in0=iog[:],
                    in1=mi32[:, :, None].to_broadcast([128, NHT, 8]),
                    op=ALU.is_equal))
                vop(lambda e: e.tensor_add(gmask[:], gmask[:], gm2h[:]))
                vop(lambda e, cur=cur, nxt=nxt: e.scalar_tensor_tensor(
                    out=nxt[:], in0=gm2h[:], scalar=-BIGF, in1=cur[:],
                    op0=ALU.mult, op1=ALU.add))
                cur, nxt = nxt, cur
            vop(lambda e: e.tensor_tensor(
                out=tmpA[:].rearrange("p a (g f) -> p a g f", g=8),
                in0=sfch[:].rearrange("p a (g f) -> p a g f", g=8),
                in1=gmask[:, :, :, None].to_broadcast([128, NHT, 8, 8]),
                op=ALU.mult))
            vop(lambda e: e.memset(rkA[:], BIGIDX))
            vop(lambda e: e.memset(indh[:], 0.0))
            curt, nxtt = tmpA, tmpB
            curr, nxtr = rkA, rkB
            for k in range(8):
                vop(lambda e, curt=curt: e.tensor_reduce(out=m32[:], in_=curt[:],
                                                         axis=AX.X, op=ALU.max))
                vop(lambda e, curt=curt: e.tensor_tensor(
                    out=eqh[:], in0=curt[:],
                    in1=m32[:, :, None].to_broadcast([128, NHT, E]),
                    op=ALU.is_equal))
                vop(lambda e: e.scalar_tensor_tensor(
                    out=ivh[:], in0=eqh[:], scalar=-BIGIDX, in1=io64b[:],
                    op0=ALU.mult, op1=ALU.add))
                vop(lambda e: e.tensor_reduce(out=mi32[:], in_=ivh[:], axis=AX.X,
                                              op=ALU.min))
                vop(lambda e: e.tensor_tensor(
                    out=eqh[:], in0=io64[:],
                    in1=mi32[:, :, None].to_broadcast([128, NHT, E]),
                    op=ALU.is_equal))
                vop(lambda e: e.tensor_add(indh[:], indh[:], eqh[:]))
                vop(lambda e, k=k: e.tensor_scalar(
                    out=ivh[:], in0=eqh[:], scalar1=-BIGIDX, scalar2=float(k) + BIGIDX,
                    op0=ALU.mult, op1=ALU.add))
                vop(lambda e, curr=curr, nxtr=nxtr: e.tensor_tensor(
                    out=nxtr[:], in0=curr[:], in1=ivh[:], op=ALU.min))
                curr, nxtr = nxtr, curr
                vop(lambda e, curt=curt, nxtt=nxtt: e.scalar_tensor_tensor(
                    out=nxtt[:], in0=eqh[:], scalar=-BIGF, in1=curt[:],
                    op0=ALU.mult, op1=ALU.add))
                curt, nxtt = nxtt, curt
            vop(lambda e: e.tensor_tensor(out=wnh[:], in0=scoresh[:], in1=indh[:],
                                          op=ALU.mult))
            vop(lambda e: e.tensor_reduce(out=m32[:], in_=wnh[:], axis=AX.X, op=ALU.add))
            vop(lambda e: e.tensor_scalar_add(m32[:], m32[:], 1e-20))
            vop(lambda e: e.reciprocal(mi32[:], m32[:]))
            vop(lambda e: e.tensor_scalar_mul(mi32[:], mi32[:], 2.5))
            vop(lambda e: e.tensor_tensor(
                out=wnh[:], in0=wnh[:],
                in1=mi32[:, :, None].to_broadcast([128, NHT, E]),
                op=ALU.mult))
            wRouteH = P.ctr["V"]
            P.wait("tensor", "V", wRouteH)
            trev = {}
            for sbl in range(NHT):
                gsb = h * NHT + sbl
                pbs = pb[4 + (sbl % 2)]
                if sbl >= 2:
                    P.wait("tensor", "V", trev[sbl - 2])
                P.op("tensor", lambda e, sbl=sbl, pbs=pbs: e.transpose(
                    pbs[0:E, 0:128], indh[:, sbl, :], ident_f[:]), "T")
                wT = P.ctr["T"]
                P.wait("vector", "T", wT)
                P.op("vector", lambda e, gsb=gsb, pbs=pbs: e.tensor_copy(
                    out=indT[:, gsb * 128:(gsb + 1) * 128], in_=pbs[0:E, 0:128]), "V")
                trev[sbl] = P.ctr["V"]
            if h == 0:
                vop(lambda e: e.tensor_copy(out=wn_h0[:], in_=wnh[:]))
                vop(lambda e, curr=curr: e.tensor_copy(out=rk_h0[:], in_=curr[:]))
                rk_ref[0] = rk_h0
                wn_ref[0] = wn_h0
            else:
                rk_ref[1] = curr
                wn_ref[1] = wnh
        wRouteAll = P.ctr["V"]

        P.wait("vector", "V", wRouteAll)
        vop(lambda e: e.tensor_tensor_scan(
            out=posT[:], data0=indT[:], data1=indT[:], initial=0.0,
            op0=ALU.add, op1=ALU.bypass))
        wCnt = P.ctr["V"]
        P.wait("tensor", "V", wCnt)
        P.op("tensor", lambda e: e.transpose(pb[4][0:1, 0:E], posT[:, S - 1:S],
                                             ident_f[0:64, 0:64]), "T")
        wCntT = P.ctr["T"]
        P.wait("vector", "T", wCntT)
        vop(lambda e: e.tensor_copy(out=cnt1[:, :], in_=pb[4][0:1, 0:E]))
        wCnt1 = P.ctr["V"]
        vop(lambda e: e.tensor_tensor(out=posT[:], in0=posT[:], in1=indT[:],
                                      op=ALU.subtract))
        wPosT = P.ctr["V"]
        _cut(1.1)
        P.wait("gpsimd", "V", wCnt1)
        tok_ci = P.dma("gpsimd", cnt_in[0:1, :], cnt1[:, :], "G")
        P.wtok("gpsimd", tok_ci)
        _cut(1.2)
        P.coll(lambda e: e.collective_compute(
            "AllGather", ALU.bypass, replica_groups=[list(range(NCORE))],
            ins=[cnt_in.opt()], outs=[cnt_out.opt()]))
        _cut(1.4)
        P.wait("sync", "CC")
        tok_cnts = P.dma("sync", cnts_sb[:, :], cnt_out[:, :], "L")
        P.wtok("tensor", tok_cnts)
        _cut(1.6)
        P.wtok("tensor", tok_cm)
        P.op("tensor", lambda e: e.matmul(pb[4][0:E, 0:1], cnts_sb[:, :], cm_sb[:, :],
                                          start=True, stop=True), "T")
        wBase = P.ctr["T"]
        P.wait("vector", "T", wBase)
        vop(lambda e: e.tensor_copy(out=base64[:, 0:1], in_=pb[4][:, 0:1]))
        _cut(1.7)
        P.wait("vector", "V", wPosT)
        vop(lambda e: e.tensor_scalar(out=posT[:], in0=posT[:],
                                      scalar1=base64[0:64, 0:1], scalar2=None, op0=ALU.add))
        _cut(1.8)
        vop(lambda e: e.tensor_scalar(out=keepT[:], in0=posT[:], scalar1=float(CAP),
                                      scalar2=None, op0=ALU.is_lt))
        _cut(1.9)
        vop(lambda e: e.tensor_tensor(out=keepT[:], in0=keepT[:], in1=indT[:],
                                      op=ALU.mult))
        wKeep = P.ctr["V"]
        _cut(2)

        P.wait("tensor", "V", wKeep)
        trev = {}
        for h in range(2):
            if h == 1:
                _cut(2.478)
            for sbl in range(NHT):
                gsb = h * NHT + sbl
                pbs = pb[4 + (sbl % 2)]
                pbs2 = pb[2 + (sbl % 2)]
                if gsb >= 2:
                    P.wait("tensor", "V", trev[gsb - 2])
                P.op("tensor", lambda e, gsb=gsb, pbs=pbs: e.transpose(
                    pbs[0:128, 0:E], posT[:, gsb * 128:(gsb + 1) * 128],
                    ident_f[0:64, 0:64]), "T")
                P.op("tensor", lambda e, gsb=gsb, pbs2=pbs2: e.transpose(
                    pbs2[0:128, 0:E], keepT[:, gsb * 128:(gsb + 1) * 128],
                    ident_f[0:64, 0:64]), "T")
                wT = P.ctr["T"]
                P.wait("vector", "T", wT)
                P.op("vector", lambda e, sbl=sbl, pbs=pbs: e.tensor_copy(
                    out=posh[:, sbl, :], in_=pbs[0:128, 0:E]), "V")
                P.op("vector", lambda e, sbl=sbl, pbs2=pbs2: e.tensor_copy(
                    out=keeph[:, sbl, :], in_=pbs2[0:128, 0:E]), "V")
                trev[gsb] = P.ctr["V"]
            _cut(2.2)
            vop(lambda e: e.tensor_tensor(out=sidh[:], in0=ebase[:], in1=posh[:],
                                          op=ALU.add))
            vop(lambda e: e.tensor_scalar_add(sidh[:], sidh[:], -BIGIDX))
            vop(lambda e: e.tensor_tensor(out=sidh[:], in0=sidh[:], in1=keeph[:],
                                          op=ALU.mult))
            vop(lambda e: e.tensor_scalar_add(sidh[:], sidh[:], BIGIDX))
            _cut(2.3)
            vop(lambda e: e.tensor_tensor(out=ivh[:], in0=ybase[:], in1=posh[:],
                                          op=ALU.add))
            _cut(2.32)
            vop(lambda e: e.tensor_scalar_add(ivh[:], ivh[:], -ZROWID))
            _cut(2.34)
            vop(lambda e: e.tensor_tensor(out=ivh[:], in0=ivh[:], in1=keeph[:],
                                          op=ALU.mult))
            _cut(2.36)
            vop(lambda e: e.tensor_scalar_add(ivh[:], ivh[:], ZROWID))
            _cut(2.4)
            for _d in range(DUMMYV if h == 0 else 0):
                vop(lambda e: e.tensor_scalar_add(m32[:], m32[:], 0.0))
            for _d in range(DUMMYR if h == 0 else 0):
                vop(lambda e: e.tensor_reduce(out=m32[:], in_=tmpA[:], axis=AX.X,
                                              op=ALU.add))
            for k in range(8):
                _cut(2.4 + 0.01 * (h * 8 + k))
                sk = K7SKIP if (h == 0 and k == 7) else 0
                if sk & 1 == 0:
                    vop(lambda e, h=h, k=k: e.tensor_scalar(
                        out=eqh[:], in0=rk_ref[h][:], scalar1=float(k), scalar2=None,
                        op0=ALU.is_equal))
                if sk & 2 == 0:
                    vop(lambda e: e.tensor_tensor(out=tmpA[:], in0=eqh[:], in1=sidh[:],
                                                  op=ALU.mult))
                    vop(lambda e: e.tensor_tensor(out=tmpB[:], in0=eqh[:], in1=ivh[:],
                                                  op=ALU.mult)) if False else None
                if sk & 4 == 0:
                    vop(lambda e: e.tensor_reduce(out=m32[:], in_=tmpA[:], axis=AX.X,
                                                  op=ALU.add))
                if sk & 8 == 0:
                    vop(lambda e, h=h, k=k: e.tensor_copy(
                        out=scu[:, h * NHT:(h + 1) * NHT, k], in_=m32[:]))
                if sk & 2 == 0:
                    vop(lambda e: e.tensor_tensor(out=tmpA[:], in0=eqh[:], in1=ivh[:],
                                                  op=ALU.mult))
                if sk & 4 == 0:
                    vop(lambda e: e.tensor_reduce(out=m32[:], in_=tmpA[:], axis=AX.X,
                                                  op=ALU.add))
                if sk & 8 == 0:
                    vop(lambda e, h=h, k=k: e.tensor_copy(
                        out=ycu[:, h * NHT:(h + 1) * NHT, k], in_=m32[:]))
                if sk & 2 == 0:
                    vop(lambda e, h=h: e.tensor_tensor(out=tmpA[:], in0=eqh[:],
                                                       in1=wn_ref[h][:], op=ALU.mult))
                if sk & 4 == 0:
                    vop(lambda e: e.tensor_reduce(out=m32[:], in_=tmpA[:], axis=AX.X,
                                                  op=ALU.add))
                if sk & 8 == 0:
                    vop(lambda e, h=h, k=k: e.tensor_copy(
                        out=wk8[:, h * NHT:(h + 1) * NHT, k], in_=m32[:]))
        _cut(2.59)
        vop(lambda e: e.tensor_copy(out=scu_u[:], in_=scu[:]))
        vop(lambda e: e.tensor_copy(out=ycu_u[:], in_=ycu[:]))
        wExtract = P.ctr["V"]

        if debug:
            P.wait("gpsimd", "V", wExtract)
            P.dma("gpsimd", pos_db[:, :].rearrange("(p f) b -> p (f b)", p=64),
                  posT[:], "G")
            P.dma("gpsimd", ind_db[:, :].rearrange("(p f) b -> p (f b)", p=64),
                  indT[:], "G")
            for nm2, ap2 in (("sfch_db", sfch[:]), ("gmask_db", gmask[:]),
                             ("indh_db", indh[:]), ("rk1_db", rk_ref[1][:]),
                             ("wn1_db", wn_ref[1][:]), ("rk0_db", rk_h0[:]),
                             ("wn0_db", wn_h0[:]), ("io64_db", io64[:]),
                             ("io64b_db", io64b[:]), ("iog_db", iog[:]),
                             ("iogb_db", iogb[:]), ("io8_db", io8[:]),
                             ("io8b_db", io8b[:]), ("ebase_db", ebase[:]),
                             ("ybase_db", ybase[:]), ("posh_db", posh[:]),
                             ("keeph_db", keeph[:]), ("sidh_db", sidh[:]),
                             ("ivh_db", ivh[:]), ("scoresh_db", scoresh[:]),
                             ("base64_db", base64[:]), ("tokid2_db", tokid[:])):
                t2 = nc.dram_tensor(nm2, [128] + [int(x) for x in ap2.shape[1:]],
                                    ap2.dtype, kind="ExternalOutput").ap()
                P.dma("gpsimd", t2[tuple([slice(None)] * len(ap2.shape))], ap2, "G")
            P.wall("gpsimd", "G")

        _cut(3)
        vop(lambda e: e.memset(ztile[:], 0.0), fence=False)
        wZ = P.ctr["V"]
        P.wait("gpsimd", "V", wZ)
        slabv = slab_in.rearrange("(p a) c -> p a c", p=128)
        ztoks = []
        for j in range(2):
            ztoks.append(P.dma(
                "gpsimd", slabv[:, j * 256:(j + 1) * 256, :],
                ztile[:].rearrange("p (a c) -> p a c", c=2), "G"))
        for t in ztoks:
            P.wtok("gpsimd", t)
        _cut(3.2)
        P.wait("vector", "V", wExtract)
        bc_cell = {}

        def _mkbc(e):
            r = e.alloc_register(name="bcreg")
            e.reg_mov(r, E * CAP - 1)
            bc_cell["r"] = r
        P.raw("gpsimd", _mkbc)
        for k in range(8):
            P.wall("vector", "G")
            vop(lambda e: e.tensor_copy(out=payk[:, :, 0:1],
                                        in_=tokid[:, :, None]))
            vop(lambda e, k=k: e.tensor_copy(out=payk[:, :, 1:2], in_=wk8[:, :, k:k + 1]))
            wPay = P.ctr["V"]
            P.wait("gpsimd", "V", wPay)
            for sb2 in range(SB):
                P.ind(lambda e, k=k, sb2=sb2: e.indirect_dma_start(
                    out=slab_in[:, :],
                    out_offset=IndirectOffsetOnAxis(scu_u[:, sb2, k:k + 1], 0),
                    in_=payk[:, sb2, :], in_offset=None,
                    bounds_check=bc_cell["r"], oob_is_err=False))
        P.wall("gpsimd", "G")
        _cut(3.4)
        P.wait("gpsimd", "CC")
        if not SKIPRS:
            P.coll(lambda e: e.collective_compute(
                "ReduceScatter", ALU.add, replica_groups=[list(range(NCORE))],
                ins=[slab_in.opt()], outs=[slab_out.opt()]))
        wRS = P.ctr["CC"]
        _cut(3.6)
        P.wait("gpsimd", "CC", wRS)
        P.coll(lambda e: e.collective_compute(
            "AllGather", ALU.bypass, replica_groups=[list(range(NCORE))],
            ins=[xln2_sp.opt()], outs=[xg_out.opt()]))
        wAGX = P.ctr["CC"]

        _cut(4)
        AR.pop()
        AR.push()
        sgT_sb = AR.alloc([128, 8, F2], BF16, "sgTsb")
        suT_sb = AR.alloc([128, 8, F2], BF16, "suTsb")
        sdT_sb = AR.alloc([128, 8, H], BF16, "sdTsb")
        xt2T = AR.alloc([128, 8, 1024], BF16, "xt2T")
        x2h = AR.alloc([128, 4, H], F32, "x2h")
        gh = AR.alloc([128, 8, 1024], BF16, "gh")
        uh = AR.alloc([128, 8, 1024], BF16, "uh")
        dsh = AR.alloc([128, 8, 1024], BF16, "dsh")

        for hk in range(8):
            P.dma("gpsimd", sgT_sb[:, hk, :], sgT[hk * 128:(hk + 1) * 128, :], "G")
            P.dma("gpsimd", suT_sb[:, hk, :], suT[hk * 128:(hk + 1) * 128, :], "G")
            P.dma("gpsimd", sdT_sb[:, hk, :], sdT[hk * 128:(hk + 1) * 128, :], "G")
        P.wall("tensor", "G")
        P.wall("sync", "G")

        padd = {}
        tok_x3store = [None]
        wGU = [None]
        bl = {}
        P.wait("tensor", "V")
        P.wait("tensor", "Sc")
        for p2 in range(4):
            s0 = p2 * 1024
            if wGU[0] is not None:
                P.wait("sync", "T", wGU[0])
            tok_x = P.dma("sync", xt2T[:],
                          xln2T_sp[:, s0:s0 + 1024].rearrange("(a p) b -> p a b", p=128), "L")
            P.wtok("tensor", tok_x)
            gidx = 0
            for dst, wsb in ((gh, sgT_sb), (uh, suT_sb)):
                for f2b in range(8):
                    for half in range(2):
                        b = gidx % 4
                        gidx += 1
                        pbk = pb[b]
                        prev = bl.get(b)
                        if prev is not None:
                            P.wait("tensor", prev[0], prev[1])
                        for hk in range(8):
                            P.op("tensor", lambda e, wsb=wsb, f2b=f2b, hk=hk, half=half,
                                 pbk=pbk: e.matmul(
                                     pbk[:, 0:512], wsb[:, hk, f2b * 128:(f2b + 1) * 128],
                                     xt2T[:, hk, half * 512:(half + 1) * 512],
                                     start=(hk == 0), stop=(hk == 7),
                                     skip_group_check=True), "T")
                        wMM = P.ctr["T"]
                        if dst is gh:
                            P.wait("scalar", "T", wMM)
                            P.op("scalar", lambda e, dst=dst, f2b=f2b, half=half, pbk=pbk:
                                 e.activation(out=dst[:, f2b, half * 512:(half + 1) * 512],
                                              in_=pbk[:, 0:512], func=AF.Silu), "Sc")
                            bl[b] = ("Sc", P.ctr["Sc"])
                        else:
                            P.wait("vector", "T", wMM)
                            P.op("vector", lambda e, dst=dst, f2b=f2b, half=half, pbk=pbk:
                                 e.tensor_copy(out=dst[:, f2b, half * 512:(half + 1) * 512],
                                               in_=pbk[:, 0:512]), "V")
                            bl[b] = ("V", P.ctr["V"])
            wUp = P.ctr["V"]
            wGU[0] = P.ctr["T"]
            P.wait("vector", "Sc")
            P.op("vector", lambda e: e.tensor_tensor(out=gh[:], in0=gh[:], in1=uh[:],
                                                     op=ALU.mult), "V", fence=True)
            wHH = P.ctr["V"]
            P.wait("tensor", "V", wHH)
            gidx = 0
            for hb in range(8):
                for half in range(2):
                    b = gidx % 4
                    gidx += 1
                    pbk = pb[b]
                    prev = bl.get(b)
                    if prev is not None:
                        P.wait("tensor", prev[0], prev[1])
                    for fk in range(8):
                        P.op("tensor", lambda e, hb=hb, fk=fk, half=half, pbk=pbk:
                             e.matmul(pbk[:, 0:512], sdT_sb[:, fk, hb * 128:(hb + 1) * 128],
                                      gh[:, fk, half * 512:(half + 1) * 512],
                                      start=(fk == 0), stop=(fk == 7),
                                      skip_group_check=True), "T")
                    wMM = P.ctr["T"]
                    P.wait("vector", "T", wMM)
                    P.op("vector", lambda e, hb=hb, half=half, pbk=pbk: e.tensor_copy(
                        out=dsh[:, hb, half * 512:(half + 1) * 512], in_=pbk[:, 0:512]),
                        "V")
                    bl[b] = ("V", P.ctr["V"])
            wDS = P.ctr["V"]
            for half in range(2):
                if tok_x3store[0] is not None:
                    P.wtok("sync", tok_x3store[0])
                tok_x2 = P.dma("sync", x2h[:],
                               tokrows(x2_sp, s0 + half * 512, s0 + (half + 1) * 512), "L")
                P.wait("tensor", "V", wDS)
                P.wtok("vector", tok_x2)
                for sb2 in range(4):
                    pt = ptr[sb2 % 2]
                    if padd.get(sb2 % 2) is not None:
                        P.wait("tensor", "V", padd[sb2 % 2])
                    for ob in range(8):
                        P.op("tensor", lambda e, sb2=sb2, ob=ob, half=half, pt=pt:
                             e.transpose(pt[:, ob * 128:(ob + 1) * 128],
                                         dsh[:, ob, half * 512 + sb2 * 128:
                                             half * 512 + (sb2 + 1) * 128],
                                         ident_bf[:]), "T")
                    wT = P.ctr["T"]
                    P.wait("vector", "T", wT)
                    P.op("vector", lambda e, sb2=sb2, pt=pt: e.tensor_tensor(
                        out=x2h[:, sb2, :], in0=x2h[:, sb2, :], in1=pt[:, 0:H],
                        op=ALU.add), "V")
                    padd[sb2 % 2] = P.ctr["V"]
                wX3 = P.ctr["V"]
                P.wait("gpsimd", "V", wX3)
                tok_x3store[0] = P.dma(
                    "gpsimd", tokrows(x3_sp, s0 + half * 512, s0 + (half + 1) * 512),
                    x2h[:], "G")

        _cut(5)
        AR.pop()
        AR.push()
        wv_sb = AR.alloc([128, 64, 2], F32, "wv_sb")
        tok_u = AR.alloc([128, 64], U32, "tok_u")
        xe_sb = AR.alloc([128, 8, 1024], BF16, "xe_sb")
        xeT = AR.alloc([128, 8, CAP], BF16, "xeT")
        w1_sb = [AR.alloc([128, 8, F], BF16, "w1a"), AR.alloc([128, 8, F], BF16, "w1b")]
        w3_sb = [AR.alloc([128, 8, F], BF16, "w3a"), AR.alloc([128, 8, F], BF16, "w3b")]
        w2_sb = [AR.alloc([128, 4, H], BF16, "w2a"), AR.alloc([128, 4, H], BF16, "w2b")]
        h1_sb = AR.alloc([128, 4, CAP], BF16, "h1_sb")
        h3_sb = AR.alloc([128, 4, CAP], BF16, "h3_sb")
        ye_sb = AR.alloc([128, 8, H], BF16, "ye_sb")
        zbf = AR.alloc([128, 1024], BF16, "zbf")

        vop(lambda e: e.memset(zbf[:], 0.0), fence=False)
        wZB = P.ctr["V"]
        P.wait("gpsimd", "V", wZB)
        P.dma("gpsimd", ye_loc[EO * CAP:YBLK, :].rearrange("(a p) b -> p a b", p=128),
              zbf[:].rearrange("p (a b) -> p a b", a=1), "G")

        P.wait("sync", "CC", wRS)
        for col in range(64):
            tok_wv = P.dma("sync", wv_sb[:, col, :],
                           slab_out[col * 128:(col + 1) * 128, :], "L")
        P.wall("vector", "L")
        vop(lambda e: e.tensor_copy(out=tok_u[:], in_=wv_sb[:, :, 0]))
        wTOKU = P.ctr["V"]
        P.wait("gpsimd", "V", wTOKU)
        P.wait("gpsimd", "CC", wAGX)

        wei_toks = {}
        trv = {0: padd.get(0), 1: padd.get(1)}
        wXPOSE = {}
        wWUSE = {}
        tok_ye = {}
        for ei in range(EO):
            j = ei % 2
            if ei >= 2:
                P.wait("gpsimd", "T", wWUSE[j])
            for hk in range(8):
                P.dma("gpsimd", w1_sb[j][:, hk, :],
                      w1T[ei, hk * 128:(hk + 1) * 128, :], "G")
                P.dma("gpsimd", w3_sb[j][:, hk, :],
                      w3T[ei, hk * 128:(hk + 1) * 128, :], "G")
            for fk in range(4):
                wei_toks[ei] = P.dma("gpsimd", w2_sb[j][:, fk, :],
                                     w2T[ei, fk * 128:(fk + 1) * 128, :], "G")
            if ei >= 1:
                P.wait("gpsimd", "T", wXPOSE[ei - 1])
            gat = []
            for jb in range(8):
                gat.append(P.ind(lambda e, ei=ei, jb=jb: e.indirect_dma_start(
                    out=xe_sb[:, jb, :], out_offset=None, in_=xg_out[:, :],
                    in_offset=IndirectOffsetOnAxis(tok_u[:, ei * 8 + jb:ei * 8 + jb + 1], 0))))
            for t in gat:
                P.wtok("tensor", t)
            for hb in range(8):
                g = ei * 8 + hb
                pt = ptr[hb % 2]
                prev = trv.get(g - 2) if g >= 2 else trv.get(g)
                if prev is not None:
                    P.wait("tensor", "V", prev)
                for jb in range(8):
                    P.op("tensor", lambda e, hb=hb, jb=jb, pt=pt: e.transpose(
                        pt[:, jb * 128:(jb + 1) * 128],
                        xe_sb[:, jb, hb * 128:(hb + 1) * 128], ident_bf[:]), "T")
                wT = P.ctr["T"]
                P.wait("vector", "T", wT)
                P.op("vector", lambda e, hb=hb, pt=pt: e.tensor_copy(
                    out=xeT[:, hb, :], in_=pt[:, 0:CAP]), "V")
                trv[ei * 8 + hb] = P.ctr["V"]
            wXPOSE[ei] = P.ctr["T"]
            wXE = P.ctr["V"]
            P.wait("tensor", "V", wXE)
            P.wtok("tensor", wei_toks[ei])
            gidx = 0
            for dst, wsb, act in ((h1_sb, w1_sb[j], True), (h3_sb, w3_sb[j], False)):
                for fb in range(4):
                    for half in range(2):
                        b = gidx % 4
                        gidx += 1
                        pbk = pb[b]
                        prev = bl.get(b)
                        if prev is not None:
                            P.wait("tensor", prev[0], prev[1])
                        for hk in range(8):
                            P.op("tensor", lambda e, wsb=wsb, fb=fb, hk=hk, half=half,
                                 pbk=pbk: e.matmul(
                                     pbk[:, 0:512], wsb[:, hk, fb * 128:(fb + 1) * 128],
                                     xeT[:, hk, half * 512:(half + 1) * 512],
                                     start=(hk == 0), stop=(hk == 7),
                                     skip_group_check=True), "T")
                        wMM = P.ctr["T"]
                        if act:
                            P.wait("scalar", "T", wMM)
                            P.op("scalar", lambda e, dst=dst, fb=fb, half=half, pbk=pbk:
                                 e.activation(out=dst[:, fb, half * 512:(half + 1) * 512],
                                              in_=pbk[:, 0:512], func=AF.Silu), "Sc")
                            bl[b] = ("Sc", P.ctr["Sc"])
                        else:
                            P.wait("vector", "T", wMM)
                            P.op("vector", lambda e, dst=dst, fb=fb, half=half, pbk=pbk:
                                 e.tensor_copy(out=dst[:, fb, half * 512:(half + 1) * 512],
                                               in_=pbk[:, 0:512]), "V")
                            bl[b] = ("V", P.ctr["V"])
            P.wait("vector", "Sc")
            P.op("vector", lambda e: e.tensor_tensor(out=h1_sb[:], in0=h1_sb[:],
                                                     in1=h3_sb[:], op=ALU.mult),
                 "V", fence=True)
            wHH = P.ctr["V"]
            P.wait("tensor", "V", wHH)
            if ei >= 1:
                P.wtok("vector", tok_ye[ei - 1])
            gidx = 0
            for sb2 in range(8):
                for half in range(2):
                    b = gidx % 4
                    gidx += 1
                    pbk = pb[b]
                    prev = bl.get(b)
                    if prev is not None:
                        P.wait("tensor", prev[0], prev[1])
                    for fk in range(4):
                        P.op("tensor", lambda e, sb2=sb2, fk=fk, half=half, pbk=pbk, j=j:
                             e.matmul(pbk[:, 0:512],
                                      h1_sb[:, fk, sb2 * 128:(sb2 + 1) * 128],
                                      w2_sb[j][:, fk, half * 512:(half + 1) * 512],
                                      start=(fk == 0), stop=(fk == 3),
                                      skip_group_check=True), "T")
                    wMM = P.ctr["T"]
                    P.wait("vector", "T", wMM)
                    P.op("vector", lambda e, sb2=sb2, half=half, pbk=pbk, ei=ei:
                         e.tensor_scalar(
                             out=ye_sb[:, sb2, half * 512:(half + 1) * 512],
                             in0=pbk[:, 0:512],
                             scalar1=wv_sb[:, ei * 8 + sb2, 1:2], scalar2=None,
                             op0=ALU.mult), "V")
                    bl[b] = ("V", P.ctr["V"])
            wYE = P.ctr["V"]
            wWUSE[j] = P.ctr["T"]
            P.wait("gpsimd", "V", wYE)
            tok_ye[ei] = P.dma(
                "gpsimd", ye_loc[ei * CAP:(ei + 1) * CAP, :].rearrange(
                    "(a p) b -> p a b", p=128), ye_sb[:], "G")

        _cut(6)
        P.wall("gpsimd", "G")
        P.wall("sync", "G")
        P.wait("gpsimd", "CC")
        P.coll(lambda e: e.collective_compute(
            "AllGather", ALU.bypass, replica_groups=[list(range(NCORE))],
            ins=[ye_loc.opt()], outs=[ye_out.opt()]))
        wAGY = P.ctr["CC"]
        P.wait("gpsimd", "CC", wAGY)
        AR.pop()
        AR.push()
        x3g = AR.alloc([128, 8, H], F32, "x3g")
        x3g2 = AR.alloc([128, 8, H], F32, "x3g2")
        tok_out = {}
        for grp in range(4):
            buf = x3g if grp % 2 == 0 else x3g2
            s0 = grp * 1024
            if grp >= 2:
                P.wtok("sync", tok_out[grp - 2])
            tok_x3 = P.dma("sync", buf[:], tokrows(x3_sp, s0, s0 + 1024), "L")
            P.wtok("gpsimd", tok_x3)
            add_toks = {}
            for k in range(8):
                for sb2 in range(8):
                    gsb = grp * 8 + sb2
                    prev = add_toks.get((k - 1, sb2))
                    if prev is not None:
                        P.wtok("gpsimd", prev)
                    add_toks[(k, sb2)] = P.ind(
                        lambda e, gsb=gsb, k=k, sb2=sb2, buf=buf: e.indirect_dma_start(
                            out=buf[:, sb2, :], out_offset=None, in_=ye_out[:, :],
                            in_offset=IndirectOffsetOnAxis(ycu_u[:, gsb, k:k + 1], 0),
                            compute_op=ALU.add))
            for kk in range(8):
                P.wtok("sync", add_toks[(7, kk)])
            tok_out[grp] = P.dma("sync", tokrows(out_ap, s0, s0 + 1024), buf[:], "L")

    try:
        _phase2()
    except _Cut:
        pass

    if debug:
        P.wait("gpsimd", "V")
        P.wait("gpsimd", "Sc")
        if STAGE2 >= 6:
            for nm2, ap2 in (("scu_db", dbg_refs["scu"][:]),
                             ("ycu_db", dbg_refs["ycu"][:]),
                             ("wk8_db", dbg_refs["wk8"][:]),
                             ("tokid_db", dbg_refs["tokid"][:])):
                t2 = nc.dram_tensor(nm2, [128] + [int(x) for x in ap2.shape[1:]],
                                    ap2.dtype, kind="ExternalOutput").ap()
                P.dma("gpsimd", t2[tuple([slice(None)] * len(ap2.shape))], ap2, "G")
            slabo_db = nc.dram_tensor("slabo_db", [EO * CAP, 2], F32,
                                      kind="ExternalOutput").ap()
            P.dma("gpsimd", slabo_db[:, :], slab_out[:, :], "G")
            yel_db = nc.dram_tensor("yel_db", [YBLK, H], BF16,
                                    kind="ExternalOutput").ap()
            P.dma("gpsimd", yel_db[:, :].rearrange("(a p) b -> p a b", p=128),
                  ye_loc[:, :].rearrange("(a p) b -> p a b", p=128), "G")
            xg_db = nc.dram_tensor("xg_db", [256, H], BF16,
                                   kind="ExternalOutput").ap()
            P.dma("gpsimd", xg_db[:, :], xg_out[0:256, :], "G")
        P.dma("gpsimd", logits_db[:, :].rearrange("(a p) b -> p a b", p=128),
              logits[:], "G")
        P.dma("gpsimd", xln2_db[:, :].rearrange("(a p) b -> p a b", p=128),
              xln2_sp[:, :].rearrange("(a p) b -> p a b", p=128), "G")
        for nm, ap in (("hbuf_db", hbuf[:]), ("dt_db", dt_all[:]), ("sg_db", sg_all[:]),
                       ("xlnT_db", xlnT[:]), ("xln_db", xln_bf[:]),
                       ("lnst_db", lnst[:])):
            t = nc.dram_tensor(nm, [128] + [int(x) for x in ap.shape[1:]], ap.dtype,
                               kind="ExternalOutput").ap()
            idx = tuple([slice(None)] * len(ap.shape))
            P.dma("gpsimd", t[idx], ap, "G")
        t = nc.dram_tensor("ssmp_db", [96, SC], F32, kind="ExternalOutput").ap()
        P.dma("gpsimd", t[:, :], ssm_p[:, :], "G")

    with nc.Block() as block:
        @block.sync
        def _(eng):
            for fn in P.items["sync"]:
                fn(eng)

        @block.scalar
        def _(eng):
            for fn in P.items["scalar"]:
                fn(eng)

        @block.vector
        def _(eng):
            for fn in P.items["vector"]:
                fn(eng)

        @block.tensor
        def _(eng):
            for fn in P.items["tensor"]:
                fn(eng)

        @block.gpsimd
        def _(eng):
            for fn in P.items["gpsimd"]:
                fn(eng)
            for pool in ("L", "G"):
                for n in P.pools[pool]["names"]:
                    if P.ctr[n] > 0:
                        eng.wait_ge(P.sems[n], P.ctr[n])

    build_program.last_ctr = dict(P.ctr)
    return nc


def make_in_maps(inputs):
    in_maps = []
    for c in range(NCORE):
        cm = np.zeros((8, 1), np.float32)
        cm[:c] = 1.0
        m = {
            "x_loc": np.ascontiguousarray(np.asarray(inputs["L0_attn_x"])[c]),
            "W_in": np.asarray(inputs["L0_attn_W_in"]),
            "conv_wT": np.ascontiguousarray(np.asarray(inputs["L0_attn_conv_w"]).reshape(4, C).T),
            "conv_b": np.ascontiguousarray(np.asarray(inputs["L0_attn_conv_b"]).reshape(C, 1)),
            "W_x": np.asarray(inputs["L0_attn_W_x"]),
            "W_dt": np.asarray(inputs["L0_attn_W_dt"]),
            "b_dt": np.ascontiguousarray(np.asarray(inputs["L0_attn_b_dt"]).reshape(1, C)),
            "A_log": np.asarray(inputs["L0_attn_A_log"]),
            "D_vec": np.ascontiguousarray(np.asarray(inputs["L0_attn_D"]).reshape(C, 1)),
            "W_out": np.asarray(inputs["L0_attn_W_out"]),
            "gwT": np.ascontiguousarray(np.asarray(inputs["L0_ffn_gate_weight"]).T),
            "e_bias": np.ascontiguousarray(np.asarray(inputs["L0_ffn_e_bias"]).reshape(1, E)),
            "w1T": np.ascontiguousarray(np.asarray(inputs["L0_ffn_w1"])[c * EO:(c + 1) * EO].transpose(0, 2, 1)),
            "w2T": np.ascontiguousarray(np.asarray(inputs["L0_ffn_w2"])[c * EO:(c + 1) * EO].transpose(0, 2, 1)),
            "w3T": np.ascontiguousarray(np.asarray(inputs["L0_ffn_w3"])[c * EO:(c + 1) * EO].transpose(0, 2, 1)),
            "sgT": np.ascontiguousarray(np.asarray(inputs["L0_ffn_shared_gate"]).T),
            "suT": np.ascontiguousarray(np.asarray(inputs["L0_ffn_shared_up"]).T),
            "sdT": np.ascontiguousarray(np.asarray(inputs["L0_ffn_shared_down"]).T),
            "coremask": cm,
            "tokbase": np.array([[float(S * c)]], np.float32),
        }
        in_maps.append(m)
    return in_maps


def kernel(**inputs):
    nc = build_program()
    in_maps = make_in_maps(inputs)
    res = run_bass_kernel_spmd(nc, in_maps, core_ids=list(range(NCORE)))
    return np.stack([r["out_loc"] for r in res.results], axis=0)
